# version 02
# baseline (speedup 1.0000x reference)
import bisect
import contextlib
import numpy as np
import ml_dtypes
import concourse.bass as bass
import concourse.mybir as mybir
from concourse.bass_utils import run_bass_kernel_spmd

F32 = mybir.dt.float32
BF16 = mybir.dt.bfloat16
AF = mybir.ActivationFunctionType
OP = mybir.AluOpType
AX = mybir.AxisListType

D = 1024
T = 4096
NB = 4
TBL = 1024
NL = 6
NH = 16
DH = 64
NE = 16
DFF = 4096
LN_EPS = 1e-5

TAPS = ()
N_LAYERS = NL
DO_SHARED = True
DO_ROUTED = True
DO_FINAL_LN = True

ENGS = ("PE", "DVE", "ACT", "POOL", "SP")


class Plan:
    def __init__(self):
        self.ops = {e: [] for e in ENGS}
        self.incs = {e: [] for e in ENGS}
        self.ninc = {e: 0 for e in ENGS}
        self.hwm = {e: {} for e in ENGS}
        self.streams = {}
        self.regions = {}

    def stream(self, name, eng):
        self.streams[name] = {"eng": eng, "count": 0}

    def region(self, name):
        r = self.regions.get(name)
        if r is None:
            r = {"name": name, "w": None, "rd": {}}
            self.regions[name] = r
        return r

    def _eng_inc(self, ename, seq):
        lst = self.incs[ename]
        i = bisect.bisect_left(lst, (seq, -1))
        if i < len(lst):
            return lst[i][1]
        self.ninc[ename] += 1
        v = self.ninc[ename]
        lst.append((seq, v))
        self.ops[ename][seq]["incs"].append((("e", ename), 1))
        return v

    def _resolve(self, cons_eng, prods):
        waits = {}
        for p in prods:
            if p is None:
                continue
            if p[0] == "e":
                semkey = ("e", p[1])
                val = self._eng_inc(p[1], p[2])
            else:
                semkey = ("d", p[1])
                val = p[2]
            if val > self.hwm[cons_eng].get(semkey, 0):
                if val > waits.get(semkey, 0):
                    waits[semkey] = val
        out = list(waits.items())
        for semkey, val in out:
            self.hwm[cons_eng][semkey] = val
        return out

    def op(self, eng, fn, reads=(), writes=(), tag=""):
        prods = []
        for r in reads:
            if r["w"] is not None:
                prods.append(r["w"])
        for wr in writes:
            w = wr["w"]
            if w is not None and not (w[0] == "e" and w[1] == eng):
                prods.append(w)
            for k, v in wr["rd"].items():
                if k[0] == "e":
                    if k[1] == eng:
                        continue
                    prods.append(("e", k[1], v))
                else:
                    prods.append(("d", k[1], v))
        waits = self._resolve(eng, prods)
        seq = len(self.ops[eng])
        self.ops[eng].append({"fn": fn, "waits": waits, "incs": [], "tag": tag})
        for r in reads:
            r["rd"][("e", eng)] = seq
        for wr in writes:
            wr["w"] = ("e", eng, seq)
            wr["rd"] = {}
        return seq

    def dma(self, sname, fn, reads=(), writes=(), tag=""):
        st = self.streams[sname]
        eng = st["eng"]
        prods = []
        for r in reads:
            if r["w"] is not None:
                prods.append(r["w"])
        for wr in writes:
            if wr["w"] is not None:
                prods.append(wr["w"])
            for k, v in wr["rd"].items():
                prods.append(("e", k[1], v) if k[0] == "e" else ("d", k[1], v))
        waits = self._resolve(eng, prods)
        st["count"] += 16
        cnt = st["count"]
        self.ops[eng].append(
            {"fn": fn, "waits": waits, "incs": [(("d", sname), 16)], "tag": tag})
        for r in reads:
            r["rd"][("d", sname)] = cnt
        for wr in writes:
            wr["w"] = ("d", sname, cnt)
            wr["rd"] = {}
        return cnt


def _build():
    nc = bass.Bass("TRN2", target_bir_lowering=False)
    P = Plan()
    P.stream("w", "SP")
    P.stream("pc", "POOL")
    P.stream("xio", "POOL")

    dt_in = {}

    def din(name, shape, dt=F32):
        dt_in[name] = nc.dram_tensor(name, list(shape), dt, kind="ExternalInput")
        return dt_in[name]

    x_d = din("x", (T, D))
    wq_d = din("Wq", (NL, D, D)); wk_d = din("Wk", (NL, D, D))
    wv_d = din("Wv", (NL, D, D)); wo_d = din("Wo", (NL, D, D))
    w1_d = din("W1", (NL, D, DFF)); w2_d = din("W2", (NL, DFF, D))
    sw1_d = din("sw1", (1, D, DFF)); sw2_d = din("sw2", (1, D, DFF))
    sw3_d = din("sw3", (1, DFF, D))
    rw1_d = din("rw1", (NE, D, D)); rw2_d = din("rw2", (NE, D, D))
    rw3_d = din("rw3", (NE, D, D))
    rout_d = din("router", (D, NE))
    bv_d = din("bv_rows", (NL, D))
    bia_d = din("bias_pp", (128, 640))
    cb_d = din("cst_bf", (128, 1216), BF16)
    cf_d = din("cst_f32", (128, 144))
    or_d = din("ones_row", (1, 128))
    sel_d = din("sel", (16, 2048), BF16)
    out_d = nc.dram_tensor("out", [T, D], F32, kind="ExternalOutput")

    scr = {}
    offs = [0, 0]

    def salloc(grp, key, din_, dout_):
        scr[key] = (grp, offs[grp], din_, dout_)
        offs[grp] += din_ * dout_

    salloc(0, "router", D, NE)
    salloc(0, "sw1", D, DFF); salloc(0, "sw2", D, DFF); salloc(0, "sw3", DFF, D)
    for e in range(NE):
        salloc(0, f"rw1_{e}", D, D); salloc(0, f"rw2_{e}", D, D)
        salloc(0, f"rw3_{e}", D, D)
    for l in range(NL):
        salloc(1, f"Wq_{l}", D, D); salloc(1, f"Wk_{l}", D, D)
        salloc(1, f"Wv_{l}", D, D); salloc(1, f"Wo_{l}", D, D)
        salloc(1, f"W1_{l}", D, DFF); salloc(1, f"W2_{l}", DFF, D)
    wscs = [nc.dram_tensor("wsc0", [offs[0]], BF16),
            nc.dram_tensor("wsc1", [offs[1]], BF16)]

    def src_of(key):
        if key == "router":
            return rout_d, 0
        if key in ("sw1", "sw2", "sw3"):
            return {"sw1": sw1_d, "sw2": sw2_d, "sw3": sw3_d}[key], 0
        kind, idx = key.split("_")
        idx = int(idx)
        m = {"rw1": rw1_d, "rw2": rw2_d, "rw3": rw3_d,
             "Wq": wq_d, "Wk": wk_d, "Wv": wv_d, "Wo": wo_d,
             "W1": w1_d, "W2": w2_d}[kind]
        sz = D * DFF if kind in ("W1", "W2") else D * D
        return m, idx * sz

    pc_units = []

    def pc_add(key):
        grp, base, din_, dout_ = scr[key]
        n = din_ * dout_
        u = 0
        while u < n:
            sz = min(1024 * 1024, n - u)
            pc_units.append((key, u, sz))
            u += sz

    pc_add("router"); pc_add("sw1"); pc_add("sw2"); pc_add("sw3")
    for e in range(NE):
        pc_add(f"rw1_{e}"); pc_add(f"rw2_{e}"); pc_add(f"rw3_{e}")
    for l in range(NL):
        for k in ("Wq", "Wk", "Wv", "Wo", "W1", "W2"):
            pc_add(f"{k}_{l}")
    pc_pos = [0]
    pc_last_unit = {}
    for i, (key, u, sz) in enumerate(pc_units):
        pc_last_unit[key] = i

    def pump_until(key):
        while pc_pos[0] <= pc_last_unit[key]:
            pump(1)

    def pump(n):
        while n > 0 and pc_pos[0] < len(pc_units):
            key, u, sz = pc_units[pc_pos[0]]
            pc_pos[0] += 1
            n -= 1
            grp, base, _, _ = scr[key]
            src, soff = src_of(key)
            in_ap = bass.AP(src, soff + u, [[1, sz]])
            out_ap = bass.AP(wscs[grp], base + u, [[1, sz]])
            P.dma("pc", (lambda h, o=out_ap, i=in_ap: h.dma_start(out=o, in_=i)),
                  writes=(P.region("wsc_" + key),), tag="pc")

    stack = contextlib.ExitStack()

    def sbt(name, shape, dt):
        return stack.enter_context(nc.sbuf_tensor(name, list(shape), dt))

    XF = sbt("XF", (128, 8192), BF16)
    FA = sbt("FA", (128, 8192), BF16)
    QF = sbt("QF", (128, 4096), BF16)
    KF = sbt("KF", (128, 4096), BF16)
    KT = sbt("KT", (128, 4096), BF16)
    VT = sbt("VT", (128, 4096), BF16)
    AO = sbt("AO", (128, 4096), BF16)
    HB = sbt("HB", (128, 16384), BF16)
    MA = sbt("MA", (128, 2048), BF16)
    REC = sbt("REC", (128, 1024), F32)
    ES0 = sbt("ES0", (128, 1024), BF16)
    ES1 = sbt("ES1", (128, 1024), BF16)
    WBUF = [sbt("WBA", (128, 4096), BF16), sbt("WBB", (128, 4096), BF16)]
    KVS = [sbt(f"KVS{l}", (128, 1024), F32) for l in range(NL)]
    KVB = sbt("KVB", (128, 1024), BF16)
    X0S = sbt("X0S", (128, 1024), F32)
    X0B = sbt("X0B", (128, 1024), BF16)
    OT = [sbt("OT0", (128, 1024), F32), sbt("OT1", (128, 1024), F32)]
    ROWS = sbt("ROWS", (1, 2048), F32)
    GFB = sbt("GFB", (16, 1024), BF16)
    SCB = sbt("SCB", (128, 128), F32)
    G1 = sbt("G1", (128, 16), F32)
    G2 = sbt("G2", (128, 16), F32)
    G3 = sbt("G3", (128, 16), F32)
    GM = sbt("GM", (128, 1), F32)
    RWB = sbt("RWB", (128, 128), BF16)
    VROW = sbt("VROW", (1, 1024), F32)
    GSB = sbt("GSB", (128, 2048), BF16)
    BQ1 = sbt("BQ1", (128, 96), F32)
    BIA = sbt("BIA", (128, 640), F32)
    CB = sbt("CB", (128, 1216), BF16)
    CF = sbt("CF", (128, 144), F32)
    ONR = sbt("ONR", (1, 128), F32)
    SEL = sbt("SEL", (16, 2048), BF16)
    PS = stack.enter_context(nc.psum_tensor("PS", [128, 4096], F32))
    PSbf = PS[:].bitcast(BF16)

    IDB = CB[:, 0:128]
    MSK = CB[:, 128:1152]
    ONC = CB[:, 1152:1216]
    IDF = CF[:, 0:128]
    EBC = CF[:, 128:144]

    R = P.region
    RPS = [R(f"psb{b}") for b in range(8)]
    RES = {("ES0", 0): R("ES0a"), ("ES0", 1): R("ES0b"),
           ("ES1", 0): R("ES1a"), ("ES1", 1): R("ES1b")}

    def bcol(l, name):
        base = {"bq": 0, "bk": 8, "bv": 16, "bo": 24, "b2": 32, "b1": 40,
                "g1": 72, "be1": 80, "g2": 88, "be2": 96}[name]
        return l * 104 + base

    GFCOL, BFCOL = 624, 632

    def MM(h, out, lhsT, rhs, start, stop):
        return h.matmul(out, lhsT, rhs, start=start, stop=stop,
                        skip_group_check=True)

    def load_slab(key, r, c, tag="w"):
        pump_until(key)
        grp, base, din_, dout_ = scr[key]
        buf = WBUF[load_slab.idx % 2]
        breg = R("WBUF%d" % (load_slab.idx % 2))
        load_slab.idx += 1
        o = base + r * 1024 * dout_ + c * 512
        in_ap = bass.AP(wscs[grp], o, [[dout_, 128], [128 * dout_, 8], [1, 512]])
        out_ap = buf[:].rearrange("p (k c) -> p k c", c=512)
        P.dma("w", (lambda h, oa=out_ap, ia=in_ap: h.dma_start(out=oa, in_=ia)),
              reads=(R("wsc_" + key),), writes=(breg,), tag=tag)
        return buf, breg
    load_slab.idx = 0

    def row_dma(dst_ap, src_ap, wreg, tag="row"):
        P.dma("xio", (lambda h, o=dst_ap, i=src_ap: h.dma_start(out=o, in_=i)),
              writes=(wreg,), tag=tag)

    tap_outs = {}

    def tap(name, buf_ap, rregs):
        if name not in TAPS or name in tap_outs:
            return
        shp = list(buf_ap.shape)
        td = nc.dram_tensor("tap_" + name, shp, buf_ap.dtype, kind="ExternalOutput")
        tap_outs[name] = td
        P.dma("xio", (lambda h, o=td.ap(), i=buf_ap: h.dma_start(out=o, in_=i)),
              reads=tuple(rregs), writes=(R("tapd_" + name),), tag="tap")

    def ph_init():
        row_dma(BIA[:], bia_d.ap(), R("BIA"))
        row_dma(CB[:], cb_d.ap(), R("CB"))
        row_dma(CF[:], cf_d.ap(), R("CF"))
        row_dma(ONR[:], or_d.ap(), R("ONR"))
        row_dma(SEL[:], sel_d.ap(), R("SEL"))
        for l in range(NL):
            for (o0, name) in ((0, "bq"), (8, "bk")):
                c = bcol(l, name)
                P.op("DVE", (lambda h, l=l, o0=o0, c=c: h.tensor_scalar(
                    BQ1[:, l * 16 + o0:l * 16 + o0 + 8], BIA[:, c:c + 8],
                    1.0, 0.0, OP.add, OP.add)),
                    reads=(R("BIA"),), writes=(R("BQ1"),))
            P.op("DVE", (lambda h, l=l: h.memset(KVS[l][:], 0.0)),
                 writes=(R(f"KVS{l}"),))

    def ph_xload(blk):
        XF3 = XF[:].rearrange("p (a b) -> p a b", b=1024)
        PSb7 = PSbf[:, 7168:8192]
        PSb7v = PSb7.rearrange("p (a b) -> p a b", b=128)
        for tc in range(8):
            r0 = blk * TBL + tc * 128
            row_dma(X0S[:], bass.AP(x_d, r0 * D, [[D, 128], [1, D]]),
                    R("X0S"), tag="xld")
            P.op("DVE", (lambda h: h.tensor_copy(X0B[:], X0S[:])),
                 reads=(R("X0S"),), writes=(R("X0B"),))
            for dc in range(8):
                P.op("PE", (lambda h, dc=dc: h.transpose(
                    PSb7[:, dc * 128:dc * 128 + 128],
                    X0B[:, dc * 128:dc * 128 + 128], IDB)),
                    reads=(R("X0B"), R("CB")), writes=(RPS[7],))
            P.op("DVE", (lambda h, tc=tc: h.tensor_copy(
                XF3[:, :, tc * 128:tc * 128 + 128], PSb7v)),
                reads=(RPS[7],), writes=(R("XFh0"), R("XFh1")))

    def ph_ln(gcol, hf):
        for dc in range(8):
            xsl = XF[:, dc * 1024 + hf * 512:dc * 1024 + hf * 512 + 512]
            xq = ES1[:, 0:512] if dc % 2 == 0 else ES1[:, 512:1024]
            xqr = RES[("ES1", dc % 2)]
            P.op("ACT", (lambda h, o=xq, i=xsl: h.activation(o, i, AF.Square)),
                 reads=(R("XFh%d" % hf),), writes=(xqr,))
            P.op("PE", (lambda h, i=xsl, st=(dc == 0), sp=(dc == 7): MM(
                h, PS[0:1, 1024:1536], ONC[:, 0:1], i, st, sp)),
                reads=(R("XFh%d" % hf), R("CB")), writes=(RPS[2],))
            P.op("PE", (lambda h, i=xq, st=(dc == 0), sp=(dc == 7): MM(
                h, PS[0:1, 1536:2048], ONC[:, 0:1], i, st, sp)),
                reads=(xqr, R("CB")), writes=(RPS[3],))
        P.op("DVE", (lambda h: h.tensor_scalar(
            ROWS[0:1, 0:1024], PS[0:1, 1024:2048], 1.0 / D, 0.0, OP.mult, OP.add)),
            reads=(RPS[2], RPS[3]), writes=(R("ROWS"),))
        P.op("DVE", (lambda h: h.tensor_tensor(
            ROWS[0:1, 1024:1536], ROWS[0:1, 0:512], ROWS[0:1, 0:512], OP.mult)),
            reads=(R("ROWS"),), writes=(R("ROWS"),))
        P.op("DVE", (lambda h: h.scalar_tensor_tensor(
            ROWS[0:1, 512:1024], ROWS[0:1, 512:1024], LN_EPS,
            ROWS[0:1, 1024:1536], OP.add, OP.subtract)),
            reads=(R("ROWS"),), writes=(R("ROWS"),))
        P.op("ACT", (lambda h: h.activation(
            ROWS[0:1, 1024:1536], ROWS[0:1, 512:1024], AF.Ln)),
            reads=(R("ROWS"),), writes=(R("ROWS"),))
        P.op("ACT", (lambda h: h.activation(
            ROWS[0:1, 1536:2048], ROWS[0:1, 1024:1536], AF.Exp, scale=-0.5)),
            reads=(R("ROWS"),), writes=(R("ROWS"),))
        P.op("DVE", (lambda h: h.tensor_tensor(
            ROWS[0:1, 1024:1536], ROWS[0:1, 0:512], ROWS[0:1, 1536:2048],
            OP.mult)), reads=(R("ROWS"),), writes=(R("ROWS"),))
        P.op("PE", (lambda h: MM(h, PS[:, 1024:1536], ONR[:],
                                 ROWS[0:1, 1536:2048], True, True)),
             reads=(R("ROWS"), R("ONR")), writes=(RPS[2],))
        P.op("PE", (lambda h: MM(h, PS[:, 1536:2048], ONR[:],
                                 ROWS[0:1, 1024:1536], True, True)),
             reads=(R("ROWS"), R("ONR")), writes=(RPS[3],))
        for dc in range(8):
            xsl = XF[:, dc * 1024 + hf * 512:dc * 1024 + hf * 512 + 512]
            es = ES0[:, 0:512] if dc % 2 == 0 else ES0[:, 512:1024]
            esr = RES[("ES0", dc % 2)]
            P.op("DVE", (lambda h, o=es, i=xsl: h.tensor_tensor(
                o, i, PS[:, 1024:1536], OP.mult)),
                reads=(R("XFh%d" % hf), RPS[2]), writes=(esr,))
            if gcol is None:
                P.op("DVE", (lambda h, o=xsl, i=es: h.tensor_tensor(
                    o, i, PS[:, 1536:2048], OP.subtract)),
                    reads=(esr, RPS[3]), writes=(R("XFh%d" % hf),))
            else:
                P.op("DVE", (lambda h, o=es, i=es: h.tensor_tensor(
                    o, i, PS[:, 1536:2048], OP.subtract)),
                    reads=(esr, RPS[3]), writes=(esr,))
                gc, bc = gcol
                P.op("DVE", (lambda h, o=xsl, i=es, g=BIA[:, gc + dc:gc + dc + 1],
                             b=BIA[:, bc + dc:bc + dc + 1]: h.tensor_scalar(
                    o, i, g, b, OP.mult, OP.add)),
                    reads=(esr, R("BIA")), writes=(R("XFh%d" % hf),))

    def ph_router(blk):
        pump_until("router")
        grp, base, _, _ = scr["router"]
        in_ap = bass.AP(wscs[grp], base, [[16, 128], [128 * 16, 8], [1, 16]])
        out_ap = RWB[:].rearrange("p (k c) -> p k c", c=16)
        P.dma("w", (lambda h, o=out_ap, i=in_ap: h.dma_start(out=o, in_=i)),
              reads=(R("wsc_router"),), writes=(R("RWB"),))
        for tc in range(8):
            pb = tc % 2
            for dc in range(8):
                P.op("PE", (lambda h, dc=dc, tc=tc, pb=pb: MM(
                    h, PS[:, pb * 512:pb * 512 + 16],
                    XF[:, dc * 1024 + tc * 128:dc * 1024 + tc * 128 + 128],
                    RWB[:, dc * 16:dc * 16 + 16], dc == 0, dc == 7)),
                    reads=(R("XFh0"), R("XFh1"), R("RWB")), writes=(RPS[pb],))
            P.op("ACT", (lambda h, tc=tc, pb=pb: h.activation(
                SCB[:, tc * 16:tc * 16 + 16], PS[:, pb * 512:pb * 512 + 16],
                AF.Sigmoid)), reads=(RPS[pb],), writes=(R("SCB"),))
            ssl = SCB[:, tc * 16:tc * 16 + 16]
            P.op("DVE", (lambda h, s=ssl: h.tensor_tensor(G1[:], s, EBC, OP.add)),
                 reads=(R("SCB"), R("CF")), writes=(R("G1"),))
            P.op("DVE", (lambda h: h.reduce_max(GM[:], G1[:], axis=AX.X)),
                 reads=(R("G1"),), writes=(R("GM"),))
            P.op("DVE", (lambda h: h.tensor_scalar(
                G2[:], G1[:], GM[:], 0.0, OP.is_equal, OP.add)),
                reads=(R("G1"), R("GM")), writes=(R("G2"),))
            P.op("DVE", (lambda h: h.scalar_tensor_tensor(
                G3[:], G2[:], -1e30, G1[:], OP.mult, OP.add)),
                reads=(R("G2"), R("G1")), writes=(R("G3"),))
            P.op("DVE", (lambda h: h.reduce_max(GM[:], G3[:], axis=AX.X)),
                 reads=(R("G3"),), writes=(R("GM"),))
            P.op("DVE", (lambda h: h.tensor_scalar(
                G3[:], G1[:], GM[:], 0.0, OP.is_equal, OP.add)),
                reads=(R("G1"), R("GM")), writes=(R("G3"),))
            P.op("DVE", (lambda h: h.tensor_tensor(G2[:], G2[:], G3[:], OP.add)),
                 reads=(R("G2"), R("G3")), writes=(R("G2"),))
            P.op("DVE", (lambda h, s=ssl: h.tensor_tensor(G1[:], s, G2[:],
                                                          OP.mult)),
                 reads=(R("SCB"), R("G2")), writes=(R("G1"),))
            P.op("PE", (lambda h: h.transpose(PS[0:16, 1024:1152], G1[:], IDF)),
                 reads=(R("G1"), R("CF")), writes=(RPS[2],))
            P.op("DVE", (lambda h, tc=tc: h.tensor_copy(
                GFB[0:16, tc * 128:tc * 128 + 128], PS[0:16, 1024:1152])),
                reads=(RPS[2],), writes=(R("GFB"),))

    def ph_shared(blk):
        for hf in (1, 0):
            for sbk in range(8):
                w1s, w1r = load_slab("sw1", 0, sbk)
                w2s, w2r = load_slab("sw2", 0, sbk)
                for hcl in range(4):
                    hc = sbk * 4 + hcl
                    for kc in range(8):
                        P.op("PE", (lambda h, w=w1s, kc=kc, hcl=hcl, hf=hf: MM(
                            h, PS[:, 1024:1536],
                            w[:, kc * 512 + hcl * 128:kc * 512 + hcl * 128 + 128],
                            XF[:, kc * 1024 + hf * 512:kc * 1024 + hf * 512 + 512],
                            kc == 0, kc == 7)),
                            reads=(w1r, R("XFh%d" % hf)), writes=(RPS[2],))
                        P.op("PE", (lambda h, w=w2s, kc=kc, hcl=hcl, hf=hf: MM(
                            h, PS[:, 1536:2048],
                            w[:, kc * 512 + hcl * 128:kc * 512 + hcl * 128 + 128],
                            XF[:, kc * 1024 + hf * 512:kc * 1024 + hf * 512 + 512],
                            kc == 0, kc == 7)),
                            reads=(w2r, R("XFh%d" % hf)), writes=(RPS[3],))
                    es = ES0[:, 0:512] if hc % 2 == 0 else ES0[:, 512:1024]
                    esr = RES[("ES0", hc % 2)]
                    P.op("ACT", (lambda h, o=es: h.activation(
                        o, PS[:, 1024:1536], AF.Silu)),
                        reads=(RPS[2],), writes=(esr,))
                    P.op("DVE", (lambda h, o=HB[:, hc * 512:hc * 512 + 512],
                                 i=es: h.tensor_tensor(
                        o, i, PS[:, 1536:2048], OP.mult)),
                        reads=(esr, RPS[3]), writes=(R("HB"),))
            for c in range(2):
                for r4 in range(4):
                    w3s, w3r = load_slab("sw3", r4, c)
                    for mol in range(4):
                        for kc in range(8):
                            hidc = r4 * 8 + kc
                            P.op("PE", (lambda h, w=w3s, kc=kc, mol=mol,
                                        hidc=hidc, st=(r4 == 0 and kc == 0),
                                        sp=(r4 == 3 and kc == 7): MM(
                                h, PS[:, (4 + mol) * 512:(4 + mol) * 512 + 512],
                                w[:, kc * 512 + mol * 128:kc * 512 + mol * 128 + 128],
                                HB[:, hidc * 512:hidc * 512 + 512], st, sp)),
                                reads=(w3r, R("HB")), writes=(RPS[4 + mol],))
                for mol in range(4):
                    mo = c * 4 + mol
                    fsl = FA[:, mo * 1024 + hf * 512:mo * 1024 + hf * 512 + 512]
                    psl = PS[:, (4 + mol) * 512:(4 + mol) * 512 + 512]
                    P.op("DVE", (lambda h, o=fsl, i=psl: h.tensor_copy(o, i)),
                         reads=(RPS[4 + mol],), writes=(R("FA"),))

    def ph_routed(blk):
        for p in range(8):
            pump(6)
            e0 = 2 * p
            for px in range(2):
                e = e0 + px
                for tb in range(2):
                    pb = px * 2 + tb
                    P.op("PE", (lambda h, e=e, tb=tb, pb=pb: MM(
                        h, PS[:, pb * 512:pb * 512 + 512],
                        SEL[:, e * 128:e * 128 + 128],
                        GFB[0:16, tb * 512:tb * 512 + 512], True, True)),
                        reads=(R("SEL"), R("GFB")), writes=(RPS[pb],))
                P.op("DVE", (lambda h, px=px: h.tensor_copy(
                    GSB[:, px * 1024:px * 1024 + 1024],
                    PS[:, px * 1024:px * 1024 + 1024])),
                    reads=(RPS[2 * px], RPS[2 * px + 1]), writes=(R("GSB"),))
            for px in range(2):
                e = e0 + px
                for c in range(2):
                    w1s, w1r = load_slab(f"rw1_{e}", 0, c)
                    w2s, w2r = load_slab(f"rw2_{e}", 0, c)
                    for rhl in range(4):
                        rh = c * 4 + rhl
                        for kc in range(8):
                            for tb in range(2):
                                P.op("PE", (lambda h, w=w1s, kc=kc, rhl=rhl,
                                            tb=tb: MM(
                                    h, PS[:, 1024 + tb * 512:1024 + tb * 512 + 512],
                                    w[:, kc * 512 + rhl * 128:
                                      kc * 512 + rhl * 128 + 128],
                                    XF[:, kc * 1024 + tb * 512:
                                       kc * 1024 + tb * 512 + 512],
                                    kc == 0, kc == 7)),
                                    reads=(w1r, R("XFh0"), R("XFh1")),
                                    writes=(RPS[2 + tb],))
                                P.op("PE", (lambda h, w=w2s, kc=kc, rhl=rhl,
                                            tb=tb: MM(
                                    h, PS[:, 2048 + tb * 512:2048 + tb * 512 + 512],
                                    w[:, kc * 512 + rhl * 128:
                                      kc * 512 + rhl * 128 + 128],
                                    XF[:, kc * 1024 + tb * 512:
                                       kc * 1024 + tb * 512 + 512],
                                    kc == 0, kc == 7)),
                                    reads=(w2r, R("XFh0"), R("XFh1")),
                                    writes=(RPS[4 + tb],))
                        es = ES0 if rh % 2 == 0 else ES1
                        esn = "ES0" if rh % 2 == 0 else "ES1"
                        era, erb = RES[(esn, 0)], RES[(esn, 1)]
                        P.op("ACT", (lambda h, o=es[:, 0:1024]: h.activation(
                            o, PS[:, 1024:2048], AF.Silu)),
                            reads=(RPS[2], RPS[3]), writes=(era, erb))
                        P.op("DVE", (lambda h, o=es[:, 0:1024]: h.tensor_tensor(
                            o, o, PS[:, 2048:3072], OP.mult)),
                            reads=(era, erb, RPS[4], RPS[5]), writes=(era, erb))
                        P.op("POOL", (lambda h, o=HB[:, px * 8192 + rh * 1024:
                                                     px * 8192 + rh * 1024 + 1024],
                                      i=es[:, 0:1024], px=px: h.tensor_tensor(
                            o, i, GSB[:, px * 1024:px * 1024 + 1024], OP.mult)),
                            reads=(era, erb, R("GSB")), writes=(R("HB"),))
            for c in range(2):
                w3a, w3ar = load_slab(f"rw3_{e0}", 0, c)
                w3b, w3br = load_slab(f"rw3_{e0 + 1}", 0, c)
                for mol in range(4):
                    mo = c * 4 + mol
                    pb = (mol % 2) * 2
                    for px, (w3s, w3r) in enumerate(((w3a, w3ar), (w3b, w3br))):
                        for kc in range(8):
                            for tb in range(2):
                                P.op("PE", (lambda h, w=w3s, kc=kc, mol=mol,
                                            px=px, tb=tb, pb=pb,
                                            st=(px == 0 and kc == 0),
                                            sp=(px == 1 and kc == 7): MM(
                                    h, PS[:, pb * 512 + tb * 512:
                                          pb * 512 + tb * 512 + 512],
                                    w[:, kc * 512 + mol * 128:
                                      kc * 512 + mol * 128 + 128],
                                    HB[:, px * 8192 + kc * 1024 + tb * 512:
                                       px * 8192 + kc * 1024 + tb * 512 + 512],
                                    st, sp)),
                                    reads=(w3r, R("HB")), writes=(RPS[pb + tb],))
                    fsl = FA[:, mo * 1024:mo * 1024 + 1024]
                    psl = PS[:, pb * 512:pb * 512 + 1024]
                    P.op("DVE", (lambda h, o=fsl, i=psl: h.tensor_tensor(
                        o, i, o, OP.add)),
                        reads=(RPS[pb], RPS[pb + 1], R("FA")), writes=(R("FA"),))

    def ph_qkv(l, hf):
        for (wkey, dst, dstn, b1c, esn) in (
                (f"Wq_{l}", QF, "QF", 0, "ES0"), (f"Wk_{l}", KF, "KF", 8, "ES1")):
            es = ES0 if esn == "ES0" else ES1
            bq1c = l * 16 + b1c
            bpc = bcol(l, "bq" if b1c == 0 else "bk")
            for sbk in range(2):
                ws, wr = load_slab(wkey, 0, sbk)
                if wkey.startswith("Wq") and sbk == 0:
                    tap("wslab", ws[:], (wr,))
                for mol in range(4):
                    mo = sbk * 4 + mol
                    pb = mol % 2
                    for kc in range(8):
                        P.op("PE", (lambda h, w=ws, kc=kc, mol=mol, hf=hf,
                                    pb=pb: MM(
                            h, PS[:, pb * 512:pb * 512 + 512],
                            w[:, kc * 512 + mol * 128:kc * 512 + mol * 128 + 128],
                            XF[:, kc * 1024 + hf * 512:kc * 1024 + hf * 512 + 512],
                            kc == 0, kc == 7)),
                            reads=(wr, R("XFh%d" % hf)), writes=(RPS[pb],))
                    if (wkey.startswith("Wq") and sbk == 0 and mol == 0
                            and "qpsum" in TAPS and "qpsum" not in tap_outs):
                        P.op("DVE", (lambda h: h.tensor_copy(
                            REC[:, 0:512], PS[:, 0:512])),
                            reads=(RPS[0],), writes=(R("REC"),))
                        tap("qpsum", REC[:, 0:512], (R("REC"),))
                    eh = es[:, 0:512] if mo % 2 == 0 else es[:, 512:1024]
                    ehr = RES[(esn, mo % 2)]
                    P.op("ACT", (lambda h, o=eh, pb=pb,
                                 b=BIA[:, bpc + mo:bpc + mo + 1]: h.activation(
                        o, PS[:, pb * 512:pb * 512 + 512], AF.Exp, bias=b)),
                        reads=(RPS[pb], R("BIA")), writes=(ehr,))
                    P.op("DVE", (lambda h, o=eh: h.tensor_scalar(
                        o, o, 1.0, 0.0, OP.min, OP.add)),
                        reads=(ehr,), writes=(ehr,))
                    P.op("DVE", (lambda h, o=dst[:, mo * 512:mo * 512 + 512],
                                 pb=pb, b=BQ1[:, bq1c + mo:bq1c + mo + 1],
                                 i=eh: h.scalar_tensor_tensor(
                        o, PS[:, pb * 512:pb * 512 + 512], b, i, OP.add, OP.max)),
                        reads=(RPS[pb], R("BQ1"), ehr), writes=(R(dstn),))
        PSb7 = PSbf[:, 7168:8192]
        for tcl in range(4):
            for dc in range(8):
                P.op("PE", (lambda h, dc=dc, tcl=tcl: h.transpose(
                    PSb7[:, dc * 128:dc * 128 + 128],
                    KF[:, dc * 512 + tcl * 128:dc * 512 + tcl * 128 + 128], IDB)),
                    reads=(R("KF"), R("CB")), writes=(RPS[7],))
            P.op("DVE", (lambda h, tcl=tcl: h.tensor_copy(
                KT[:, tcl * 1024:tcl * 1024 + 1024], PSb7)),
                reads=(RPS[7],), writes=(R("KT"),))
        row_dma(VROW[0:1, :], bass.AP(bv_d, l * D, [[D, 1], [1, D]]), R("VROW"),
                tag="vrow")
        for sbk in range(2):
            ws, wr = load_slab(f"Wv_{l}", 0, sbk)
            for tcl in range(4):
                pb = tcl % 2
                for kc in range(8):
                    P.op("PE", (lambda h, w=ws, kc=kc, tcl=tcl, hf=hf, pb=pb: MM(
                        h, PS[:, pb * 512:pb * 512 + 512],
                        XF[:, kc * 1024 + hf * 512 + tcl * 128:
                           kc * 1024 + hf * 512 + tcl * 128 + 128],
                        w[:, kc * 512:kc * 512 + 512], kc == 0, False)),
                        reads=(wr, R("XFh%d" % hf)), writes=(RPS[pb],))
                P.op("PE", (lambda h, sbk=sbk, pb=pb: MM(
                    h, PS[:, pb * 512:pb * 512 + 512], ONR[:],
                    VROW[0:1, sbk * 512:sbk * 512 + 512], False, True)),
                    reads=(R("ONR"), R("VROW")), writes=(RPS[pb],))
                P.op("ACT", (lambda h, tcl=tcl, sbk=sbk, pb=pb: h.activation(
                    VT[:, tcl * 1024 + sbk * 512:tcl * 1024 + sbk * 512 + 512],
                    PS[:, pb * 512:pb * 512 + 512], AF.Copy)),
                    reads=(RPS[pb],), writes=(R("VT"),))

    def ph_chunks(l, hf, dbg=False):
        kvr = R(f"KVS{l}")
        P.op("DVE", (lambda h: h.tensor_copy(KVB[:], KVS[l][:])),
             reads=(kvr,), writes=(R("KVB"),))
        for cl in (3, 2, 1, 0):
            def mm1(par):
                for s in range(8):
                    kf = KF[64 * par:64 * par + 64,
                            s * 512 + cl * 128:s * 512 + cl * 128 + 128]
                    qf = QF[64 * par:64 * par + 64,
                            s * 512 + cl * 128:s * 512 + cl * 128 + 128]
                    P.op("PE", (lambda h, kf=kf, qf=qf, s=s: MM(
                        h, PS[:, s * 128:s * 128 + 128], kf, qf, True, True)),
                        reads=(R("KF"), R("QF")), writes=(RPS[s // 4],))

            def maskev(par):
                P.op("DVE", (lambda h, par=par: h.tensor_tensor(
                    MA[:, par * 1024:par * 1024 + 1024], PS[:, 0:1024], MSK,
                    OP.mult)), reads=(RPS[0], RPS[1], R("CB")), writes=(R("MA"),))

            def mm4s():
                for par in range(2):
                    p0 = 64 * par
                    for s in range(8):
                        hh = 2 * s + par
                        kt = KT[:, cl * 1024 + hh * 64:cl * 1024 + hh * 64 + 64]
                        vt = VT[:, cl * 1024 + hh * 64:cl * 1024 + hh * 64 + 64]
                        P.op("PE", (lambda h, kt=kt, vt=vt, s=s, p0=p0: MM(
                            h, PS[p0:p0 + 64, 3072 + s * 128:3072 + s * 128 + 64],
                            kt, vt, True, True)),
                            reads=(R("KT"), R("VT")), writes=(RPS[6 + s // 4],))
                        P.op("PE", (lambda h, kt=kt, s=s, p0=p0: MM(
                            h, PS[p0:p0 + 64,
                                  3072 + s * 128 + 64:3072 + s * 128 + 128],
                            kt, ONC[:], True, True)),
                            reads=(R("KT"), R("CB")), writes=(RPS[6 + s // 4],))

            mm1(0)
            maskev(0)
            mm4s()
            mm1(1)
            P.op("DVE", (lambda h: h.tensor_tensor(
                KVS[l][:], KVS[l][:], PS[:, 3072:4096], OP.add)),
                reads=(kvr, RPS[6], RPS[7]), writes=(kvr,))
            maskev(1)
            for par in range(2):
                p0 = 64 * par
                for g in range(2):
                    P.op("PE", (lambda h, par=par, g=g, p0=p0: MM(
                        h, PS[p0:p0 + 64, 2048 + g * 512:2048 + g * 512 + 512],
                        ONC[:],
                        MA[:, par * 1024 + g * 512:par * 1024 + g * 512 + 512],
                        True, False)),
                        reads=(R("CB"), R("MA")), writes=(RPS[4 + g],))
                for s in range(8):
                    qf = QF[p0:p0 + 64, s * 512 + cl * 128:s * 512 + cl * 128 + 128]
                    P.op("PE", (lambda h, par=par, s=s, cl=cl, p0=p0: MM(
                        h, PS[p0:p0 + 64, 1024 + s * 128:1024 + s * 128 + 128],
                        VT[:, cl * 1024 + (2 * s + par) * 64:
                           cl * 1024 + (2 * s + par) * 64 + 64],
                        MA[:, par * 1024 + s * 128:par * 1024 + s * 128 + 128],
                        True, False)),
                        reads=(R("VT"), R("MA")), writes=(RPS[2 + s // 4],))
                    P.op("PE", (lambda h, s=s, qf=qf, p0=p0: MM(
                        h, PS[p0:p0 + 64, 1024 + s * 128:1024 + s * 128 + 128],
                        KVB[p0:p0 + 64, s * 128:s * 128 + 64], qf, False, True)),
                        reads=(R("KVB"), R("QF")), writes=(RPS[2 + s // 4],))
                    P.op("PE", (lambda h, s=s, qf=qf, p0=p0: MM(
                        h, PS[p0:p0 + 64, 2048 + s * 128:2048 + s * 128 + 128],
                        KVB[p0:p0 + 64, s * 128 + 64:s * 128 + 128], qf,
                        False, True)),
                        reads=(R("KVB"), R("QF")), writes=(RPS[4 + s // 4],))
            if dbg and cl == 3:
                tap("ma", MA[:], (R("MA"),))
                P.op("DVE", (lambda h: h.tensor_copy(ES1[:, 0:1024],
                                                     PS[:, 1024:2048])),
                     reads=(RPS[2], RPS[3]), writes=(RES[("ES1", 0)],
                                                     RES[("ES1", 1)]))
                tap("numo", ES1[:], (RES[("ES1", 0)], RES[("ES1", 1)]))
            P.op("ACT", (lambda h: h.activation(REC[:], PS[:, 2048:3072],
                                                AF.Ln)),
                 reads=(RPS[4], RPS[5]), writes=(R("REC"),))
            P.op("ACT", (lambda h: h.activation(REC[:], REC[:], AF.Exp,
                                                scale=-1.0)),
                 reads=(R("REC"),), writes=(R("REC"),))
            if dbg and cl == 3:
                tap("rec", REC[:], (R("REC"),))
            AO3 = AO[:].rearrange("p (a b) -> p a b", b=512)
            P.op("DVE", (lambda h, A=AO3, cl=cl: h.tensor_tensor(
                A[:, :, cl * 128:cl * 128 + 128],
                PS[:, 1024:2048].rearrange("p (a b) -> p a b", b=128),
                REC[:].rearrange("p (a b) -> p a b", b=128), OP.mult)),
                reads=(RPS[2], RPS[3], R("REC")), writes=(R("AO"),))
            P.op("DVE", (lambda h: h.tensor_copy(KVB[:], KVS[l][:])),
                 reads=(kvr,), writes=(R("KVB"),))

    def ph_oproj(l, hf):
        boc = bcol(l, "bo")
        for sbk in range(2):
            ws, wr = load_slab(f"Wo_{l}", 0, sbk)
            for mol in range(4):
                mo = sbk * 4 + mol
                pb = mol % 2
                for kc in range(8):
                    P.op("PE", (lambda h, w=ws, kc=kc, mol=mol, pb=pb: MM(
                        h, PS[:, pb * 512:pb * 512 + 512],
                        w[:, kc * 512 + mol * 128:kc * 512 + mol * 128 + 128],
                        AO[:, kc * 512:kc * 512 + 512], kc == 0, kc == 7)),
                        reads=(wr, R("AO")), writes=(RPS[pb],))
                xsl = XF[:, mo * 1024 + hf * 512:mo * 1024 + hf * 512 + 512]
                P.op("DVE", (lambda h, o=xsl, pb=pb,
                             b=BIA[:, boc + mo:boc + mo + 1]:
                             h.scalar_tensor_tensor(
                    o, PS[:, pb * 512:pb * 512 + 512], b, o, OP.add, OP.add)),
                    reads=(RPS[pb], R("BIA"), R("XFh%d" % hf)),
                    writes=(R("XFh%d" % hf),))

    def ph_gelu_ffn(l, hf):
        b1c = bcol(l, "b1")
        for sbk in range(8):
            ws, wr = load_slab(f"W1_{l}", 0, sbk)
            for hcl in range(4):
                hc = sbk * 4 + hcl
                pb = hc % 2
                for kc in range(8):
                    P.op("PE", (lambda h, w=ws, kc=kc, hcl=hcl, hf=hf, pb=pb: MM(
                        h, PS[:, pb * 512:pb * 512 + 512],
                        w[:, kc * 512 + hcl * 128:kc * 512 + hcl * 128 + 128],
                        XF[:, kc * 1024 + hf * 512:kc * 1024 + hf * 512 + 512],
                        kc == 0, kc == 7)),
                        reads=(wr, R("XFh%d" % hf)), writes=(RPS[pb],))
                P.op("ACT", (lambda h, hc=hc, pb=pb,
                             b=BIA[:, b1c + hc:b1c + hc + 1]: h.activation(
                    HB[:, hc * 512:hc * 512 + 512],
                    PS[:, pb * 512:pb * 512 + 512], AF.Gelu, bias=b)),
                    reads=(RPS[pb], R("BIA")), writes=(R("HB"),))
        b2c = bcol(l, "b2")
        for c in range(2):
            for r4 in range(4):
                ws, wr = load_slab(f"W2_{l}", r4, c)
                for kc in range(8):
                    hidc = r4 * 8 + kc
                    for mol in range(4):
                        P.op("PE", (lambda h, w=ws, kc=kc, mol=mol, hidc=hidc,
                                    st=(r4 == 0 and kc == 0),
                                    sp=(r4 == 3 and kc == 7): MM(
                            h, PS[:, (4 + mol) * 512:(4 + mol) * 512 + 512],
                            w[:, kc * 512 + mol * 128:kc * 512 + mol * 128 + 128],
                            HB[:, hidc * 512:hidc * 512 + 512], st, sp)),
                            reads=(wr, R("HB")), writes=(RPS[4 + mol],))
            for mol in range(4):
                mo = c * 4 + mol
                xsl = XF[:, mo * 1024 + hf * 512:mo * 1024 + hf * 512 + 512]
                P.op("DVE", (lambda h, o=xsl, mol=mol,
                             b=BIA[:, b2c + mo:b2c + mo + 1]:
                             h.scalar_tensor_tensor(
                    o, PS[:, (4 + mol) * 512:(4 + mol) * 512 + 512], b, o,
                    OP.add, OP.add)),
                    reads=(RPS[4 + mol], R("BIA"), R("XFh%d" % hf)),
                    writes=(R("XFh%d" % hf),))

    def ph_assembly(blk):
        PSb7 = PSbf[:, 7168:8192]
        PSb6 = PSbf[:, 6144:7168]
        for tc in range(8):
            r0 = blk * TBL + tc * 128
            row_dma(X0S[:], bass.AP(x_d, r0 * D, [[D, 128], [1, D]]),
                    R("X0S"), tag="x2")
            for dc in range(8):
                P.op("PE", (lambda h, dc=dc, tc=tc: h.transpose(
                    PSb7[:, dc * 128:dc * 128 + 128],
                    XF[:, dc * 1024 + tc * 128:dc * 1024 + tc * 128 + 128], IDB)),
                    reads=(R("XFh0"), R("XFh1"), R("CB")), writes=(RPS[7],))
                P.op("PE", (lambda h, dc=dc, tc=tc: h.transpose(
                    PSb6[:, dc * 128:dc * 128 + 128],
                    FA[:, dc * 1024 + tc * 128:dc * 1024 + tc * 128 + 128], IDB)),
                    reads=(R("FA"), R("CB")), writes=(RPS[6],))
            ot = OT[tc % 2]
            otr = R("OT%d" % (tc % 2))
            P.op("DVE", (lambda h, o=ot: h.tensor_tensor(
                o[:], PSb7, X0S[:], OP.add)),
                reads=(RPS[7], R("X0S")), writes=(otr,))
            P.op("DVE", (lambda h, o=ot: h.tensor_tensor(
                o[:], o[:], PSb6, OP.add)),
                reads=(otr, RPS[6]), writes=(otr,))
            P.dma("xio", (lambda h, i=ot, r0=r0: h.dma_start(
                out=bass.AP(out_d, r0 * D, [[D, 128], [1, D]]), in_=i[:])),
                reads=(otr,), writes=(R("outd"),), tag="st")

    ph_init()
    for blk in (3, 2, 1, 0):
        if blk == 3:
            pump(25)
        ph_xload(blk)
        for hf in (1, 0):
            ph_ln(None, hf)
        ph_router(blk)
        if DO_SHARED:
            pump(4)
            ph_shared(blk)
        else:
            for mo in range(8):
                P.op("DVE", (lambda h, mo=mo: h.memset(
                    FA[:, mo * 1024:mo * 1024 + 1024], 0.0)), writes=(R("FA"),))
        if DO_ROUTED:
            ph_routed(blk)
        pend = [None]

        def run_pend():
            if pend[0] is not None:
                pend[0]()
                pend[0] = None

        for l in range(N_LAYERS):
            if blk == 3:
                pump(12)
            ph_qkv(l, 1)
            run_pend()
            ph_chunks(l, 1)
            ph_oproj(l, 1)
            ph_qkv(l, 0)
            ph_ln((bcol(l, "g1"), bcol(l, "be1")), 1)
            ph_chunks(l, 0)
            ph_oproj(l, 0)
            ph_gelu_ffn(l, 1)
            ph_ln((bcol(l, "g1"), bcol(l, "be1")), 0)
            ph_ln((bcol(l, "g2"), bcol(l, "be2")), 1)
            ph_gelu_ffn(l, 0)
            pend[0] = (lambda l=l: ph_ln((bcol(l, "g2"), bcol(l, "be2")), 0))
        run_pend()
        if DO_FINAL_LN:
            for hf in (1, 0):
                ph_ln((GFCOL, BFCOL), hf)
        ph_assembly(blk)
    pump(len(pc_units))

    n_instr = sum(len(v) for v in P.ops.values())
    n_waits = sum(len(o["waits"]) for v in P.ops.values() for o in v)
    sems = {}
    with stack:
        for e in ENGS:
            sems[("e", e)] = stack.enter_context(nc.semaphore("s_" + e))
        for s in P.streams:
            sems[("d", s)] = stack.enter_context(nc.semaphore("sd_" + s))
        block = stack.enter_context(nc.Block())

        def emit(ename, handle):
            for o in P.ops[ename]:
                for semkey, val in o["waits"]:
                    handle.wait_ge(sems[semkey], val)
                inst = o["fn"](handle)
                for semkey, amt in o["incs"]:
                    inst.then_inc(sems[semkey], amt)

        @block.tensor
        def _(h):
            emit("PE", h)

        @block.vector
        def _(h):
            emit("DVE", h)

        @block.scalar
        def _(h):
            emit("ACT", h)

        @block.gpsimd
        def _(h):
            emit("POOL", h)

        @block.sync
        def _(h):
            emit("SP", h)
            h.wait_ge(sems[("d", "xio")], P.streams["xio"]["count"])

    return nc, n_instr, n_waits


def _prep_core_inputs(inputs, core):
    d = {}
    d["x"] = np.ascontiguousarray(np.asarray(inputs["L0_attn_x"][core],
                                             dtype=np.float32))
    for k, src in (("Wq", "L0_attn_Wq"), ("Wk", "L0_attn_Wk"),
                   ("Wv", "L0_attn_Wv"), ("Wo", "L0_attn_Wo"),
                   ("W1", "L0_attn_W1"), ("W2", "L0_attn_W2"),
                   ("sw1", "L0_ffn_shared_w1"), ("sw2", "L0_ffn_shared_w2"),
                   ("sw3", "L0_ffn_shared_w3"), ("rw1", "L0_ffn_routed_w1"),
                   ("rw2", "L0_ffn_routed_w2"), ("rw3", "L0_ffn_routed_w3"),
                   ("router", "L0_ffn_router_w"), ("bv_rows", "L0_attn_bv")):
        d[k] = np.ascontiguousarray(np.asarray(inputs[src], dtype=np.float32))
    return d


def _prep_shared(inputs):
    f32 = np.float32
    bf16 = ml_dtypes.bfloat16

    def pp(v):
        return np.ascontiguousarray(np.asarray(v, dtype=f32).reshape(-1, 128).T)

    cols = []
    for l in range(NL):
        for key in ("L0_attn_bq", "L0_attn_bk", "L0_attn_bv", "L0_attn_bo",
                    "L0_attn_b2"):
            cols.append(pp(inputs[key][l]))
        cols.append(pp(inputs["L0_attn_b1"][l]))
        for key in ("L0_attn_g1", "L0_attn_be1", "L0_attn_g2", "L0_attn_be2"):
            cols.append(pp(inputs[key][l]))
    cols.append(pp(inputs["L0_attn_gf"]))
    cols.append(pp(inputs["L0_attn_bf"]))
    bias_pp = np.concatenate(cols, axis=1).astype(f32)
    assert bias_pp.shape == (128, 640), bias_pp.shape

    idn = np.eye(128, dtype=f32)
    mask = np.tril(np.ones((128, 128), dtype=f32))
    cb = np.concatenate([idn, np.tile(mask, (1, 8)),
                         np.ones((128, 64), dtype=f32)], axis=1).astype(bf16)
    cf = np.concatenate(
        [idn, np.tile(np.asarray(inputs["L0_ffn_expert_bias"],
                                 dtype=f32)[None, :], (128, 1))],
        axis=1).astype(f32)
    sel = np.zeros((16, 2048), dtype=f32)
    for e in range(16):
        sel[e, e * 128:(e + 1) * 128] = 1.0
    return {"bias_pp": bias_pp, "cst_bf": np.ascontiguousarray(cb),
            "cst_f32": np.ascontiguousarray(cf),
            "ones_row": np.ones((1, 128), dtype=f32),
            "sel": np.ascontiguousarray(sel.astype(bf16))}


def kernel(**inputs):
    shared = _prep_shared(inputs)
    in_maps = []
    for c in range(8):
        m = _prep_core_inputs(inputs, c)
        m.update(shared)
        in_maps.append(m)
    nc, n_instr, n_waits = _build()
    res = run_bass_kernel_spmd(nc, in_maps, core_ids=list(range(8)))
    out = np.stack([np.asarray(r["out"], dtype=np.float32)
                    for r in res.results], axis=0)
    if TAPS:
        kernel.taps = {k: np.asarray(v) for k, v in res.results[0].items()
                       if k.startswith("tap_")}
    return out


# version 03
# speedup vs baseline: 1.1723x; 1.1723x over previous
import bisect
import contextlib
import numpy as np
import ml_dtypes
import concourse.bass as bass
import concourse.mybir as mybir
from concourse.bass_utils import run_bass_kernel_spmd

F32 = mybir.dt.float32
BF16 = mybir.dt.bfloat16
AF = mybir.ActivationFunctionType
OP = mybir.AluOpType
AX = mybir.AxisListType

D = 1024
T = 4096
NB = 4
TBL = 1024
NL = 6
NH = 16
DH = 64
NE = 16
DFF = 4096
LN_EPS = 1e-5

TAPS = ()
N_LAYERS = NL
DO_SHARED = True
DO_ROUTED = True
DO_FINAL_LN = True

ENGS = ("PE", "DVE", "ACT", "POOL", "SP")


class Plan:
    def __init__(self):
        self.ops = {e: [] for e in ENGS}
        self.incs = {e: [] for e in ENGS}
        self.ninc = {e: 0 for e in ENGS}
        self.hwm = {e: {} for e in ENGS}
        self.streams = {}
        self.regions = {}

    def stream(self, name, eng):
        self.streams[name] = {"eng": eng, "count": 0}

    def region(self, name):
        r = self.regions.get(name)
        if r is None:
            r = {"name": name, "w": None, "rd": {}}
            self.regions[name] = r
        return r

    def _eng_inc(self, ename, seq):
        lst = self.incs[ename]
        i = bisect.bisect_left(lst, (seq, -1))
        if i < len(lst):
            return lst[i][1]
        self.ninc[ename] += 1
        v = self.ninc[ename]
        lst.append((seq, v))
        self.ops[ename][seq]["incs"].append((("e", ename), 1))
        return v

    def _resolve(self, cons_eng, prods):
        waits = {}
        for p in prods:
            if p is None:
                continue
            if p[0] == "e":
                semkey = ("e", p[1])
                val = self._eng_inc(p[1], p[2])
            else:
                semkey = ("d", p[1])
                val = p[2]
            if val > self.hwm[cons_eng].get(semkey, 0):
                if val > waits.get(semkey, 0):
                    waits[semkey] = val
        out = list(waits.items())
        for semkey, val in out:
            self.hwm[cons_eng][semkey] = val
        return out

    def op(self, eng, fn, reads=(), writes=(), tag=""):
        prods = []
        for r in reads:
            if r["w"] is not None:
                prods.append(r["w"])
        for wr in writes:
            w = wr["w"]
            if w is not None and not (w[0] == "e" and w[1] == eng):
                prods.append(w)
            for k, v in wr["rd"].items():
                if k[0] == "e":
                    if k[1] == eng:
                        continue
                    prods.append(("e", k[1], v))
                else:
                    prods.append(("d", k[1], v))
        waits = self._resolve(eng, prods)
        seq = len(self.ops[eng])
        self.ops[eng].append({"fn": fn, "waits": waits, "incs": [], "tag": tag})
        for r in reads:
            r["rd"][("e", eng)] = seq
        for wr in writes:
            wr["w"] = ("e", eng, seq)
            wr["rd"] = {}
        return seq

    def dma(self, sname, fn, reads=(), writes=(), tag=""):
        st = self.streams[sname]
        eng = st["eng"]
        prods = []
        for r in reads:
            if r["w"] is not None:
                prods.append(r["w"])
        for wr in writes:
            if wr["w"] is not None:
                prods.append(wr["w"])
            for k, v in wr["rd"].items():
                prods.append(("e", k[1], v) if k[0] == "e" else ("d", k[1], v))
        waits = self._resolve(eng, prods)
        st["count"] += 16
        cnt = st["count"]
        self.ops[eng].append(
            {"fn": fn, "waits": waits, "incs": [(("d", sname), 16)], "tag": tag})
        for r in reads:
            r["rd"][("d", sname)] = cnt
        for wr in writes:
            wr["w"] = ("d", sname, cnt)
            wr["rd"] = {}
        return cnt


def _build():
    nc = bass.Bass("TRN2", target_bir_lowering=False)
    P = Plan()
    P.stream("w", "SP")
    P.stream("pc", "POOL")
    P.stream("xio", "POOL")

    dt_in = {}

    def din(name, shape, dt=F32):
        dt_in[name] = nc.dram_tensor(name, list(shape), dt, kind="ExternalInput")
        return dt_in[name]

    x_d = din("x", (T, D))
    wq_d = din("Wq", (NL, D, D)); wk_d = din("Wk", (NL, D, D))
    wv_d = din("Wv", (NL, D, D)); wo_d = din("Wo", (NL, D, D))
    w1_d = din("W1", (NL, D, DFF)); w2_d = din("W2", (NL, DFF, D))
    sw1_d = din("sw1", (1, D, DFF)); sw2_d = din("sw2", (1, D, DFF))
    sw3_d = din("sw3", (1, DFF, D))
    rw1_d = din("rw1", (NE, D, D)); rw2_d = din("rw2", (NE, D, D))
    rw3_d = din("rw3", (NE, D, D))
    rout_d = din("router", (D, NE))
    bv_d = din("bv_rows", (NL, D))
    bia_d = din("bias_pp", (128, 640))
    cb_d = din("cst_bf", (128, 1216), BF16)
    cf_d = din("cst_f32", (128, 144))
    or_d = din("ones_row", (1, 128))
    sel_d = din("sel", (16, 2048), BF16)
    out_d = nc.dram_tensor("out", [T, D], F32, kind="ExternalOutput")

    scr = {}
    offs = [0, 0]

    def salloc(grp, key, din_, dout_):
        scr[key] = (grp, offs[grp], din_, dout_)
        offs[grp] += din_ * dout_

    salloc(0, "router", D, NE)
    salloc(0, "sw1", D, DFF); salloc(0, "sw2", D, DFF); salloc(0, "sw3", DFF, D)
    for e in range(NE):
        salloc(0, f"rw1_{e}", D, D); salloc(0, f"rw2_{e}", D, D)
        salloc(0, f"rw3_{e}", D, D)
    for l in range(NL):
        salloc(1, f"Wq_{l}", D, D); salloc(1, f"Wk_{l}", D, D)
        salloc(1, f"Wv_{l}", D, D); salloc(1, f"Wo_{l}", D, D)
        salloc(1, f"W1_{l}", D, DFF); salloc(1, f"W2_{l}", DFF, D)
    wscs = [nc.dram_tensor("wsc0", [offs[0]], BF16),
            nc.dram_tensor("wsc1", [offs[1]], BF16)]

    def src_of(key):
        if key == "router":
            return rout_d, 0
        if key in ("sw1", "sw2", "sw3"):
            return {"sw1": sw1_d, "sw2": sw2_d, "sw3": sw3_d}[key], 0
        kind, idx = key.split("_")
        idx = int(idx)
        m = {"rw1": rw1_d, "rw2": rw2_d, "rw3": rw3_d,
             "Wq": wq_d, "Wk": wk_d, "Wv": wv_d, "Wo": wo_d,
             "W1": w1_d, "W2": w2_d}[kind]
        sz = D * DFF if kind in ("W1", "W2") else D * D
        return m, idx * sz

    pc_units = []

    def pc_add(key):
        grp, base, din_, dout_ = scr[key]
        n = din_ * dout_
        u = 0
        while u < n:
            sz = min(1024 * 1024, n - u)
            pc_units.append((key, u, sz))
            u += sz

    pc_add("router"); pc_add("sw1"); pc_add("sw2"); pc_add("sw3")
    for e in range(NE):
        pc_add(f"rw1_{e}"); pc_add(f"rw2_{e}"); pc_add(f"rw3_{e}")
    for l in range(NL):
        for k in ("Wq", "Wk", "Wv", "Wo", "W1", "W2"):
            pc_add(f"{k}_{l}")
    pc_pos = [0]
    pc_last_unit = {}
    for i, (key, u, sz) in enumerate(pc_units):
        pc_last_unit[key] = i

    def pump_until(key):
        while pc_pos[0] <= pc_last_unit[key]:
            pump(1)

    def pump(n):
        while n > 0 and pc_pos[0] < len(pc_units):
            key, u, sz = pc_units[pc_pos[0]]
            pc_pos[0] += 1
            n -= 1
            grp, base, _, _ = scr[key]
            src, soff = src_of(key)
            in_ap = bass.AP(src, soff + u, [[1, sz]])
            out_ap = bass.AP(wscs[grp], base + u, [[1, sz]])
            P.dma("pc", (lambda h, o=out_ap, i=in_ap: h.dma_start(out=o, in_=i)),
                  writes=(P.region("wsc_" + key),), tag="pc")

    stack = contextlib.ExitStack()

    def sbt(name, shape, dt):
        return stack.enter_context(nc.sbuf_tensor(name, list(shape), dt))

    XF = sbt("XF", (128, 8192), BF16)
    FA = sbt("FA", (128, 8192), BF16)
    QF = sbt("QF", (128, 4096), BF16)
    KF = sbt("KF", (128, 4096), BF16)
    KT = sbt("KT", (128, 4096), BF16)
    VT = sbt("VT", (128, 4096), BF16)
    AO = sbt("AO", (128, 4096), BF16)
    HB = sbt("HB", (128, 16384), BF16)
    MA = sbt("MA", (128, 2048), BF16)
    REC = sbt("REC", (128, 1024), F32)
    ES0 = sbt("ES0", (128, 1024), BF16)
    ES1 = sbt("ES1", (128, 1024), BF16)
    WBUF = [sbt("WBA", (128, 4096), BF16), sbt("WBB", (128, 4096), BF16),
            sbt("WBC", (128, 4096), BF16)]
    KVS = [sbt(f"KVS{l}", (128, 1024), F32) for l in range(NL)]
    KVB = sbt("KVB", (128, 1024), BF16)
    X0S = sbt("X0S", (128, 1024), F32)
    X0B = sbt("X0B", (128, 1024), BF16)
    OT = [sbt("OT0", (128, 1024), F32)]
    ROWS = sbt("ROWS", (1, 2048), F32)
    GFB = sbt("GFB", (16, 1024), BF16)
    SCB = sbt("SCB", (128, 128), F32)
    G1 = sbt("G1", (128, 16), F32)
    G2 = sbt("G2", (128, 16), F32)
    G3 = sbt("G3", (128, 16), F32)
    GM = sbt("GM", (128, 1), F32)
    RWB = sbt("RWB", (128, 128), BF16)
    VROW = sbt("VROW", (1, 512), F32)
    GSB = sbt("GSB", (128, 2048), BF16)
    BQ1 = sbt("BQ1", (128, 96), F32)
    BIA = sbt("BIA", (128, 640), F32)
    CB = sbt("CB", (128, 1216), BF16)
    CF = sbt("CF", (128, 144), F32)
    ONR = sbt("ONR", (1, 128), F32)
    SEL = sbt("SEL", (16, 2048), BF16)
    PS = stack.enter_context(nc.psum_tensor("PS", [128, 4096], F32))
    PSbf = PS[:].bitcast(BF16)

    IDB = CB[:, 0:128]
    MSK = CB[:, 128:1152]
    ONC = CB[:, 1152:1216]
    IDF = CF[:, 0:128]
    EBC = CF[:, 128:144]

    R = P.region
    RPS = [R(f"psb{b}") for b in range(8)]
    RES = {("ES0", 0): R("ES0a"), ("ES0", 1): R("ES0b"),
           ("ES1", 0): R("ES1a"), ("ES1", 1): R("ES1b")}

    def bcol(l, name):
        base = {"bq": 0, "bk": 8, "bv": 16, "bo": 24, "b2": 32, "b1": 40,
                "g1": 72, "be1": 80, "g2": 88, "be2": 96}[name]
        return l * 104 + base

    GFCOL, BFCOL = 624, 632

    def MM(h, out, lhsT, rhs, start, stop):
        return h.matmul(out, lhsT, rhs, start=start, stop=stop,
                        skip_group_check=True)

    def load_slab(key, r, c, tag="w"):
        pump_until(key)
        grp, base, din_, dout_ = scr[key]
        buf = WBUF[load_slab.idx % 3]
        breg = R("WBUF%d" % (load_slab.idx % 3))
        load_slab.idx += 1
        o = base + r * 1024 * dout_ + c * 512
        in_ap = bass.AP(wscs[grp], o, [[dout_, 128], [128 * dout_, 8], [1, 512]])
        out_ap = buf[:].rearrange("p (k c) -> p k c", c=512)
        P.dma("w", (lambda h, oa=out_ap, ia=in_ap: h.dma_start(out=oa, in_=ia)),
              reads=(R("wsc_" + key),), writes=(breg,), tag=tag)
        return buf, breg
    load_slab.idx = 0

    def row_dma(dst_ap, src_ap, wreg, tag="row"):
        P.dma("xio", (lambda h, o=dst_ap, i=src_ap: h.dma_start(out=o, in_=i)),
              writes=(wreg,), tag=tag)

    tap_outs = {}

    def tap(name, buf_ap, rregs):
        if name not in TAPS or name in tap_outs:
            return
        shp = list(buf_ap.shape)
        td = nc.dram_tensor("tap_" + name, shp, buf_ap.dtype, kind="ExternalOutput")
        tap_outs[name] = td
        P.dma("xio", (lambda h, o=td.ap(), i=buf_ap: h.dma_start(out=o, in_=i)),
              reads=tuple(rregs), writes=(R("tapd_" + name),), tag="tap")

    def ph_init():
        row_dma(BIA[:], bia_d.ap(), R("BIA"))
        row_dma(CB[:], cb_d.ap(), R("CB"))
        row_dma(CF[:], cf_d.ap(), R("CF"))
        row_dma(ONR[:], or_d.ap(), R("ONR"))
        row_dma(SEL[:], sel_d.ap(), R("SEL"))
        for l in range(NL):
            for (o0, name) in ((0, "bq"), (8, "bk")):
                c = bcol(l, name)
                P.op("DVE", (lambda h, l=l, o0=o0, c=c: h.tensor_scalar(
                    BQ1[:, l * 16 + o0:l * 16 + o0 + 8], BIA[:, c:c + 8],
                    1.0, 0.0, OP.add, OP.add)),
                    reads=(R("BIA"),), writes=(R("BQ1"),))
            P.op("DVE", (lambda h, l=l: h.memset(KVS[l][:], 0.0)),
                 writes=(R(f"KVS{l}"),))

    def ph_xload(blk):
        XF3 = XF[:].rearrange("p (a b) -> p a b", b=1024)
        PSb7 = PSbf[:, 7168:8192]
        PSb7v = PSb7.rearrange("p (a b) -> p a b", b=128)
        for tc in range(8):
            r0 = blk * TBL + tc * 128
            row_dma(X0S[:], bass.AP(x_d, r0 * D, [[D, 128], [1, D]]),
                    R("X0S"), tag="xld")
            P.op("DVE", (lambda h: h.tensor_copy(X0B[:], X0S[:])),
                 reads=(R("X0S"),), writes=(R("X0B"),))
            for dc in range(8):
                P.op("PE", (lambda h, dc=dc: h.transpose(
                    PSb7[:, dc * 128:dc * 128 + 128],
                    X0B[:, dc * 128:dc * 128 + 128], IDB)),
                    reads=(R("X0B"), R("CB")), writes=(RPS[7],))
            P.op("DVE", (lambda h, tc=tc: h.tensor_copy(
                XF3[:, :, tc * 128:tc * 128 + 128], PSb7v)),
                reads=(RPS[7],), writes=(R("XFh0"), R("XFh1")))

    def ph_ln_stats(hf):
        for dc in range(8):
            xsl = XF[:, dc * 1024 + hf * 512:dc * 1024 + hf * 512 + 512]
            xq = ES1[:, 0:512] if dc % 2 == 0 else ES1[:, 512:1024]
            xqr = RES[("ES1", dc % 2)]
            P.op("ACT", (lambda h, o=xq, i=xsl: h.activation(o, i, AF.Square)),
                 reads=(R("XFh%d" % hf),), writes=(xqr,))
            P.op("PE", (lambda h, i=xsl, st=(dc == 0), sp=(dc == 7): MM(
                h, PS[0:1, 1024:1536], ONC[:, 0:1], i, st, sp)),
                reads=(R("XFh%d" % hf), R("CB")), writes=(RPS[2],))
            P.op("PE", (lambda h, i=xq, st=(dc == 0), sp=(dc == 7): MM(
                h, PS[0:1, 1536:2048], ONC[:, 0:1], i, st, sp)),
                reads=(xqr, R("CB")), writes=(RPS[3],))
        P.op("DVE", (lambda h: h.tensor_scalar(
            ROWS[0:1, 0:1024], PS[0:1, 1024:2048], 1.0 / D, 0.0, OP.mult, OP.add)),
            reads=(RPS[2], RPS[3]), writes=(R("ROWS"),))
        P.op("DVE", (lambda h: h.tensor_tensor(
            ROWS[0:1, 1024:1536], ROWS[0:1, 0:512], ROWS[0:1, 0:512], OP.mult)),
            reads=(R("ROWS"),), writes=(R("ROWS"),))
        P.op("DVE", (lambda h: h.scalar_tensor_tensor(
            ROWS[0:1, 512:1024], ROWS[0:1, 512:1024], LN_EPS,
            ROWS[0:1, 1024:1536], OP.add, OP.subtract)),
            reads=(R("ROWS"),), writes=(R("ROWS"),))
        P.op("ACT", (lambda h: h.activation(
            ROWS[0:1, 1024:1536], ROWS[0:1, 512:1024], AF.Ln)),
            reads=(R("ROWS"),), writes=(R("ROWS"),))
        P.op("ACT", (lambda h: h.activation(
            ROWS[0:1, 1536:2048], ROWS[0:1, 1024:1536], AF.Exp, scale=-0.5)),
            reads=(R("ROWS"),), writes=(R("ROWS"),))
        P.op("DVE", (lambda h: h.tensor_tensor(
            ROWS[0:1, 1024:1536], ROWS[0:1, 0:512], ROWS[0:1, 1536:2048],
            OP.mult)), reads=(R("ROWS"),), writes=(R("ROWS"),))

    def ph_ln_apply(gcol, hf):
        P.op("PE", (lambda h: MM(h, PS[:, 1024:1536], ONR[:],
                                 ROWS[0:1, 1536:2048], True, True)),
             reads=(R("ROWS"), R("ONR")), writes=(RPS[2],))
        P.op("PE", (lambda h: MM(h, PS[:, 1536:2048], ONR[:],
                                 ROWS[0:1, 1024:1536], True, True)),
             reads=(R("ROWS"), R("ONR")), writes=(RPS[3],))
        for dc in range(8):
            xsl = XF[:, dc * 1024 + hf * 512:dc * 1024 + hf * 512 + 512]
            es = ES0[:, 0:512] if dc % 2 == 0 else ES0[:, 512:1024]
            esr = RES[("ES0", dc % 2)]
            P.op("DVE", (lambda h, o=es, i=xsl: h.tensor_tensor(
                o, i, PS[:, 1024:1536], OP.mult)),
                reads=(R("XFh%d" % hf), RPS[2]), writes=(esr,))
            if gcol is None:
                P.op("DVE", (lambda h, o=xsl, i=es: h.tensor_tensor(
                    o, i, PS[:, 1536:2048], OP.subtract)),
                    reads=(esr, RPS[3]), writes=(R("XFh%d" % hf),))
            else:
                P.op("DVE", (lambda h, o=es, i=es: h.tensor_tensor(
                    o, i, PS[:, 1536:2048], OP.subtract)),
                    reads=(esr, RPS[3]), writes=(esr,))
                gc, bc = gcol
                P.op("DVE", (lambda h, o=xsl, i=es, g=BIA[:, gc + dc:gc + dc + 1],
                             b=BIA[:, bc + dc:bc + dc + 1]: h.tensor_scalar(
                    o, i, g, b, OP.mult, OP.add)),
                    reads=(esr, R("BIA")), writes=(R("XFh%d" % hf),))

    def ph_ln(gcol, hf):
        ph_ln_stats(hf)
        ph_ln_apply(gcol, hf)

    def ph_router(blk):
        pump_until("router")
        grp, base, _, _ = scr["router"]
        in_ap = bass.AP(wscs[grp], base, [[16, 128], [128 * 16, 8], [1, 16]])
        out_ap = RWB[:].rearrange("p (k c) -> p k c", c=16)
        P.dma("w", (lambda h, o=out_ap, i=in_ap: h.dma_start(out=o, in_=i)),
              reads=(R("wsc_router"),), writes=(R("RWB"),))
        for tc in range(8):
            pb = tc % 2
            for dc in range(8):
                P.op("PE", (lambda h, dc=dc, tc=tc, pb=pb: MM(
                    h, PS[:, pb * 512:pb * 512 + 16],
                    XF[:, dc * 1024 + tc * 128:dc * 1024 + tc * 128 + 128],
                    RWB[:, dc * 16:dc * 16 + 16], dc == 0, dc == 7)),
                    reads=(R("XFh0"), R("XFh1"), R("RWB")), writes=(RPS[pb],))
            P.op("ACT", (lambda h, tc=tc, pb=pb: h.activation(
                SCB[:, tc * 16:tc * 16 + 16], PS[:, pb * 512:pb * 512 + 16],
                AF.Sigmoid)), reads=(RPS[pb],), writes=(R("SCB"),))
            ssl = SCB[:, tc * 16:tc * 16 + 16]
            P.op("DVE", (lambda h, s=ssl: h.tensor_tensor(G1[:], s, EBC, OP.add)),
                 reads=(R("SCB"), R("CF")), writes=(R("G1"),))
            P.op("DVE", (lambda h: h.reduce_max(GM[:], G1[:], axis=AX.X)),
                 reads=(R("G1"),), writes=(R("GM"),))
            P.op("DVE", (lambda h: h.tensor_scalar(
                G2[:], G1[:], GM[:], 0.0, OP.is_equal, OP.add)),
                reads=(R("G1"), R("GM")), writes=(R("G2"),))
            P.op("DVE", (lambda h: h.scalar_tensor_tensor(
                G3[:], G2[:], -1e30, G1[:], OP.mult, OP.add)),
                reads=(R("G2"), R("G1")), writes=(R("G3"),))
            P.op("DVE", (lambda h: h.reduce_max(GM[:], G3[:], axis=AX.X)),
                 reads=(R("G3"),), writes=(R("GM"),))
            P.op("DVE", (lambda h: h.tensor_scalar(
                G3[:], G1[:], GM[:], 0.0, OP.is_equal, OP.add)),
                reads=(R("G1"), R("GM")), writes=(R("G3"),))
            P.op("DVE", (lambda h: h.tensor_tensor(G2[:], G2[:], G3[:], OP.add)),
                 reads=(R("G2"), R("G3")), writes=(R("G2"),))
            P.op("DVE", (lambda h, s=ssl: h.tensor_tensor(G1[:], s, G2[:],
                                                          OP.mult)),
                 reads=(R("SCB"), R("G2")), writes=(R("G1"),))
            P.op("PE", (lambda h: h.transpose(PS[0:16, 1024:1152], G1[:], IDF)),
                 reads=(R("G1"), R("CF")), writes=(RPS[2],))
            P.op("DVE", (lambda h, tc=tc: h.tensor_copy(
                GFB[0:16, tc * 128:tc * 128 + 128], PS[0:16, 1024:1152])),
                reads=(RPS[2],), writes=(R("GFB"),))

    def ph_shared(blk):
        for hf in (1, 0):
            for sbk in range(8):
                w1s, w1r = load_slab("sw1", 0, sbk)
                w2s, w2r = load_slab("sw2", 0, sbk)
                for hcl in range(4):
                    hc = sbk * 4 + hcl
                    for kc in range(8):
                        P.op("PE", (lambda h, w=w1s, kc=kc, hcl=hcl, hf=hf: MM(
                            h, PS[:, 1024:1536],
                            w[:, kc * 512 + hcl * 128:kc * 512 + hcl * 128 + 128],
                            XF[:, kc * 1024 + hf * 512:kc * 1024 + hf * 512 + 512],
                            kc == 0, kc == 7)),
                            reads=(w1r, R("XFh%d" % hf)), writes=(RPS[2],))
                        P.op("PE", (lambda h, w=w2s, kc=kc, hcl=hcl, hf=hf: MM(
                            h, PS[:, 1536:2048],
                            w[:, kc * 512 + hcl * 128:kc * 512 + hcl * 128 + 128],
                            XF[:, kc * 1024 + hf * 512:kc * 1024 + hf * 512 + 512],
                            kc == 0, kc == 7)),
                            reads=(w2r, R("XFh%d" % hf)), writes=(RPS[3],))
                    es = ES0[:, 0:512] if hc % 2 == 0 else ES0[:, 512:1024]
                    esr = RES[("ES0", hc % 2)]
                    P.op("ACT", (lambda h, o=es: h.activation(
                        o, PS[:, 1024:1536], AF.Silu)),
                        reads=(RPS[2],), writes=(esr,))
                    P.op("DVE", (lambda h, o=HB[:, hc * 512:hc * 512 + 512],
                                 i=es: h.tensor_tensor(
                        o, i, PS[:, 1536:2048], OP.mult)),
                        reads=(esr, RPS[3]), writes=(R("HB"),))
            for c in range(2):
                for r4 in range(4):
                    w3s, w3r = load_slab("sw3", r4, c)
                    for mol in range(4):
                        for kc in range(8):
                            hidc = r4 * 8 + kc
                            P.op("PE", (lambda h, w=w3s, kc=kc, mol=mol,
                                        hidc=hidc, st=(r4 == 0 and kc == 0),
                                        sp=(r4 == 3 and kc == 7): MM(
                                h, PS[:, (4 + mol) * 512:(4 + mol) * 512 + 512],
                                w[:, kc * 512 + mol * 128:kc * 512 + mol * 128 + 128],
                                HB[:, hidc * 512:hidc * 512 + 512], st, sp)),
                                reads=(w3r, R("HB")), writes=(RPS[4 + mol],))
                for mol in range(4):
                    mo = c * 4 + mol
                    fsl = FA[:, mo * 1024 + hf * 512:mo * 1024 + hf * 512 + 512]
                    psl = PS[:, (4 + mol) * 512:(4 + mol) * 512 + 512]
                    P.op("DVE", (lambda h, o=fsl, i=psl: h.tensor_copy(o, i)),
                         reads=(RPS[4 + mol],), writes=(R("FA"),))

    def ph_routed(blk):
        for p in range(8):
            pump(6)
            e0 = 2 * p
            for px in range(2):
                e = e0 + px
                for tb in range(2):
                    pb = px * 2 + tb
                    P.op("PE", (lambda h, e=e, tb=tb, pb=pb: MM(
                        h, PS[:, pb * 512:pb * 512 + 512],
                        SEL[:, e * 128:e * 128 + 128],
                        GFB[0:16, tb * 512:tb * 512 + 512], True, True)),
                        reads=(R("SEL"), R("GFB")), writes=(RPS[pb],))
                P.op("DVE", (lambda h, px=px: h.tensor_copy(
                    GSB[:, px * 1024:px * 1024 + 1024],
                    PS[:, px * 1024:px * 1024 + 1024])),
                    reads=(RPS[2 * px], RPS[2 * px + 1]), writes=(R("GSB"),))
            for px in range(2):
                e = e0 + px
                for c in range(2):
                    w1s, w1r = load_slab(f"rw1_{e}", 0, c)
                    w2s, w2r = load_slab(f"rw2_{e}", 0, c)
                    for rhl in range(4):
                        rh = c * 4 + rhl
                        for kc in range(8):
                            for tb in range(2):
                                P.op("PE", (lambda h, w=w1s, kc=kc, rhl=rhl,
                                            tb=tb: MM(
                                    h, PS[:, 1024 + tb * 512:1024 + tb * 512 + 512],
                                    w[:, kc * 512 + rhl * 128:
                                      kc * 512 + rhl * 128 + 128],
                                    XF[:, kc * 1024 + tb * 512:
                                       kc * 1024 + tb * 512 + 512],
                                    kc == 0, kc == 7)),
                                    reads=(w1r, R("XFh0"), R("XFh1")),
                                    writes=(RPS[2 + tb],))
                                P.op("PE", (lambda h, w=w2s, kc=kc, rhl=rhl,
                                            tb=tb: MM(
                                    h, PS[:, 2048 + tb * 512:2048 + tb * 512 + 512],
                                    w[:, kc * 512 + rhl * 128:
                                      kc * 512 + rhl * 128 + 128],
                                    XF[:, kc * 1024 + tb * 512:
                                       kc * 1024 + tb * 512 + 512],
                                    kc == 0, kc == 7)),
                                    reads=(w2r, R("XFh0"), R("XFh1")),
                                    writes=(RPS[4 + tb],))
                        es = ES0 if rh % 2 == 0 else ES1
                        esn = "ES0" if rh % 2 == 0 else "ES1"
                        era, erb = RES[(esn, 0)], RES[(esn, 1)]
                        P.op("ACT", (lambda h, o=es[:, 0:1024]: h.activation(
                            o, PS[:, 1024:2048], AF.Silu)),
                            reads=(RPS[2], RPS[3]), writes=(era, erb))
                        P.op("DVE", (lambda h, o=es[:, 0:1024]: h.tensor_tensor(
                            o, o, PS[:, 2048:3072], OP.mult)),
                            reads=(era, erb, RPS[4], RPS[5]), writes=(era, erb))
                        P.op("POOL", (lambda h, o=HB[:, px * 8192 + rh * 1024:
                                                     px * 8192 + rh * 1024 + 1024],
                                      i=es[:, 0:1024], px=px: h.tensor_tensor(
                            o, i, GSB[:, px * 1024:px * 1024 + 1024], OP.mult)),
                            reads=(era, erb, R("GSB")), writes=(R("HB"),))
            for c in range(2):
                w3a, w3ar = load_slab(f"rw3_{e0}", 0, c)
                w3b, w3br = load_slab(f"rw3_{e0 + 1}", 0, c)
                for mol in range(4):
                    mo = c * 4 + mol
                    pb = (mol % 2) * 2
                    for px, (w3s, w3r) in enumerate(((w3a, w3ar), (w3b, w3br))):
                        for kc in range(8):
                            for tb in range(2):
                                P.op("PE", (lambda h, w=w3s, kc=kc, mol=mol,
                                            px=px, tb=tb, pb=pb,
                                            st=(px == 0 and kc == 0),
                                            sp=(px == 1 and kc == 7): MM(
                                    h, PS[:, pb * 512 + tb * 512:
                                          pb * 512 + tb * 512 + 512],
                                    w[:, kc * 512 + mol * 128:
                                      kc * 512 + mol * 128 + 128],
                                    HB[:, px * 8192 + kc * 1024 + tb * 512:
                                       px * 8192 + kc * 1024 + tb * 512 + 512],
                                    st, sp)),
                                    reads=(w3r, R("HB")), writes=(RPS[pb + tb],))
                    fsl = FA[:, mo * 1024:mo * 1024 + 1024]
                    psl = PS[:, pb * 512:pb * 512 + 1024]
                    P.op("DVE", (lambda h, o=fsl, i=psl: h.tensor_tensor(
                        o, i, o, OP.add)),
                        reads=(RPS[pb], RPS[pb + 1], R("FA")), writes=(R("FA"),))

    def ph_qkv(l, hf):
        for (wkey, dst, dstn, b1c, esn) in (
                (f"Wq_{l}", QF, "QF", 0, "ES0"), (f"Wk_{l}", KF, "KF", 8, "ES1")):
            es = ES0 if esn == "ES0" else ES1
            bq1c = l * 16 + b1c
            bpc = bcol(l, "bq" if b1c == 0 else "bk")
            for sbk in range(2):
                ws, wr = load_slab(wkey, 0, sbk)
                if wkey.startswith("Wq") and sbk == 0:
                    tap("wslab", ws[:], (wr,))
                for mol in range(4):
                    mo = sbk * 4 + mol
                    pb = mol % 2
                    for kc in range(8):
                        P.op("PE", (lambda h, w=ws, kc=kc, mol=mol, hf=hf,
                                    pb=pb: MM(
                            h, PS[:, pb * 512:pb * 512 + 512],
                            w[:, kc * 512 + mol * 128:kc * 512 + mol * 128 + 128],
                            XF[:, kc * 1024 + hf * 512:kc * 1024 + hf * 512 + 512],
                            kc == 0, kc == 7)),
                            reads=(wr, R("XFh%d" % hf)), writes=(RPS[pb],))
                    if (wkey.startswith("Wq") and sbk == 0 and mol == 0
                            and "qpsum" in TAPS and "qpsum" not in tap_outs):
                        P.op("DVE", (lambda h: h.tensor_copy(
                            REC[:, 0:512], PS[:, 0:512])),
                            reads=(RPS[0],), writes=(R("REC"),))
                        tap("qpsum", REC[:, 0:512], (R("REC"),))
                    eh = es[:, 0:512] if mo % 2 == 0 else es[:, 512:1024]
                    ehr = RES[(esn, mo % 2)]
                    P.op("ACT", (lambda h, o=eh, pb=pb,
                                 b=BIA[:, bpc + mo:bpc + mo + 1]: h.activation(
                        o, PS[:, pb * 512:pb * 512 + 512], AF.Exp, bias=b)),
                        reads=(RPS[pb], R("BIA")), writes=(ehr,))
                    P.op("DVE", (lambda h, o=eh: h.tensor_scalar(
                        o, o, 1.0, 0.0, OP.min, OP.add)),
                        reads=(ehr,), writes=(ehr,))
                    P.op("DVE", (lambda h, o=dst[:, mo * 512:mo * 512 + 512],
                                 pb=pb, b=BQ1[:, bq1c + mo:bq1c + mo + 1],
                                 i=eh: h.scalar_tensor_tensor(
                        o, PS[:, pb * 512:pb * 512 + 512], b, i, OP.add, OP.max)),
                        reads=(RPS[pb], R("BQ1"), ehr), writes=(R(dstn),))
        PSb7 = PSbf[:, 7168:8192]
        for tcl in range(4):
            for dc in range(8):
                P.op("PE", (lambda h, dc=dc, tcl=tcl: h.transpose(
                    PSb7[:, dc * 128:dc * 128 + 128],
                    KF[:, dc * 512 + tcl * 128:dc * 512 + tcl * 128 + 128], IDB)),
                    reads=(R("KF"), R("CB")), writes=(RPS[7],))
            P.op("DVE", (lambda h, tcl=tcl: h.tensor_copy(
                KT[:, tcl * 1024:tcl * 1024 + 1024], PSb7)),
                reads=(RPS[7],), writes=(R("KT"),))
        for sbk in range(2):
            row_dma(VROW[0:1, :],
                    bass.AP(bv_d, l * D + sbk * 512, [[512, 1], [1, 512]]),
                    R("VROW"), tag="vrow")
            ws, wr = load_slab(f"Wv_{l}", 0, sbk)
            for tcl in range(4):
                pb = tcl % 2
                for kc in range(8):
                    P.op("PE", (lambda h, w=ws, kc=kc, tcl=tcl, hf=hf, pb=pb: MM(
                        h, PS[:, pb * 512:pb * 512 + 512],
                        XF[:, kc * 1024 + hf * 512 + tcl * 128:
                           kc * 1024 + hf * 512 + tcl * 128 + 128],
                        w[:, kc * 512:kc * 512 + 512], kc == 0, False)),
                        reads=(wr, R("XFh%d" % hf)), writes=(RPS[pb],))
                P.op("PE", (lambda h, pb=pb: MM(
                    h, PS[:, pb * 512:pb * 512 + 512], ONR[:],
                    VROW[0:1, 0:512], False, True)),
                    reads=(R("ONR"), R("VROW")), writes=(RPS[pb],))
                P.op("ACT", (lambda h, tcl=tcl, sbk=sbk, pb=pb: h.activation(
                    VT[:, tcl * 1024 + sbk * 512:tcl * 1024 + sbk * 512 + 512],
                    PS[:, pb * 512:pb * 512 + 512], AF.Copy)),
                    reads=(RPS[pb],), writes=(R("VT"),))

    def ph_chunks(l, hf, dbg=False):
        kvr = R(f"KVS{l}")
        tail_box = [None]

        def run_tail():
            if tail_box[0] is not None:
                tail_box[0]()
                tail_box[0] = None

        P.op("DVE", (lambda h: h.tensor_copy(KVB[:], KVS[l][:])),
             reads=(kvr,), writes=(R("KVB"),))
        for cl in (3, 2, 1, 0):
            def mm1(par):
                for s in range(8):
                    kf = KF[64 * par:64 * par + 64,
                            s * 512 + cl * 128:s * 512 + cl * 128 + 128]
                    qf = QF[64 * par:64 * par + 64,
                            s * 512 + cl * 128:s * 512 + cl * 128 + 128]
                    P.op("PE", (lambda h, kf=kf, qf=qf, s=s: MM(
                        h, PS[:, s * 128:s * 128 + 128], kf, qf, True, True)),
                        reads=(R("KF"), R("QF")), writes=(RPS[s // 4],))

            def maskev(par):
                P.op("DVE", (lambda h, par=par: h.tensor_tensor(
                    MA[:, par * 1024:par * 1024 + 1024], PS[:, 0:1024], MSK,
                    OP.mult)), reads=(RPS[0], RPS[1], R("CB")), writes=(R("MA"),))

            def mm4s():
                for par in range(2):
                    p0 = 64 * par
                    for s in range(8):
                        hh = 2 * s + par
                        kt = KT[:, cl * 1024 + hh * 64:cl * 1024 + hh * 64 + 64]
                        vt = VT[:, cl * 1024 + hh * 64:cl * 1024 + hh * 64 + 64]
                        P.op("PE", (lambda h, kt=kt, vt=vt, s=s, p0=p0: MM(
                            h, PS[p0:p0 + 64, 3072 + s * 128:3072 + s * 128 + 64],
                            kt, vt, True, True)),
                            reads=(R("KT"), R("VT")), writes=(RPS[6 + s // 4],))
                        P.op("PE", (lambda h, kt=kt, s=s, p0=p0: MM(
                            h, PS[p0:p0 + 64,
                                  3072 + s * 128 + 64:3072 + s * 128 + 128],
                            kt, ONC[:], True, True)),
                            reads=(R("KT"), R("CB")), writes=(RPS[6 + s // 4],))

            mm1(0)
            maskev(0)
            mm4s()
            mm1(1)
            P.op("DVE", (lambda h: h.tensor_tensor(
                KVS[l][:], KVS[l][:], PS[:, 3072:4096], OP.add)),
                reads=(kvr, RPS[6], RPS[7]), writes=(kvr,))
            maskev(1)
            run_tail()
            for par in range(2):
                p0 = 64 * par
                for g in range(2):
                    P.op("PE", (lambda h, par=par, g=g, p0=p0: MM(
                        h, PS[p0:p0 + 64, 2048 + g * 512:2048 + g * 512 + 512],
                        ONC[:],
                        MA[:, par * 1024 + g * 512:par * 1024 + g * 512 + 512],
                        True, False)),
                        reads=(R("CB"), R("MA")), writes=(RPS[4 + g],))
                for s in range(8):
                    qf = QF[p0:p0 + 64, s * 512 + cl * 128:s * 512 + cl * 128 + 128]
                    P.op("PE", (lambda h, par=par, s=s, cl=cl, p0=p0: MM(
                        h, PS[p0:p0 + 64, 1024 + s * 128:1024 + s * 128 + 128],
                        VT[:, cl * 1024 + (2 * s + par) * 64:
                           cl * 1024 + (2 * s + par) * 64 + 64],
                        MA[:, par * 1024 + s * 128:par * 1024 + s * 128 + 128],
                        True, False)),
                        reads=(R("VT"), R("MA")), writes=(RPS[2 + s // 4],))
                    P.op("PE", (lambda h, s=s, qf=qf, p0=p0: MM(
                        h, PS[p0:p0 + 64, 1024 + s * 128:1024 + s * 128 + 128],
                        KVB[p0:p0 + 64, s * 128:s * 128 + 64], qf, False, True)),
                        reads=(R("KVB"), R("QF")), writes=(RPS[2 + s // 4],))
                    P.op("PE", (lambda h, s=s, qf=qf, p0=p0: MM(
                        h, PS[p0:p0 + 64, 2048 + s * 128:2048 + s * 128 + 128],
                        KVB[p0:p0 + 64, s * 128 + 64:s * 128 + 128], qf,
                        False, True)),
                        reads=(R("KVB"), R("QF")), writes=(RPS[4 + s // 4],))
            if dbg and cl == 3:
                tap("ma", MA[:], (R("MA"),))
                P.op("DVE", (lambda h: h.tensor_copy(ES1[:, 0:1024],
                                                     PS[:, 1024:2048])),
                     reads=(RPS[2], RPS[3]), writes=(RES[("ES1", 0)],
                                                     RES[("ES1", 1)]))
                tap("numo", ES1[:], (RES[("ES1", 0)], RES[("ES1", 1)]))
            def mk_tail(cl):
                def tail():
                    P.op("ACT", (lambda h: h.activation(
                        REC[:], PS[:, 2048:3072], AF.Ln)),
                        reads=(RPS[4], RPS[5]), writes=(R("REC"),))
                    P.op("ACT", (lambda h: h.activation(
                        REC[:], REC[:], AF.Exp, scale=-1.0)),
                        reads=(R("REC"),), writes=(R("REC"),))
                    AO3 = AO[:].rearrange("p (a b) -> p a b", b=512)
                    P.op("DVE", (lambda h, A=AO3, cl=cl: h.tensor_tensor(
                        A[:, :, cl * 128:cl * 128 + 128],
                        PS[:, 1024:2048].rearrange("p (a b) -> p a b", b=128),
                        REC[:].rearrange("p (a b) -> p a b", b=128), OP.mult)),
                        reads=(RPS[2], RPS[3], R("REC")), writes=(R("AO"),))
                    P.op("DVE", (lambda h: h.tensor_copy(KVB[:], KVS[l][:])),
                         reads=(kvr,), writes=(R("KVB"),))
                return tail
            tail_box[0] = mk_tail(cl)
        run_tail()

    def ph_oproj(l, hf):
        boc = bcol(l, "bo")
        for sbk in range(2):
            ws, wr = load_slab(f"Wo_{l}", 0, sbk)
            for mol in range(4):
                mo = sbk * 4 + mol
                pb = mol % 2
                for kc in range(8):
                    P.op("PE", (lambda h, w=ws, kc=kc, mol=mol, pb=pb: MM(
                        h, PS[:, pb * 512:pb * 512 + 512],
                        w[:, kc * 512 + mol * 128:kc * 512 + mol * 128 + 128],
                        AO[:, kc * 512:kc * 512 + 512], kc == 0, kc == 7)),
                        reads=(wr, R("AO")), writes=(RPS[pb],))
                xsl = XF[:, mo * 1024 + hf * 512:mo * 1024 + hf * 512 + 512]
                P.op("DVE", (lambda h, o=xsl, pb=pb,
                             b=BIA[:, boc + mo:boc + mo + 1]:
                             h.scalar_tensor_tensor(
                    o, PS[:, pb * 512:pb * 512 + 512], b, o, OP.add, OP.add)),
                    reads=(RPS[pb], R("BIA"), R("XFh%d" % hf)),
                    writes=(R("XFh%d" % hf),))

    def ph_gelu_ffn(l, hf):
        b1c = bcol(l, "b1")
        for sbk in range(8):
            ws, wr = load_slab(f"W1_{l}", 0, sbk)
            for hcl in range(4):
                hc = sbk * 4 + hcl
                pb = hc % 2
                for kc in range(8):
                    P.op("PE", (lambda h, w=ws, kc=kc, hcl=hcl, hf=hf, pb=pb: MM(
                        h, PS[:, pb * 512:pb * 512 + 512],
                        w[:, kc * 512 + hcl * 128:kc * 512 + hcl * 128 + 128],
                        XF[:, kc * 1024 + hf * 512:kc * 1024 + hf * 512 + 512],
                        kc == 0, kc == 7)),
                        reads=(wr, R("XFh%d" % hf)), writes=(RPS[pb],))
                P.op("ACT", (lambda h, hc=hc, pb=pb,
                             b=BIA[:, b1c + hc:b1c + hc + 1]: h.activation(
                    HB[:, hc * 512:hc * 512 + 512],
                    PS[:, pb * 512:pb * 512 + 512], AF.Gelu, bias=b)),
                    reads=(RPS[pb], R("BIA")), writes=(R("HB"),))
        b2c = bcol(l, "b2")
        for c in range(2):
            for r4 in range(4):
                ws, wr = load_slab(f"W2_{l}", r4, c)
                for kc in range(8):
                    hidc = r4 * 8 + kc
                    for mol in range(4):
                        P.op("PE", (lambda h, w=ws, kc=kc, mol=mol, hidc=hidc,
                                    st=(r4 == 0 and kc == 0),
                                    sp=(r4 == 3 and kc == 7): MM(
                            h, PS[:, (4 + mol) * 512:(4 + mol) * 512 + 512],
                            w[:, kc * 512 + mol * 128:kc * 512 + mol * 128 + 128],
                            HB[:, hidc * 512:hidc * 512 + 512], st, sp)),
                            reads=(wr, R("HB")), writes=(RPS[4 + mol],))
            for mol in range(4):
                mo = c * 4 + mol
                xsl = XF[:, mo * 1024 + hf * 512:mo * 1024 + hf * 512 + 512]
                P.op("DVE", (lambda h, o=xsl, mol=mol,
                             b=BIA[:, b2c + mo:b2c + mo + 1]:
                             h.scalar_tensor_tensor(
                    o, PS[:, (4 + mol) * 512:(4 + mol) * 512 + 512], b, o,
                    OP.add, OP.add)),
                    reads=(RPS[4 + mol], R("BIA"), R("XFh%d" % hf)),
                    writes=(R("XFh%d" % hf),))

    def ph_assembly(blk):
        PSb7 = PSbf[:, 7168:8192]
        PSb6 = PSbf[:, 6144:7168]
        for tc in range(8):
            r0 = blk * TBL + tc * 128
            row_dma(X0S[:], bass.AP(x_d, r0 * D, [[D, 128], [1, D]]),
                    R("X0S"), tag="x2")
            for dc in range(8):
                P.op("PE", (lambda h, dc=dc, tc=tc: h.transpose(
                    PSb7[:, dc * 128:dc * 128 + 128],
                    XF[:, dc * 1024 + tc * 128:dc * 1024 + tc * 128 + 128], IDB)),
                    reads=(R("XFh0"), R("XFh1"), R("CB")), writes=(RPS[7],))
                P.op("PE", (lambda h, dc=dc, tc=tc: h.transpose(
                    PSb6[:, dc * 128:dc * 128 + 128],
                    FA[:, dc * 1024 + tc * 128:dc * 1024 + tc * 128 + 128], IDB)),
                    reads=(R("FA"), R("CB")), writes=(RPS[6],))
            ot = OT[0]
            otr = R("OT0")
            P.op("DVE", (lambda h, o=ot: h.tensor_tensor(
                o[:], PSb7, X0S[:], OP.add)),
                reads=(RPS[7], R("X0S")), writes=(otr,))
            P.op("DVE", (lambda h, o=ot: h.tensor_tensor(
                o[:], o[:], PSb6, OP.add)),
                reads=(otr, RPS[6]), writes=(otr,))
            P.dma("xio", (lambda h, i=ot, r0=r0: h.dma_start(
                out=bass.AP(out_d, r0 * D, [[D, 128], [1, D]]), in_=i[:])),
                reads=(otr,), writes=(R("outd"),), tag="st")

    ph_init()
    for blk in (3, 2, 1, 0):
        if blk == 3:
            pump(25)
        ph_xload(blk)
        for hf in (1, 0):
            ph_ln(None, hf)
        ph_router(blk)
        if DO_SHARED:
            pump(4)
            ph_shared(blk)
        else:
            for mo in range(8):
                P.op("DVE", (lambda h, mo=mo: h.memset(
                    FA[:, mo * 1024:mo * 1024 + 1024], 0.0)), writes=(R("FA"),))
        if DO_ROUTED:
            ph_routed(blk)
        pend = [None]

        def run_pend():
            if pend[0] is not None:
                pend[0]()
                pend[0] = None

        for l in range(N_LAYERS):
            if blk == 3:
                pump(12)
            ph_qkv(l, 1)
            run_pend()
            ph_chunks(l, 1)
            ph_oproj(l, 1)
            ph_ln_stats(1)
            ph_qkv(l, 0)
            ph_ln_apply((bcol(l, "g1"), bcol(l, "be1")), 1)
            ph_chunks(l, 0)
            ph_oproj(l, 0)
            ph_ln_stats(0)
            ph_gelu_ffn(l, 1)
            ph_ln_apply((bcol(l, "g1"), bcol(l, "be1")), 0)
            ph_ln_stats(1)
            ph_gelu_ffn(l, 0)
            ph_ln_apply((bcol(l, "g2"), bcol(l, "be2")), 1)
            ph_ln_stats(0)
            pend[0] = (lambda l=l: ph_ln_apply((bcol(l, "g2"), bcol(l, "be2")),
                                               0))
        run_pend()
        if DO_FINAL_LN:
            for hf in (1, 0):
                ph_ln((GFCOL, BFCOL), hf)
        ph_assembly(blk)
    pump(len(pc_units))

    n_instr = sum(len(v) for v in P.ops.values())
    n_waits = sum(len(o["waits"]) for v in P.ops.values() for o in v)
    sems = {}
    with stack:
        for e in ENGS:
            sems[("e", e)] = stack.enter_context(nc.semaphore("s_" + e))
        for s in P.streams:
            sems[("d", s)] = stack.enter_context(nc.semaphore("sd_" + s))
        block = stack.enter_context(nc.Block())

        def emit(ename, handle):
            for o in P.ops[ename]:
                for semkey, val in o["waits"]:
                    handle.wait_ge(sems[semkey], val)
                inst = o["fn"](handle)
                for semkey, amt in o["incs"]:
                    inst.then_inc(sems[semkey], amt)

        @block.tensor
        def _(h):
            emit("PE", h)

        @block.vector
        def _(h):
            emit("DVE", h)

        @block.scalar
        def _(h):
            emit("ACT", h)

        @block.gpsimd
        def _(h):
            emit("POOL", h)

        @block.sync
        def _(h):
            emit("SP", h)
            h.wait_ge(sems[("d", "xio")], P.streams["xio"]["count"])

    return nc, n_instr, n_waits


def _prep_core_inputs(inputs, core):
    d = {}
    d["x"] = np.ascontiguousarray(np.asarray(inputs["L0_attn_x"][core],
                                             dtype=np.float32))
    for k, src in (("Wq", "L0_attn_Wq"), ("Wk", "L0_attn_Wk"),
                   ("Wv", "L0_attn_Wv"), ("Wo", "L0_attn_Wo"),
                   ("W1", "L0_attn_W1"), ("W2", "L0_attn_W2"),
                   ("sw1", "L0_ffn_shared_w1"), ("sw2", "L0_ffn_shared_w2"),
                   ("sw3", "L0_ffn_shared_w3"), ("rw1", "L0_ffn_routed_w1"),
                   ("rw2", "L0_ffn_routed_w2"), ("rw3", "L0_ffn_routed_w3"),
                   ("router", "L0_ffn_router_w"), ("bv_rows", "L0_attn_bv")):
        d[k] = np.ascontiguousarray(np.asarray(inputs[src], dtype=np.float32))
    return d


def _prep_shared(inputs):
    f32 = np.float32
    bf16 = ml_dtypes.bfloat16

    def pp(v):
        return np.ascontiguousarray(np.asarray(v, dtype=f32).reshape(-1, 128).T)

    cols = []
    for l in range(NL):
        for key in ("L0_attn_bq", "L0_attn_bk", "L0_attn_bv", "L0_attn_bo",
                    "L0_attn_b2"):
            cols.append(pp(inputs[key][l]))
        cols.append(pp(inputs["L0_attn_b1"][l]))
        for key in ("L0_attn_g1", "L0_attn_be1", "L0_attn_g2", "L0_attn_be2"):
            cols.append(pp(inputs[key][l]))
    cols.append(pp(inputs["L0_attn_gf"]))
    cols.append(pp(inputs["L0_attn_bf"]))
    bias_pp = np.concatenate(cols, axis=1).astype(f32)
    assert bias_pp.shape == (128, 640), bias_pp.shape

    idn = np.eye(128, dtype=f32)
    mask = np.tril(np.ones((128, 128), dtype=f32))
    cb = np.concatenate([idn, np.tile(mask, (1, 8)),
                         np.ones((128, 64), dtype=f32)], axis=1).astype(bf16)
    cf = np.concatenate(
        [idn, np.tile(np.asarray(inputs["L0_ffn_expert_bias"],
                                 dtype=f32)[None, :], (128, 1))],
        axis=1).astype(f32)
    sel = np.zeros((16, 2048), dtype=f32)
    for e in range(16):
        sel[e, e * 128:(e + 1) * 128] = 1.0
    return {"bias_pp": bias_pp, "cst_bf": np.ascontiguousarray(cb),
            "cst_f32": np.ascontiguousarray(cf),
            "ones_row": np.ones((1, 128), dtype=f32),
            "sel": np.ascontiguousarray(sel.astype(bf16))}


def kernel(**inputs):
    shared = _prep_shared(inputs)
    in_maps = []
    for c in range(8):
        m = _prep_core_inputs(inputs, c)
        m.update(shared)
        in_maps.append(m)
    nc, n_instr, n_waits = _build()
    res = run_bass_kernel_spmd(nc, in_maps, core_ids=list(range(8)))
    out = np.stack([np.asarray(r["out"], dtype=np.float32)
                    for r in res.results], axis=0)
    if TAPS:
        kernel.taps = {k: np.asarray(v) for k, v in res.results[0].items()
                       if k.startswith("tap_")}
    return out


# version 04
# speedup vs baseline: 1.1902x; 1.0153x over previous
import bisect
import contextlib
import numpy as np
import ml_dtypes
import concourse.bass as bass
import concourse.mybir as mybir
from concourse.bass_utils import run_bass_kernel_spmd

F32 = mybir.dt.float32
BF16 = mybir.dt.bfloat16
AF = mybir.ActivationFunctionType
OP = mybir.AluOpType
AX = mybir.AxisListType

D = 1024
T = 4096
NB = 4
TBL = 1024
NL = 6
NH = 16
DH = 64
NE = 16
DFF = 4096
LN_EPS = 1e-5

TAPS = ()
N_LAYERS = NL
DO_SHARED = True
DO_ROUTED = True
DO_FINAL_LN = True

ENGS = ("PE", "DVE", "ACT", "POOL", "SP")


class Plan:
    def __init__(self):
        self.ops = {e: [] for e in ENGS}
        self.incs = {e: [] for e in ENGS}
        self.ninc = {e: 0 for e in ENGS}
        self.hwm = {e: {} for e in ENGS}
        self.streams = {}
        self.regions = {}

    def stream(self, name, eng):
        self.streams[name] = {"eng": eng, "count": 0}

    def region(self, name):
        r = self.regions.get(name)
        if r is None:
            r = {"name": name, "w": None, "rd": {}}
            self.regions[name] = r
        return r

    def _eng_inc(self, ename, seq):
        lst = self.incs[ename]
        i = bisect.bisect_left(lst, (seq, -1))
        if i < len(lst):
            return lst[i][1]
        self.ninc[ename] += 1
        v = self.ninc[ename]
        lst.append((seq, v))
        self.ops[ename][seq]["incs"].append((("e", ename), 1))
        return v

    def _resolve(self, cons_eng, prods):
        waits = {}
        for p in prods:
            if p is None:
                continue
            if p[0] == "e":
                semkey = ("e", p[1])
                val = self._eng_inc(p[1], p[2])
            else:
                semkey = ("d", p[1])
                val = p[2]
            if val > self.hwm[cons_eng].get(semkey, 0):
                if val > waits.get(semkey, 0):
                    waits[semkey] = val
        out = list(waits.items())
        for semkey, val in out:
            self.hwm[cons_eng][semkey] = val
        return out

    def op(self, eng, fn, reads=(), writes=(), tag=""):
        prods = []
        for r in reads:
            if r["w"] is not None:
                prods.append(r["w"])
        for wr in writes:
            w = wr["w"]
            if w is not None and not (w[0] == "e" and w[1] == eng):
                prods.append(w)
            for k, v in wr["rd"].items():
                if k[0] == "e":
                    if k[1] == eng:
                        continue
                    prods.append(("e", k[1], v))
                else:
                    prods.append(("d", k[1], v))
        waits = self._resolve(eng, prods)
        seq = len(self.ops[eng])
        self.ops[eng].append({"fn": fn, "waits": waits, "incs": [], "tag": tag})
        for r in reads:
            r["rd"][("e", eng)] = seq
        for wr in writes:
            wr["w"] = ("e", eng, seq)
            wr["rd"] = {}
        return seq

    def dma(self, sname, fn, reads=(), writes=(), tag=""):
        st = self.streams[sname]
        eng = st["eng"]
        prods = []
        for r in reads:
            if r["w"] is not None:
                prods.append(r["w"])
        for wr in writes:
            if wr["w"] is not None:
                prods.append(wr["w"])
            for k, v in wr["rd"].items():
                prods.append(("e", k[1], v) if k[0] == "e" else ("d", k[1], v))
        waits = self._resolve(eng, prods)
        st["count"] += 16
        cnt = st["count"]
        self.ops[eng].append(
            {"fn": fn, "waits": waits, "incs": [(("d", sname), 16)], "tag": tag})
        for r in reads:
            r["rd"][("d", sname)] = cnt
        for wr in writes:
            wr["w"] = ("d", sname, cnt)
            wr["rd"] = {}
        return cnt


def _build():
    nc = bass.Bass("TRN2", target_bir_lowering=False)
    P = Plan()
    P.stream("w", "SP")
    P.stream("pc", "POOL")
    P.stream("xio", "POOL")

    dt_in = {}

    def din(name, shape, dt=F32):
        dt_in[name] = nc.dram_tensor(name, list(shape), dt, kind="ExternalInput")
        return dt_in[name]

    x_d = din("x", (T, D))
    wq_d = din("Wq", (NL, D, D)); wk_d = din("Wk", (NL, D, D))
    wv_d = din("Wv", (NL, D, D)); wo_d = din("Wo", (NL, D, D))
    w1_d = din("W1", (NL, D, DFF)); w2_d = din("W2", (NL, DFF, D))
    sw1_d = din("sw1", (1, D, DFF)); sw2_d = din("sw2", (1, D, DFF))
    sw3_d = din("sw3", (1, DFF, D))
    rw1_d = din("rw1", (NE, D, D)); rw2_d = din("rw2", (NE, D, D))
    rw3_d = din("rw3", (NE, D, D))
    rout_d = din("router", (D, NE))
    bv_d = din("bv_rows", (NL, D))
    bia_d = din("bias_pp", (128, 640))
    cb_d = din("cst_bf", (128, 1216), BF16)
    cf_d = din("cst_f32", (128, 144))
    or_d = din("ones_row", (1, 128))
    sel_d = din("sel", (16, 2048), BF16)
    out_d = nc.dram_tensor("out", [T, D], F32, kind="ExternalOutput")

    scr = {}
    offs = [0, 0]

    def salloc(grp, key, din_, dout_):
        scr[key] = (grp, offs[grp], din_, dout_)
        offs[grp] += din_ * dout_

    salloc(0, "router", D, NE)
    salloc(0, "sw1", D, DFF); salloc(0, "sw2", D, DFF); salloc(0, "sw3", DFF, D)
    for e in range(NE):
        salloc(0, f"rw1_{e}", D, D); salloc(0, f"rw2_{e}", D, D)
        salloc(0, f"rw3_{e}", D, D)
    for l in range(NL):
        salloc(1, f"Wq_{l}", D, D); salloc(1, f"Wk_{l}", D, D)
        salloc(1, f"Wv_{l}", D, D); salloc(1, f"Wo_{l}", D, D)
        salloc(1, f"W1_{l}", D, DFF); salloc(1, f"W2_{l}", DFF, D)
    wscs = [nc.dram_tensor("wsc0", [offs[0]], BF16),
            nc.dram_tensor("wsc1", [offs[1]], BF16)]

    def src_of(key):
        if key == "router":
            return rout_d, 0
        if key in ("sw1", "sw2", "sw3"):
            return {"sw1": sw1_d, "sw2": sw2_d, "sw3": sw3_d}[key], 0
        kind, idx = key.split("_")
        idx = int(idx)
        m = {"rw1": rw1_d, "rw2": rw2_d, "rw3": rw3_d,
             "Wq": wq_d, "Wk": wk_d, "Wv": wv_d, "Wo": wo_d,
             "W1": w1_d, "W2": w2_d}[kind]
        sz = D * DFF if kind in ("W1", "W2") else D * D
        return m, idx * sz

    pc_units = []

    def pc_add(key):
        grp, base, din_, dout_ = scr[key]
        n = din_ * dout_
        u = 0
        while u < n:
            sz = min(1024 * 1024, n - u)
            pc_units.append((key, u, sz))
            u += sz

    pc_add("router"); pc_add("sw1"); pc_add("sw2"); pc_add("sw3")
    for e in range(NE):
        pc_add(f"rw1_{e}"); pc_add(f"rw2_{e}"); pc_add(f"rw3_{e}")
    for l in range(NL):
        for k in ("Wq", "Wk", "Wv", "Wo", "W1", "W2"):
            pc_add(f"{k}_{l}")
    pc_pos = [0]
    pc_last_unit = {}
    for i, (key, u, sz) in enumerate(pc_units):
        pc_last_unit[key] = i

    def pump_until(key):
        while pc_pos[0] <= pc_last_unit[key]:
            pump(1)

    def pump(n):
        while n > 0 and pc_pos[0] < len(pc_units):
            key, u, sz = pc_units[pc_pos[0]]
            pc_pos[0] += 1
            n -= 1
            grp, base, _, _ = scr[key]
            src, soff = src_of(key)
            in_ap = bass.AP(src, soff + u, [[1, sz]])
            out_ap = bass.AP(wscs[grp], base + u, [[1, sz]])
            P.dma("pc", (lambda h, o=out_ap, i=in_ap: h.dma_start(out=o, in_=i)),
                  writes=(P.region("wsc_" + key),), tag="pc")

    stack = contextlib.ExitStack()

    def sbt(name, shape, dt):
        return stack.enter_context(nc.sbuf_tensor(name, list(shape), dt))

    XF = sbt("XF", (128, 8192), BF16)
    FA = sbt("FA", (128, 8192), BF16)
    QF = sbt("QF", (128, 4096), BF16)
    KF = sbt("KF", (128, 4096), BF16)
    KT = sbt("KT", (128, 4096), BF16)
    VT = sbt("VT", (128, 4096), BF16)
    AO = sbt("AO", (128, 4096), BF16)
    HB = sbt("HB", (128, 16384), BF16)
    MA = sbt("MA", (128, 2048), BF16)
    REC = sbt("REC", (128, 1024), F32)
    ES0 = sbt("ES0", (128, 1024), BF16)
    ES1 = sbt("ES1", (128, 1024), BF16)
    WBUF = [sbt("WBA", (128, 4096), BF16), sbt("WBB", (128, 4096), BF16),
            sbt("WBC", (128, 4096), BF16)]
    KVS = [sbt(f"KVS{l}", (128, 1024), F32) for l in range(NL)]
    KVB = sbt("KVB", (128, 1024), BF16)
    X0S = sbt("X0S", (128, 1024), F32)
    X0B = sbt("X0B", (128, 1024), BF16)
    OT = [sbt("OT0", (128, 1024), F32)]
    ROWS = sbt("ROWS", (1, 2048), F32)
    GFB = sbt("GFB", (16, 1024), BF16)
    SCB = sbt("SCB", (128, 128), F32)
    G1 = sbt("G1", (128, 16), F32)
    G2 = sbt("G2", (128, 16), F32)
    G3 = sbt("G3", (128, 16), F32)
    GM = sbt("GM", (128, 1), F32)
    RWB = sbt("RWB", (128, 128), BF16)
    VROW = sbt("VROW", (1, 512), F32)
    GSB = sbt("GSB", (128, 2048), BF16)
    BQ1 = sbt("BQ1", (128, 96), F32)
    BIA = sbt("BIA", (128, 640), F32)
    CB = sbt("CB", (128, 1216), BF16)
    CF = sbt("CF", (128, 144), F32)
    ONR = sbt("ONR", (1, 128), F32)
    SEL = sbt("SEL", (16, 2048), BF16)
    PS = stack.enter_context(nc.psum_tensor("PS", [128, 4096], F32))
    PSbf = PS[:].bitcast(BF16)

    IDB = CB[:, 0:128]
    MSK = CB[:, 128:1152]
    ONC = CB[:, 1152:1216]
    IDF = CF[:, 0:128]
    EBC = CF[:, 128:144]

    R = P.region
    RPS = [R(f"psb{b}") for b in range(8)]
    RES = {("ES0", 0): R("ES0a"), ("ES0", 1): R("ES0b"),
           ("ES1", 0): R("ES1a"), ("ES1", 1): R("ES1b")}

    def bcol(l, name):
        base = {"bq": 0, "bk": 8, "bv": 16, "bo": 24, "b2": 32, "b1": 40,
                "g1": 72, "be1": 80, "g2": 88, "be2": 96}[name]
        return l * 104 + base

    GFCOL, BFCOL = 624, 632

    def MM(h, out, lhsT, rhs, start, stop):
        return h.matmul(out, lhsT, rhs, start=start, stop=stop,
                        skip_group_check=True)

    def load_slab(key, r, c, tag="w"):
        pump_until(key)
        grp, base, din_, dout_ = scr[key]
        buf = WBUF[load_slab.idx % 3]
        breg = R("WBUF%d" % (load_slab.idx % 3))
        load_slab.idx += 1
        o = base + r * 1024 * dout_ + c * 512
        in_ap = bass.AP(wscs[grp], o, [[dout_, 128], [128 * dout_, 8], [1, 512]])
        out_ap = buf[:].rearrange("p (k c) -> p k c", c=512)
        P.dma("w", (lambda h, oa=out_ap, ia=in_ap: h.dma_start(out=oa, in_=ia)),
              reads=(R("wsc_" + key),), writes=(breg,), tag=tag)
        return buf, breg
    load_slab.idx = 0

    def row_dma(dst_ap, src_ap, wreg, tag="row"):
        P.dma("xio", (lambda h, o=dst_ap, i=src_ap: h.dma_start(out=o, in_=i)),
              writes=(wreg,), tag=tag)

    tap_outs = {}

    def tap(name, buf_ap, rregs):
        if name not in TAPS or name in tap_outs:
            return
        shp = list(buf_ap.shape)
        td = nc.dram_tensor("tap_" + name, shp, buf_ap.dtype, kind="ExternalOutput")
        tap_outs[name] = td
        P.dma("xio", (lambda h, o=td.ap(), i=buf_ap: h.dma_start(out=o, in_=i)),
              reads=tuple(rregs), writes=(R("tapd_" + name),), tag="tap")

    def ph_init():
        row_dma(BIA[:], bia_d.ap(), R("BIA"))
        row_dma(CB[:], cb_d.ap(), R("CB"))
        row_dma(CF[:], cf_d.ap(), R("CF"))
        row_dma(ONR[:], or_d.ap(), R("ONR"))
        row_dma(SEL[:], sel_d.ap(), R("SEL"))
        for l in range(NL):
            for (o0, name) in ((0, "bq"), (8, "bk")):
                c = bcol(l, name)
                P.op("DVE", (lambda h, l=l, o0=o0, c=c: h.tensor_scalar(
                    BQ1[:, l * 16 + o0:l * 16 + o0 + 8], BIA[:, c:c + 8],
                    1.0, 0.0, OP.add, OP.add)),
                    reads=(R("BIA"),), writes=(R("BQ1"),))
            P.op("DVE", (lambda h, l=l: h.memset(KVS[l][:], 0.0)),
                 writes=(R(f"KVS{l}"),))

    def ph_xload(blk):
        XF3 = XF[:].rearrange("p (a b) -> p a b", b=1024)
        PSb7 = PSbf[:, 7168:8192]
        PSb7v = PSb7.rearrange("p (a b) -> p a b", b=128)
        for tc in range(8):
            r0 = blk * TBL + tc * 128
            row_dma(X0S[:], bass.AP(x_d, r0 * D, [[D, 128], [1, D]]),
                    R("X0S"), tag="xld")
            P.op("DVE", (lambda h: h.tensor_copy(X0B[:], X0S[:])),
                 reads=(R("X0S"),), writes=(R("X0B"),))
            for dc in range(8):
                P.op("PE", (lambda h, dc=dc: h.transpose(
                    PSb7[:, dc * 128:dc * 128 + 128],
                    X0B[:, dc * 128:dc * 128 + 128], IDB)),
                    reads=(R("X0B"), R("CB")), writes=(RPS[7],))
            P.op("DVE", (lambda h, tc=tc: h.tensor_copy(
                XF3[:, :, tc * 128:tc * 128 + 128], PSb7v)),
                reads=(RPS[7],), writes=(R("XFh0"), R("XFh1")))

    def ph_ln_stats(hf):
        for dc in range(8):
            xsl = XF[:, dc * 1024 + hf * 512:dc * 1024 + hf * 512 + 512]
            xq = ES1[:, 0:512] if dc % 2 == 0 else ES1[:, 512:1024]
            xqr = RES[("ES1", dc % 2)]
            P.op("ACT", (lambda h, o=xq, i=xsl: h.activation(o, i, AF.Square)),
                 reads=(R("XFh%d" % hf),), writes=(xqr,))
            P.op("PE", (lambda h, i=xsl, st=(dc == 0), sp=(dc == 7): MM(
                h, PS[0:1, 1024:1536], ONC[:, 0:1], i, st, sp)),
                reads=(R("XFh%d" % hf), R("CB")), writes=(RPS[2],))
            P.op("PE", (lambda h, i=xq, st=(dc == 0), sp=(dc == 7): MM(
                h, PS[0:1, 1536:2048], ONC[:, 0:1], i, st, sp)),
                reads=(xqr, R("CB")), writes=(RPS[3],))
        P.op("DVE", (lambda h: h.tensor_scalar(
            ROWS[0:1, 0:1024], PS[0:1, 1024:2048], 1.0 / D, 0.0, OP.mult, OP.add)),
            reads=(RPS[2], RPS[3]), writes=(R("ROWS"),))
        P.op("DVE", (lambda h: h.tensor_tensor(
            ROWS[0:1, 1024:1536], ROWS[0:1, 0:512], ROWS[0:1, 0:512], OP.mult)),
            reads=(R("ROWS"),), writes=(R("ROWS"),))
        P.op("DVE", (lambda h: h.scalar_tensor_tensor(
            ROWS[0:1, 512:1024], ROWS[0:1, 512:1024], LN_EPS,
            ROWS[0:1, 1024:1536], OP.add, OP.subtract)),
            reads=(R("ROWS"),), writes=(R("ROWS"),))
        P.op("ACT", (lambda h: h.activation(
            ROWS[0:1, 1024:1536], ROWS[0:1, 512:1024], AF.Ln)),
            reads=(R("ROWS"),), writes=(R("ROWS"),))
        P.op("ACT", (lambda h: h.activation(
            ROWS[0:1, 1536:2048], ROWS[0:1, 1024:1536], AF.Exp, scale=-0.5)),
            reads=(R("ROWS"),), writes=(R("ROWS"),))
        P.op("DVE", (lambda h: h.tensor_tensor(
            ROWS[0:1, 1024:1536], ROWS[0:1, 0:512], ROWS[0:1, 1536:2048],
            OP.mult)), reads=(R("ROWS"),), writes=(R("ROWS"),))

    def ph_ln_apply(gcol, hf):
        P.op("PE", (lambda h: MM(h, PS[:, 1024:1536], ONR[:],
                                 ROWS[0:1, 1536:2048], True, True)),
             reads=(R("ROWS"), R("ONR")), writes=(RPS[2],))
        P.op("PE", (lambda h: MM(h, PS[:, 1536:2048], ONR[:],
                                 ROWS[0:1, 1024:1536], True, True)),
             reads=(R("ROWS"), R("ONR")), writes=(RPS[3],))
        for dc in range(8):
            xsl = XF[:, dc * 1024 + hf * 512:dc * 1024 + hf * 512 + 512]
            es = ES0[:, 0:512] if dc % 2 == 0 else ES0[:, 512:1024]
            esr = RES[("ES0", dc % 2)]
            P.op("DVE", (lambda h, o=es, i=xsl: h.tensor_tensor(
                o, i, PS[:, 1024:1536], OP.mult)),
                reads=(R("XFh%d" % hf), RPS[2]), writes=(esr,))
            if gcol is None:
                P.op("DVE", (lambda h, o=xsl, i=es: h.tensor_tensor(
                    o, i, PS[:, 1536:2048], OP.subtract)),
                    reads=(esr, RPS[3]), writes=(R("XFh%d" % hf),))
            else:
                P.op("DVE", (lambda h, o=es, i=es: h.tensor_tensor(
                    o, i, PS[:, 1536:2048], OP.subtract)),
                    reads=(esr, RPS[3]), writes=(esr,))
                gc, bc = gcol
                P.op("DVE", (lambda h, o=xsl, i=es, g=BIA[:, gc + dc:gc + dc + 1],
                             b=BIA[:, bc + dc:bc + dc + 1]: h.tensor_scalar(
                    o, i, g, b, OP.mult, OP.add)),
                    reads=(esr, R("BIA")), writes=(R("XFh%d" % hf),))

    def ph_ln(gcol, hf):
        ph_ln_stats(hf)
        ph_ln_apply(gcol, hf)

    def ph_router(blk):
        pump_until("router")
        grp, base, _, _ = scr["router"]
        in_ap = bass.AP(wscs[grp], base, [[16, 128], [128 * 16, 8], [1, 16]])
        out_ap = RWB[:].rearrange("p (k c) -> p k c", c=16)
        P.dma("w", (lambda h, o=out_ap, i=in_ap: h.dma_start(out=o, in_=i)),
              reads=(R("wsc_router"),), writes=(R("RWB"),))
        for tc in range(8):
            pb = tc % 2
            for dc in range(8):
                P.op("PE", (lambda h, dc=dc, tc=tc, pb=pb: MM(
                    h, PS[:, pb * 512:pb * 512 + 16],
                    XF[:, dc * 1024 + tc * 128:dc * 1024 + tc * 128 + 128],
                    RWB[:, dc * 16:dc * 16 + 16], dc == 0, dc == 7)),
                    reads=(R("XFh0"), R("XFh1"), R("RWB")), writes=(RPS[pb],))
            P.op("ACT", (lambda h, tc=tc, pb=pb: h.activation(
                SCB[:, tc * 16:tc * 16 + 16], PS[:, pb * 512:pb * 512 + 16],
                AF.Sigmoid)), reads=(RPS[pb],), writes=(R("SCB"),))
            ssl = SCB[:, tc * 16:tc * 16 + 16]
            P.op("DVE", (lambda h, s=ssl: h.tensor_tensor(G1[:], s, EBC, OP.add)),
                 reads=(R("SCB"), R("CF")), writes=(R("G1"),))
            P.op("DVE", (lambda h: h.reduce_max(GM[:], G1[:], axis=AX.X)),
                 reads=(R("G1"),), writes=(R("GM"),))
            P.op("DVE", (lambda h: h.tensor_scalar(
                G2[:], G1[:], GM[:], 0.0, OP.is_equal, OP.add)),
                reads=(R("G1"), R("GM")), writes=(R("G2"),))
            P.op("DVE", (lambda h: h.scalar_tensor_tensor(
                G3[:], G2[:], -1e30, G1[:], OP.mult, OP.add)),
                reads=(R("G2"), R("G1")), writes=(R("G3"),))
            P.op("DVE", (lambda h: h.reduce_max(GM[:], G3[:], axis=AX.X)),
                 reads=(R("G3"),), writes=(R("GM"),))
            P.op("DVE", (lambda h: h.tensor_scalar(
                G3[:], G1[:], GM[:], 0.0, OP.is_equal, OP.add)),
                reads=(R("G1"), R("GM")), writes=(R("G3"),))
            P.op("DVE", (lambda h: h.tensor_tensor(G2[:], G2[:], G3[:], OP.add)),
                 reads=(R("G2"), R("G3")), writes=(R("G2"),))
            P.op("DVE", (lambda h, s=ssl: h.tensor_tensor(G1[:], s, G2[:],
                                                          OP.mult)),
                 reads=(R("SCB"), R("G2")), writes=(R("G1"),))
            P.op("PE", (lambda h: h.transpose(PS[0:16, 1024:1152], G1[:], IDF)),
                 reads=(R("G1"), R("CF")), writes=(RPS[2],))
            P.op("DVE", (lambda h, tc=tc: h.tensor_copy(
                GFB[0:16, tc * 128:tc * 128 + 128], PS[0:16, 1024:1152])),
                reads=(RPS[2],), writes=(R("GFB"),))

    def ph_shared(blk):
        for hf in (1, 0):
            for sbk in range(8):
                w1s, w1r = load_slab("sw1", 0, sbk)
                w2s, w2r = load_slab("sw2", 0, sbk)
                for hcl in range(4):
                    hc = sbk * 4 + hcl
                    for kc in range(8):
                        P.op("PE", (lambda h, w=w1s, kc=kc, hcl=hcl, hf=hf: MM(
                            h, PS[:, 1024:1536],
                            w[:, kc * 512 + hcl * 128:kc * 512 + hcl * 128 + 128],
                            XF[:, kc * 1024 + hf * 512:kc * 1024 + hf * 512 + 512],
                            kc == 0, kc == 7)),
                            reads=(w1r, R("XFh%d" % hf)), writes=(RPS[2],))
                        P.op("PE", (lambda h, w=w2s, kc=kc, hcl=hcl, hf=hf: MM(
                            h, PS[:, 1536:2048],
                            w[:, kc * 512 + hcl * 128:kc * 512 + hcl * 128 + 128],
                            XF[:, kc * 1024 + hf * 512:kc * 1024 + hf * 512 + 512],
                            kc == 0, kc == 7)),
                            reads=(w2r, R("XFh%d" % hf)), writes=(RPS[3],))
                    es = ES0[:, 0:512] if hc % 2 == 0 else ES0[:, 512:1024]
                    esr = RES[("ES0", hc % 2)]
                    P.op("ACT", (lambda h, o=es: h.activation(
                        o, PS[:, 1024:1536], AF.Silu)),
                        reads=(RPS[2],), writes=(esr,))
                    P.op("DVE", (lambda h, o=HB[:, hc * 512:hc * 512 + 512],
                                 i=es: h.tensor_tensor(
                        o, i, PS[:, 1536:2048], OP.mult)),
                        reads=(esr, RPS[3]), writes=(R("HB"),))
            for c in range(2):
                for r4 in range(4):
                    w3s, w3r = load_slab("sw3", r4, c)
                    for mol in range(4):
                        for kc in range(8):
                            hidc = r4 * 8 + kc
                            P.op("PE", (lambda h, w=w3s, kc=kc, mol=mol,
                                        hidc=hidc, st=(r4 == 0 and kc == 0),
                                        sp=(r4 == 3 and kc == 7): MM(
                                h, PS[:, (4 + mol) * 512:(4 + mol) * 512 + 512],
                                w[:, kc * 512 + mol * 128:kc * 512 + mol * 128 + 128],
                                HB[:, hidc * 512:hidc * 512 + 512], st, sp)),
                                reads=(w3r, R("HB")), writes=(RPS[4 + mol],))
                for mol in range(4):
                    mo = c * 4 + mol
                    fsl = FA[:, mo * 1024 + hf * 512:mo * 1024 + hf * 512 + 512]
                    psl = PS[:, (4 + mol) * 512:(4 + mol) * 512 + 512]
                    P.op("DVE", (lambda h, o=fsl, i=psl: h.tensor_copy(o, i)),
                         reads=(RPS[4 + mol],), writes=(R("FA"),))

    def ph_routed(blk):
        for p in range(8):
            pump(6)
            e0 = 2 * p
            for px in range(2):
                e = e0 + px
                for tb in range(2):
                    pb = px * 2 + tb
                    P.op("PE", (lambda h, e=e, tb=tb, pb=pb: MM(
                        h, PS[:, pb * 512:pb * 512 + 512],
                        SEL[:, e * 128:e * 128 + 128],
                        GFB[0:16, tb * 512:tb * 512 + 512], True, True)),
                        reads=(R("SEL"), R("GFB")), writes=(RPS[pb],))
                P.op("DVE", (lambda h, px=px: h.tensor_copy(
                    GSB[:, px * 1024:px * 1024 + 1024],
                    PS[:, px * 1024:px * 1024 + 1024])),
                    reads=(RPS[2 * px], RPS[2 * px + 1]), writes=(R("GSB"),))
            for px in range(2):
                e = e0 + px
                for c in range(2):
                    w1s, w1r = load_slab(f"rw1_{e}", 0, c)
                    w2s, w2r = load_slab(f"rw2_{e}", 0, c)
                    for rhl in range(4):
                        rh = c * 4 + rhl
                        for kc in range(8):
                            for tb in range(2):
                                P.op("PE", (lambda h, w=w1s, kc=kc, rhl=rhl,
                                            tb=tb: MM(
                                    h, PS[:, 1024 + tb * 512:1024 + tb * 512 + 512],
                                    w[:, kc * 512 + rhl * 128:
                                      kc * 512 + rhl * 128 + 128],
                                    XF[:, kc * 1024 + tb * 512:
                                       kc * 1024 + tb * 512 + 512],
                                    kc == 0, kc == 7)),
                                    reads=(w1r, R("XFh0"), R("XFh1")),
                                    writes=(RPS[2 + tb],))
                                P.op("PE", (lambda h, w=w2s, kc=kc, rhl=rhl,
                                            tb=tb: MM(
                                    h, PS[:, 2048 + tb * 512:2048 + tb * 512 + 512],
                                    w[:, kc * 512 + rhl * 128:
                                      kc * 512 + rhl * 128 + 128],
                                    XF[:, kc * 1024 + tb * 512:
                                       kc * 1024 + tb * 512 + 512],
                                    kc == 0, kc == 7)),
                                    reads=(w2r, R("XFh0"), R("XFh1")),
                                    writes=(RPS[4 + tb],))
                        es = ES0 if rh % 2 == 0 else ES1
                        esn = "ES0" if rh % 2 == 0 else "ES1"
                        era, erb = RES[(esn, 0)], RES[(esn, 1)]
                        P.op("ACT", (lambda h, o=es[:, 0:1024]: h.activation(
                            o, PS[:, 1024:2048], AF.Silu)),
                            reads=(RPS[2], RPS[3]), writes=(era, erb))
                        P.op("DVE", (lambda h, o=es[:, 0:1024]: h.tensor_tensor(
                            o, o, PS[:, 2048:3072], OP.mult)),
                            reads=(era, erb, RPS[4], RPS[5]), writes=(era, erb))
                        P.op("POOL", (lambda h, o=HB[:, px * 8192 + rh * 1024:
                                                     px * 8192 + rh * 1024 + 1024],
                                      i=es[:, 0:1024], px=px: h.tensor_tensor(
                            o, i, GSB[:, px * 1024:px * 1024 + 1024], OP.mult)),
                            reads=(era, erb, R("GSB")), writes=(R("HB"),))
            for c in range(2):
                w3a, w3ar = load_slab(f"rw3_{e0}", 0, c)
                w3b, w3br = load_slab(f"rw3_{e0 + 1}", 0, c)
                for mol in range(4):
                    mo = c * 4 + mol
                    pb = (mol % 2) * 2
                    for px, (w3s, w3r) in enumerate(((w3a, w3ar), (w3b, w3br))):
                        for kc in range(8):
                            for tb in range(2):
                                P.op("PE", (lambda h, w=w3s, kc=kc, mol=mol,
                                            px=px, tb=tb, pb=pb,
                                            st=(px == 0 and kc == 0),
                                            sp=(px == 1 and kc == 7): MM(
                                    h, PS[:, pb * 512 + tb * 512:
                                          pb * 512 + tb * 512 + 512],
                                    w[:, kc * 512 + mol * 128:
                                      kc * 512 + mol * 128 + 128],
                                    HB[:, px * 8192 + kc * 1024 + tb * 512:
                                       px * 8192 + kc * 1024 + tb * 512 + 512],
                                    st, sp)),
                                    reads=(w3r, R("HB")), writes=(RPS[pb + tb],))
                    fsl = FA[:, mo * 1024:mo * 1024 + 1024]
                    psl = PS[:, pb * 512:pb * 512 + 1024]
                    P.op("DVE", (lambda h, o=fsl, i=psl: h.tensor_tensor(
                        o, i, o, OP.add)),
                        reads=(RPS[pb], RPS[pb + 1], R("FA")), writes=(R("FA"),))

    def ph_qkv(l, hf):
        for (wkey, dst, dstn, b1c, esn) in (
                (f"Wq_{l}", QF, "QF", 0, "ES0"), (f"Wk_{l}", KF, "KF", 8, "ES1")):
            es = ES0 if esn == "ES0" else ES1
            bq1c = l * 16 + b1c
            bpc = bcol(l, "bq" if b1c == 0 else "bk")
            for sbk in range(2):
                ws, wr = load_slab(wkey, 0, sbk)
                if wkey.startswith("Wq") and sbk == 0:
                    tap("wslab", ws[:], (wr,))
                for mol in range(4):
                    mo = sbk * 4 + mol
                    pb = mol % 2
                    for kc in range(8):
                        P.op("PE", (lambda h, w=ws, kc=kc, mol=mol, hf=hf,
                                    pb=pb: MM(
                            h, PS[:, pb * 512:pb * 512 + 512],
                            w[:, kc * 512 + mol * 128:kc * 512 + mol * 128 + 128],
                            XF[:, kc * 1024 + hf * 512:kc * 1024 + hf * 512 + 512],
                            kc == 0, kc == 7)),
                            reads=(wr, R("XFh%d" % hf)), writes=(RPS[pb],))
                    if (wkey.startswith("Wq") and sbk == 0 and mol == 0
                            and "qpsum" in TAPS and "qpsum" not in tap_outs):
                        P.op("DVE", (lambda h: h.tensor_copy(
                            REC[:, 0:512], PS[:, 0:512])),
                            reads=(RPS[0],), writes=(R("REC"),))
                        tap("qpsum", REC[:, 0:512], (R("REC"),))
                    eh = es[:, 0:512] if mo % 2 == 0 else es[:, 512:1024]
                    ehr = RES[(esn, mo % 2)]
                    P.op("ACT", (lambda h, o=eh, pb=pb,
                                 b=BIA[:, bpc + mo:bpc + mo + 1]: h.activation(
                        o, PS[:, pb * 512:pb * 512 + 512], AF.Exp, bias=b)),
                        reads=(RPS[pb], R("BIA")), writes=(ehr,))
                    P.op("DVE", (lambda h, o=eh: h.tensor_scalar(
                        o, o, 1.0, 0.0, OP.min, OP.add)),
                        reads=(ehr,), writes=(ehr,))
                    P.op("DVE", (lambda h, o=dst[:, mo * 512:mo * 512 + 512],
                                 pb=pb, b=BQ1[:, bq1c + mo:bq1c + mo + 1],
                                 i=eh: h.scalar_tensor_tensor(
                        o, PS[:, pb * 512:pb * 512 + 512], b, i, OP.add, OP.max)),
                        reads=(RPS[pb], R("BQ1"), ehr), writes=(R(dstn),))
        PSb7 = PSbf[:, 7168:8192]
        for tcl in range(4):
            for dc in range(8):
                P.op("PE", (lambda h, dc=dc, tcl=tcl: h.transpose(
                    PSb7[:, dc * 128:dc * 128 + 128],
                    KF[:, dc * 512 + tcl * 128:dc * 512 + tcl * 128 + 128], IDB)),
                    reads=(R("KF"), R("CB")), writes=(RPS[7],))
            P.op("DVE", (lambda h, tcl=tcl: h.tensor_copy(
                KT[:, tcl * 1024:tcl * 1024 + 1024], PSb7)),
                reads=(RPS[7],), writes=(R("KT"),))
        for sbk in range(2):
            row_dma(VROW[0:1, :],
                    bass.AP(bv_d, l * D + sbk * 512, [[512, 1], [1, 512]]),
                    R("VROW"), tag="vrow")
            ws, wr = load_slab(f"Wv_{l}", 0, sbk)
            for tcl in range(4):
                pb = tcl % 2
                for kc in range(8):
                    P.op("PE", (lambda h, w=ws, kc=kc, tcl=tcl, hf=hf, pb=pb: MM(
                        h, PS[:, pb * 512:pb * 512 + 512],
                        XF[:, kc * 1024 + hf * 512 + tcl * 128:
                           kc * 1024 + hf * 512 + tcl * 128 + 128],
                        w[:, kc * 512:kc * 512 + 512], kc == 0, False)),
                        reads=(wr, R("XFh%d" % hf)), writes=(RPS[pb],))
                P.op("PE", (lambda h, pb=pb: MM(
                    h, PS[:, pb * 512:pb * 512 + 512], ONR[:],
                    VROW[0:1, 0:512], False, True)),
                    reads=(R("ONR"), R("VROW")), writes=(RPS[pb],))
                P.op("ACT", (lambda h, tcl=tcl, sbk=sbk, pb=pb: h.activation(
                    VT[:, tcl * 1024 + sbk * 512:tcl * 1024 + sbk * 512 + 512],
                    PS[:, pb * 512:pb * 512 + 512], AF.Copy)),
                    reads=(RPS[pb],), writes=(R("VT"),))

    def ph_chunks(l, hf, dbg=False):
        kvr = R(f"KVS{l}")
        tail_box = [None]

        def run_tail():
            if tail_box[0] is not None:
                tail_box[0]()
                tail_box[0] = None

        for cl in (3, 2, 1, 0):
            def mm1(par):
                for s in range(8):
                    kf = KF[64 * par:64 * par + 64,
                            s * 512 + cl * 128:s * 512 + cl * 128 + 128]
                    qf = QF[64 * par:64 * par + 64,
                            s * 512 + cl * 128:s * 512 + cl * 128 + 128]
                    P.op("PE", (lambda h, kf=kf, qf=qf, s=s: MM(
                        h, PS[:, s * 128:s * 128 + 128], kf, qf, True, True)),
                        reads=(R("KF"), R("QF")), writes=(RPS[s // 4],))

            def maskev(par):
                P.op("DVE", (lambda h, par=par: h.tensor_tensor(
                    MA[:, par * 1024:par * 1024 + 1024], PS[:, 0:1024], MSK,
                    OP.mult)), reads=(RPS[0], RPS[1], R("CB")), writes=(R("MA"),))

            def mm4s():
                for par in range(2):
                    p0 = 64 * par
                    for s in range(8):
                        hh = 2 * s + par
                        kt = KT[:, cl * 1024 + hh * 64:cl * 1024 + hh * 64 + 64]
                        vt = VT[:, cl * 1024 + hh * 64:cl * 1024 + hh * 64 + 64]
                        P.op("PE", (lambda h, kt=kt, vt=vt, s=s, p0=p0: MM(
                            h, PS[p0:p0 + 64, 3072 + s * 128:3072 + s * 128 + 64],
                            kt, vt, True, True)),
                            reads=(R("KT"), R("VT")), writes=(RPS[6 + s // 4],))
                        P.op("PE", (lambda h, kt=kt, s=s, p0=p0: MM(
                            h, PS[p0:p0 + 64,
                                  3072 + s * 128 + 64:3072 + s * 128 + 128],
                            kt, ONC[:], True, True)),
                            reads=(R("KT"), R("CB")), writes=(RPS[6 + s // 4],))

            mm1(0)
            maskev(0)
            P.op("DVE", (lambda h: h.tensor_copy(KVB[:], KVS[l][:])),
                 reads=(kvr,), writes=(R("KVB"),))
            mm4s()
            mm1(1)
            P.op("DVE", (lambda h: h.tensor_tensor(
                KVS[l][:], KVS[l][:], PS[:, 3072:4096], OP.add)),
                reads=(kvr, RPS[6], RPS[7]), writes=(kvr,))
            maskev(1)
            run_tail()
            for par in range(2):
                p0 = 64 * par
                for g in range(2):
                    P.op("PE", (lambda h, par=par, g=g, p0=p0: MM(
                        h, PS[p0:p0 + 64, 2048 + g * 512:2048 + g * 512 + 512],
                        ONC[:],
                        MA[:, par * 1024 + g * 512:par * 1024 + g * 512 + 512],
                        True, False)),
                        reads=(R("CB"), R("MA")), writes=(RPS[4 + g],))
                for s in range(8):
                    qf = QF[p0:p0 + 64, s * 512 + cl * 128:s * 512 + cl * 128 + 128]
                    P.op("PE", (lambda h, par=par, s=s, cl=cl, p0=p0: MM(
                        h, PS[p0:p0 + 64, 1024 + s * 128:1024 + s * 128 + 128],
                        VT[:, cl * 1024 + (2 * s + par) * 64:
                           cl * 1024 + (2 * s + par) * 64 + 64],
                        MA[:, par * 1024 + s * 128:par * 1024 + s * 128 + 128],
                        True, False)),
                        reads=(R("VT"), R("MA")), writes=(RPS[2 + s // 4],))
                    P.op("PE", (lambda h, s=s, qf=qf, p0=p0: MM(
                        h, PS[p0:p0 + 64, 1024 + s * 128:1024 + s * 128 + 128],
                        KVB[p0:p0 + 64, s * 128:s * 128 + 64], qf, False, True)),
                        reads=(R("KVB"), R("QF")), writes=(RPS[2 + s // 4],))
                    P.op("PE", (lambda h, s=s, qf=qf, p0=p0: MM(
                        h, PS[p0:p0 + 64, 2048 + s * 128:2048 + s * 128 + 128],
                        KVB[p0:p0 + 64, s * 128 + 64:s * 128 + 128], qf,
                        False, True)),
                        reads=(R("KVB"), R("QF")), writes=(RPS[4 + s // 4],))
            if dbg and cl == 3:
                tap("ma", MA[:], (R("MA"),))
                P.op("DVE", (lambda h: h.tensor_copy(ES1[:, 0:1024],
                                                     PS[:, 1024:2048])),
                     reads=(RPS[2], RPS[3]), writes=(RES[("ES1", 0)],
                                                     RES[("ES1", 1)]))
                tap("numo", ES1[:], (RES[("ES1", 0)], RES[("ES1", 1)]))
            def mk_tail(cl):
                def tail():
                    P.op("ACT", (lambda h: h.activation(
                        REC[:], PS[:, 2048:3072], AF.Ln)),
                        reads=(RPS[4], RPS[5]), writes=(R("REC"),))
                    P.op("ACT", (lambda h: h.activation(
                        REC[:], REC[:], AF.Exp, scale=-1.0)),
                        reads=(R("REC"),), writes=(R("REC"),))
                    AO3 = AO[:].rearrange("p (a b) -> p a b", b=512)
                    P.op("DVE", (lambda h, A=AO3, cl=cl: h.tensor_tensor(
                        A[:, :, cl * 128:cl * 128 + 128],
                        PS[:, 1024:2048].rearrange("p (a b) -> p a b", b=128),
                        REC[:].rearrange("p (a b) -> p a b", b=128), OP.mult)),
                        reads=(RPS[2], RPS[3], R("REC")), writes=(R("AO"),))
                return tail
            tail_box[0] = mk_tail(cl)
        run_tail()

    def ph_oproj(l, hf):
        boc = bcol(l, "bo")
        for sbk in range(2):
            ws, wr = load_slab(f"Wo_{l}", 0, sbk)
            for mol in range(4):
                mo = sbk * 4 + mol
                pb = mol % 2
                for kc in range(8):
                    P.op("PE", (lambda h, w=ws, kc=kc, mol=mol, pb=pb: MM(
                        h, PS[:, pb * 512:pb * 512 + 512],
                        w[:, kc * 512 + mol * 128:kc * 512 + mol * 128 + 128],
                        AO[:, kc * 512:kc * 512 + 512], kc == 0, kc == 7)),
                        reads=(wr, R("AO")), writes=(RPS[pb],))
                xsl = XF[:, mo * 1024 + hf * 512:mo * 1024 + hf * 512 + 512]
                P.op("DVE", (lambda h, o=xsl, pb=pb,
                             b=BIA[:, boc + mo:boc + mo + 1]:
                             h.scalar_tensor_tensor(
                    o, PS[:, pb * 512:pb * 512 + 512], b, o, OP.add, OP.add)),
                    reads=(RPS[pb], R("BIA"), R("XFh%d" % hf)),
                    writes=(R("XFh%d" % hf),))

    def ph_gelu_ffn(l, hf):
        b1c = bcol(l, "b1")
        for sbk in range(8):
            ws, wr = load_slab(f"W1_{l}", 0, sbk)
            for hcl in range(4):
                hc = sbk * 4 + hcl
                pb = hc % 2
                for kc in range(8):
                    P.op("PE", (lambda h, w=ws, kc=kc, hcl=hcl, hf=hf, pb=pb: MM(
                        h, PS[:, pb * 512:pb * 512 + 512],
                        w[:, kc * 512 + hcl * 128:kc * 512 + hcl * 128 + 128],
                        XF[:, kc * 1024 + hf * 512:kc * 1024 + hf * 512 + 512],
                        kc == 0, kc == 7)),
                        reads=(wr, R("XFh%d" % hf)), writes=(RPS[pb],))
                P.op("ACT", (lambda h, hc=hc, pb=pb,
                             b=BIA[:, b1c + hc:b1c + hc + 1]: h.activation(
                    HB[:, hc * 512:hc * 512 + 512],
                    PS[:, pb * 512:pb * 512 + 512], AF.Gelu, bias=b)),
                    reads=(RPS[pb], R("BIA")), writes=(R("HB"),))
        b2c = bcol(l, "b2")
        for c in range(2):
            for r4 in range(4):
                ws, wr = load_slab(f"W2_{l}", r4, c)
                for kc in range(8):
                    hidc = r4 * 8 + kc
                    for mol in range(4):
                        P.op("PE", (lambda h, w=ws, kc=kc, mol=mol, hidc=hidc,
                                    st=(r4 == 0 and kc == 0),
                                    sp=(r4 == 3 and kc == 7): MM(
                            h, PS[:, (4 + mol) * 512:(4 + mol) * 512 + 512],
                            w[:, kc * 512 + mol * 128:kc * 512 + mol * 128 + 128],
                            HB[:, hidc * 512:hidc * 512 + 512], st, sp)),
                            reads=(wr, R("HB")), writes=(RPS[4 + mol],))
            for mol in range(4):
                mo = c * 4 + mol
                xsl = XF[:, mo * 1024 + hf * 512:mo * 1024 + hf * 512 + 512]
                P.op("DVE", (lambda h, o=xsl, mol=mol,
                             b=BIA[:, b2c + mo:b2c + mo + 1]:
                             h.scalar_tensor_tensor(
                    o, PS[:, (4 + mol) * 512:(4 + mol) * 512 + 512], b, o,
                    OP.add, OP.add)),
                    reads=(RPS[4 + mol], R("BIA"), R("XFh%d" % hf)),
                    writes=(R("XFh%d" % hf),))

    def ph_assembly(blk):
        PSb7 = PSbf[:, 7168:8192]
        PSb6 = PSbf[:, 6144:7168]
        for tc in range(8):
            r0 = blk * TBL + tc * 128
            row_dma(X0S[:], bass.AP(x_d, r0 * D, [[D, 128], [1, D]]),
                    R("X0S"), tag="x2")
            for dc in range(8):
                P.op("PE", (lambda h, dc=dc, tc=tc: h.transpose(
                    PSb7[:, dc * 128:dc * 128 + 128],
                    XF[:, dc * 1024 + tc * 128:dc * 1024 + tc * 128 + 128], IDB)),
                    reads=(R("XFh0"), R("XFh1"), R("CB")), writes=(RPS[7],))
                P.op("PE", (lambda h, dc=dc, tc=tc: h.transpose(
                    PSb6[:, dc * 128:dc * 128 + 128],
                    FA[:, dc * 1024 + tc * 128:dc * 1024 + tc * 128 + 128], IDB)),
                    reads=(R("FA"), R("CB")), writes=(RPS[6],))
            ot = OT[0]
            otr = R("OT0")
            P.op("DVE", (lambda h, o=ot: h.tensor_tensor(
                o[:], PSb7, X0S[:], OP.add)),
                reads=(RPS[7], R("X0S")), writes=(otr,))
            P.op("DVE", (lambda h, o=ot: h.tensor_tensor(
                o[:], o[:], PSb6, OP.add)),
                reads=(otr, RPS[6]), writes=(otr,))
            P.dma("xio", (lambda h, i=ot, r0=r0: h.dma_start(
                out=bass.AP(out_d, r0 * D, [[D, 128], [1, D]]), in_=i[:])),
                reads=(otr,), writes=(R("outd"),), tag="st")

    ph_init()
    for blk in (3, 2, 1, 0):
        if blk == 3:
            pump(25)
        ph_xload(blk)
        for hf in (1, 0):
            ph_ln(None, hf)
        ph_router(blk)
        if DO_SHARED:
            pump(4)
            ph_shared(blk)
        else:
            for mo in range(8):
                P.op("DVE", (lambda h, mo=mo: h.memset(
                    FA[:, mo * 1024:mo * 1024 + 1024], 0.0)), writes=(R("FA"),))
        if DO_ROUTED:
            ph_routed(blk)
        pend = [None]

        def run_pend():
            if pend[0] is not None:
                pend[0]()
                pend[0] = None

        for l in range(N_LAYERS):
            if blk == 3:
                pump(12)
            ph_qkv(l, 1)
            run_pend()
            ph_chunks(l, 1)
            ph_oproj(l, 1)
            ph_ln_stats(1)
            ph_qkv(l, 0)
            ph_ln_apply((bcol(l, "g1"), bcol(l, "be1")), 1)
            ph_chunks(l, 0)
            ph_oproj(l, 0)
            ph_ln_stats(0)
            ph_gelu_ffn(l, 1)
            ph_ln_apply((bcol(l, "g1"), bcol(l, "be1")), 0)
            ph_ln_stats(1)
            ph_gelu_ffn(l, 0)
            ph_ln_apply((bcol(l, "g2"), bcol(l, "be2")), 1)
            ph_ln_stats(0)
            pend[0] = (lambda l=l: ph_ln_apply((bcol(l, "g2"), bcol(l, "be2")),
                                               0))
        run_pend()
        if DO_FINAL_LN:
            for hf in (1, 0):
                ph_ln((GFCOL, BFCOL), hf)
        ph_assembly(blk)
    pump(len(pc_units))

    n_instr = sum(len(v) for v in P.ops.values())
    n_waits = sum(len(o["waits"]) for v in P.ops.values() for o in v)
    sems = {}
    with stack:
        for e in ENGS:
            sems[("e", e)] = stack.enter_context(nc.semaphore("s_" + e))
        for s in P.streams:
            sems[("d", s)] = stack.enter_context(nc.semaphore("sd_" + s))
        block = stack.enter_context(nc.Block())

        def emit(ename, handle):
            for o in P.ops[ename]:
                for semkey, val in o["waits"]:
                    handle.wait_ge(sems[semkey], val)
                inst = o["fn"](handle)
                for semkey, amt in o["incs"]:
                    inst.then_inc(sems[semkey], amt)

        @block.tensor
        def _(h):
            emit("PE", h)

        @block.vector
        def _(h):
            emit("DVE", h)

        @block.scalar
        def _(h):
            emit("ACT", h)

        @block.gpsimd
        def _(h):
            emit("POOL", h)

        @block.sync
        def _(h):
            emit("SP", h)
            h.wait_ge(sems[("d", "xio")], P.streams["xio"]["count"])

    return nc, n_instr, n_waits


def _prep_core_inputs(inputs, core):
    d = {}
    d["x"] = np.ascontiguousarray(np.asarray(inputs["L0_attn_x"][core],
                                             dtype=np.float32))
    for k, src in (("Wq", "L0_attn_Wq"), ("Wk", "L0_attn_Wk"),
                   ("Wv", "L0_attn_Wv"), ("Wo", "L0_attn_Wo"),
                   ("W1", "L0_attn_W1"), ("W2", "L0_attn_W2"),
                   ("sw1", "L0_ffn_shared_w1"), ("sw2", "L0_ffn_shared_w2"),
                   ("sw3", "L0_ffn_shared_w3"), ("rw1", "L0_ffn_routed_w1"),
                   ("rw2", "L0_ffn_routed_w2"), ("rw3", "L0_ffn_routed_w3"),
                   ("router", "L0_ffn_router_w"), ("bv_rows", "L0_attn_bv")):
        d[k] = np.ascontiguousarray(np.asarray(inputs[src], dtype=np.float32))
    return d


def _prep_shared(inputs):
    f32 = np.float32
    bf16 = ml_dtypes.bfloat16

    def pp(v):
        return np.ascontiguousarray(np.asarray(v, dtype=f32).reshape(-1, 128).T)

    cols = []
    for l in range(NL):
        for key in ("L0_attn_bq", "L0_attn_bk", "L0_attn_bv", "L0_attn_bo",
                    "L0_attn_b2"):
            cols.append(pp(inputs[key][l]))
        cols.append(pp(inputs["L0_attn_b1"][l]))
        for key in ("L0_attn_g1", "L0_attn_be1", "L0_attn_g2", "L0_attn_be2"):
            cols.append(pp(inputs[key][l]))
    cols.append(pp(inputs["L0_attn_gf"]))
    cols.append(pp(inputs["L0_attn_bf"]))
    bias_pp = np.concatenate(cols, axis=1).astype(f32)
    assert bias_pp.shape == (128, 640), bias_pp.shape

    idn = np.eye(128, dtype=f32)
    mask = np.tril(np.ones((128, 128), dtype=f32))
    cb = np.concatenate([idn, np.tile(mask, (1, 8)),
                         np.ones((128, 64), dtype=f32)], axis=1).astype(bf16)
    cf = np.concatenate(
        [idn, np.tile(np.asarray(inputs["L0_ffn_expert_bias"],
                                 dtype=f32)[None, :], (128, 1))],
        axis=1).astype(f32)
    sel = np.zeros((16, 2048), dtype=f32)
    for e in range(16):
        sel[e, e * 128:(e + 1) * 128] = 1.0
    return {"bias_pp": bias_pp, "cst_bf": np.ascontiguousarray(cb),
            "cst_f32": np.ascontiguousarray(cf),
            "ones_row": np.ones((1, 128), dtype=f32),
            "sel": np.ascontiguousarray(sel.astype(bf16))}


def kernel(**inputs):
    shared = _prep_shared(inputs)
    in_maps = []
    for c in range(8):
        m = _prep_core_inputs(inputs, c)
        m.update(shared)
        in_maps.append(m)
    nc, n_instr, n_waits = _build()
    res = run_bass_kernel_spmd(nc, in_maps, core_ids=list(range(8)))
    out = np.stack([np.asarray(r["out"], dtype=np.float32)
                    for r in res.results], axis=0)
    if TAPS:
        kernel.taps = {k: np.asarray(v) for k, v in res.results[0].items()
                       if k.startswith("tap_")}
    return out


# version 06
# speedup vs baseline: 1.4042x; 1.1798x over previous
import bisect
import contextlib
import numpy as np
import ml_dtypes
import concourse.bass as bass
import concourse.mybir as mybir
from concourse.bass_utils import run_bass_kernel_spmd

F32 = mybir.dt.float32
BF16 = mybir.dt.bfloat16
AF = mybir.ActivationFunctionType
OP = mybir.AluOpType
AX = mybir.AxisListType

D = 1024
T = 4096
NB = 4
TBL = 1024
NL = 6
NH = 16
DH = 64
NE = 16
DFF = 4096
LN_EPS = 1e-5

TAPS = ()
N_LAYERS = NL
DO_SHARED = True
DO_ROUTED = True
DO_FINAL_LN = True

ENGS = ("PE", "DVE", "ACT", "POOL", "SP")


class Plan:
    def __init__(self):
        self.ops = {e: [] for e in ENGS}
        self.incs = {e: [] for e in ENGS}
        self.ninc = {e: 0 for e in ENGS}
        self.hwm = {e: {} for e in ENGS}
        self.streams = {}
        self.regions = {}

    def stream(self, name, eng):
        self.streams[name] = {"eng": eng, "count": 0}

    def region(self, name):
        r = self.regions.get(name)
        if r is None:
            r = {"name": name, "w": None, "rd": {}}
            self.regions[name] = r
        return r

    def _eng_inc(self, ename, seq):
        lst = self.incs[ename]
        i = bisect.bisect_left(lst, (seq, -1))
        if i < len(lst):
            return lst[i][1]
        self.ninc[ename] += 1
        v = self.ninc[ename]
        lst.append((seq, v))
        self.ops[ename][seq]["incs"].append((("e", ename), 1))
        return v

    def _resolve(self, cons_eng, prods):
        waits = {}
        for p in prods:
            if p is None:
                continue
            if p[0] == "e":
                semkey = ("e", p[1])
                val = self._eng_inc(p[1], p[2])
            else:
                semkey = ("d", p[1])
                val = p[2]
            if val > self.hwm[cons_eng].get(semkey, 0):
                if val > waits.get(semkey, 0):
                    waits[semkey] = val
        out = list(waits.items())
        for semkey, val in out:
            self.hwm[cons_eng][semkey] = val
        return out

    def op(self, eng, fn, reads=(), writes=(), tag=""):
        prods = []
        for r in reads:
            if r["w"] is not None:
                prods.append(r["w"])
        for wr in writes:
            w = wr["w"]
            if w is not None and not (w[0] == "e" and w[1] == eng):
                prods.append(w)
            for k, v in wr["rd"].items():
                if k[0] == "e":
                    if k[1] == eng:
                        continue
                    prods.append(("e", k[1], v))
                else:
                    prods.append(("d", k[1], v))
        waits = self._resolve(eng, prods)
        seq = len(self.ops[eng])
        self.ops[eng].append({"fn": fn, "waits": waits, "incs": [], "tag": tag})
        for r in reads:
            r["rd"][("e", eng)] = seq
        for wr in writes:
            wr["w"] = ("e", eng, seq)
            wr["rd"] = {}
        return seq

    def dma(self, sname, fn, reads=(), writes=(), tag=""):
        st = self.streams[sname]
        eng = st["eng"]
        prods = []
        for r in reads:
            if r["w"] is not None:
                prods.append(r["w"])
        for wr in writes:
            if wr["w"] is not None:
                prods.append(wr["w"])
            for k, v in wr["rd"].items():
                prods.append(("e", k[1], v) if k[0] == "e" else ("d", k[1], v))
        waits = self._resolve(eng, prods)
        st["count"] += 16
        cnt = st["count"]
        self.ops[eng].append(
            {"fn": fn, "waits": waits, "incs": [(("d", sname), 16)], "tag": tag})
        for r in reads:
            r["rd"][("d", sname)] = cnt
        for wr in writes:
            wr["w"] = ("d", sname, cnt)
            wr["rd"] = {}
        return cnt


def _build():
    nc = bass.Bass("TRN2", target_bir_lowering=False)
    P = Plan()
    P.stream("w", "SP")
    P.stream("pc", "POOL")
    P.stream("xio", "POOL")

    dt_in = {}

    def din(name, shape, dt=F32):
        dt_in[name] = nc.dram_tensor(name, list(shape), dt, kind="ExternalInput")
        return dt_in[name]

    x_d = din("x", (T, D))
    wq_d = din("Wq", (NL, D, D)); wk_d = din("Wk", (NL, D, D))
    wv_d = din("Wv", (NL, D, D)); wo_d = din("Wo", (NL, D, D))
    w1_d = din("W1", (NL, D, DFF)); w2_d = din("W2", (NL, DFF, D))
    sw1_d = din("sw1", (1, D, DFF)); sw2_d = din("sw2", (1, D, DFF))
    sw3_d = din("sw3", (1, DFF, D))
    rw1_d = din("rw1", (NE, D, D)); rw2_d = din("rw2", (NE, D, D))
    rw3_d = din("rw3", (NE, D, D))
    rout_d = din("router", (D, NE))
    bv_d = din("bv_rows", (NL, D))
    bia_d = din("bias_pp", (128, 640))
    cb_d = din("cst_bf", (128, 1216), BF16)
    cf_d = din("cst_f32", (128, 144))
    or_d = din("ones_row", (1, 128))
    sel_d = din("sel", (16, 2048), BF16)
    out_d = nc.dram_tensor("out", [T, D], F32, kind="ExternalOutput")

    scr = {}
    offs = [0, 0]

    def salloc(grp, key, din_, dout_):
        scr[key] = (grp, offs[grp], din_, dout_)
        offs[grp] += din_ * dout_

    salloc(0, "router", D, NE)
    salloc(0, "sw1", D, DFF); salloc(0, "sw2", D, DFF); salloc(0, "sw3", DFF, D)
    for e in range(NE):
        salloc(0, f"rw1_{e}", D, D); salloc(0, f"rw2_{e}", D, D)
        salloc(0, f"rw3_{e}", D, D)
    for l in range(NL):
        salloc(1, f"Wq_{l}", D, D); salloc(1, f"Wk_{l}", D, D)
        salloc(1, f"Wv_{l}", D, D); salloc(1, f"Wo_{l}", D, D)
        salloc(1, f"W1_{l}", D, DFF); salloc(1, f"W2_{l}", DFF, D)
    wscs = [nc.dram_tensor("wsc0", [offs[0]], BF16),
            nc.dram_tensor("wsc1", [offs[1]], BF16)]

    def src_of(key):
        if key == "router":
            return rout_d, 0
        if key in ("sw1", "sw2", "sw3"):
            return {"sw1": sw1_d, "sw2": sw2_d, "sw3": sw3_d}[key], 0
        kind, idx = key.split("_")
        idx = int(idx)
        m = {"rw1": rw1_d, "rw2": rw2_d, "rw3": rw3_d,
             "Wq": wq_d, "Wk": wk_d, "Wv": wv_d, "Wo": wo_d,
             "W1": w1_d, "W2": w2_d}[kind]
        sz = D * DFF if kind in ("W1", "W2") else D * D
        return m, idx * sz

    pc_units = []

    def pc_add(key):
        grp, base, din_, dout_ = scr[key]
        n = din_ * dout_
        u = 0
        while u < n:
            sz = min(1024 * 1024, n - u)
            pc_units.append((key, u, sz))
            u += sz

    pc_add("router"); pc_add("sw1"); pc_add("sw2"); pc_add("sw3")
    for e in range(NE):
        pc_add(f"rw1_{e}"); pc_add(f"rw2_{e}"); pc_add(f"rw3_{e}")
    for l in range(NL):
        for k in ("Wq", "Wk", "Wv", "Wo", "W1", "W2"):
            pc_add(f"{k}_{l}")
    pc_pos = [0]
    pc_last_unit = {}
    for i, (key, u, sz) in enumerate(pc_units):
        pc_last_unit[key] = i

    def pump_until(key):
        while pc_pos[0] <= pc_last_unit[key]:
            pump(1)

    def pump(n):
        while n > 0 and pc_pos[0] < len(pc_units):
            key, u, sz = pc_units[pc_pos[0]]
            pc_pos[0] += 1
            n -= 1
            grp, base, _, _ = scr[key]
            src, soff = src_of(key)
            in_ap = bass.AP(src, soff + u, [[1, sz]])
            out_ap = bass.AP(wscs[grp], base + u, [[1, sz]])
            P.dma("pc", (lambda h, o=out_ap, i=in_ap: h.dma_start(out=o, in_=i)),
                  writes=(P.region("wsc_" + key),), tag="pc")

    stack = contextlib.ExitStack()

    def sbt(name, shape, dt):
        return stack.enter_context(nc.sbuf_tensor(name, list(shape), dt))

    XF = sbt("XF", (128, 8192), BF16)
    FA = sbt("FA", (128, 8192), BF16)
    QF = sbt("QF", (128, 4096), BF16)
    KF = sbt("KF", (128, 4096), BF16)
    KT = sbt("KT", (128, 4096), BF16)
    VT = sbt("VT", (128, 4096), BF16)
    AO = sbt("AO", (128, 4096), BF16)
    HB = sbt("HB", (128, 16384), BF16)
    MA = sbt("MA", (128, 2048), BF16)
    REC = sbt("REC", (128, 1024), F32)
    ES0 = sbt("ES0", (128, 1024), BF16)
    ES1 = sbt("ES1", (128, 1024), BF16)
    WBUF = [sbt("WBA", (128, 4096), BF16), sbt("WBB", (128, 4096), BF16),
            sbt("WBC", (128, 4096), BF16)]
    KVS = [sbt(f"KVS{l}", (128, 1024), F32) for l in range(NL)]
    KVB = sbt("KVB", (128, 1024), BF16)
    X0S = sbt("X0S", (128, 1024), F32)
    X0B = sbt("X0B", (128, 1024), BF16)
    OT = [sbt("OT0", (128, 1024), F32)]
    ROWS = sbt("ROWS", (1, 2048), F32)
    GFB = sbt("GFB", (16, 1024), BF16)
    SCB = sbt("SCB", (128, 128), F32)
    G1 = sbt("G1", (128, 16), F32)
    G2 = sbt("G2", (128, 16), F32)
    G3 = sbt("G3", (128, 16), F32)
    GM = sbt("GM", (128, 1), F32)
    RWB = sbt("RWB", (128, 128), BF16)
    VROW = sbt("VROW", (1, 512), F32)
    GSB = sbt("GSB", (128, 2048), BF16)
    BQ1 = sbt("BQ1", (128, 96), F32)
    BIA = sbt("BIA", (128, 640), F32)
    CB = sbt("CB", (128, 1216), BF16)
    CF = sbt("CF", (128, 144), F32)
    ONR = sbt("ONR", (1, 128), F32)
    SEL = sbt("SEL", (16, 2048), BF16)
    PS = stack.enter_context(nc.psum_tensor("PS", [128, 4096], F32))
    PSbf = PS[:].bitcast(BF16)

    IDB = CB[:, 0:128]
    MSK = CB[:, 128:1152]
    ONC = CB[:, 1152:1216]
    IDF = CF[:, 0:128]
    EBC = CF[:, 128:144]

    R = P.region
    RPS = [R(f"psb{b}") for b in range(8)]
    RES = {("ES0", 0): R("ES0a"), ("ES0", 1): R("ES0b"),
           ("ES1", 0): R("ES1a"), ("ES1", 1): R("ES1b")}

    def bcol(l, name):
        base = {"bq": 0, "bk": 8, "bv": 16, "bo": 24, "b2": 32, "b1": 40,
                "g1": 72, "be1": 80, "g2": 88, "be2": 96}[name]
        return l * 104 + base

    GFCOL, BFCOL = 624, 632

    B4 = (0, 1, 4, 5)

    def MM(h, out, lhsT, rhs, start, stop):
        return h.matmul(out, lhsT, rhs, start=start, stop=stop,
                        skip_group_check=True)

    def load_slab(key, r, c, tag="w"):
        pump_until(key)
        grp, base, din_, dout_ = scr[key]
        buf = WBUF[load_slab.idx % 3]
        breg = R("WBUF%d" % (load_slab.idx % 3))
        load_slab.idx += 1
        o = base + r * 1024 * dout_ + c * 512
        in_ap = bass.AP(wscs[grp], o, [[dout_, 128], [128 * dout_, 8], [1, 512]])
        out_ap = buf[:].rearrange("p (k c) -> p k c", c=512)
        P.dma("w", (lambda h, oa=out_ap, ia=in_ap: h.dma_start(out=oa, in_=ia)),
              reads=(R("wsc_" + key),), writes=(breg,), tag=tag)
        return buf, breg
    load_slab.idx = 0

    def row_dma(dst_ap, src_ap, wreg, tag="row"):
        P.dma("xio", (lambda h, o=dst_ap, i=src_ap: h.dma_start(out=o, in_=i)),
              writes=(wreg,), tag=tag)

    tap_outs = {}

    def tap(name, buf_ap, rregs):
        if name not in TAPS or name in tap_outs:
            return
        shp = list(buf_ap.shape)
        td = nc.dram_tensor("tap_" + name, shp, buf_ap.dtype, kind="ExternalOutput")
        tap_outs[name] = td
        P.dma("xio", (lambda h, o=td.ap(), i=buf_ap: h.dma_start(out=o, in_=i)),
              reads=tuple(rregs), writes=(R("tapd_" + name),), tag="tap")

    def ph_init():
        row_dma(BIA[:], bia_d.ap(), R("BIA"))
        row_dma(CB[:], cb_d.ap(), R("CB"))
        row_dma(CF[:], cf_d.ap(), R("CF"))
        row_dma(ONR[:], or_d.ap(), R("ONR"))
        row_dma(SEL[:], sel_d.ap(), R("SEL"))
        for l in range(NL):
            for (o0, name) in ((0, "bq"), (8, "bk")):
                c = bcol(l, name)
                P.op("DVE", (lambda h, l=l, o0=o0, c=c: h.tensor_scalar(
                    BQ1[:, l * 16 + o0:l * 16 + o0 + 8], BIA[:, c:c + 8],
                    1.0, 0.0, OP.add, OP.add)),
                    reads=(R("BIA"),), writes=(R("BQ1"),))
            P.op("DVE", (lambda h, l=l: h.memset(KVS[l][:], 0.0)),
                 writes=(R(f"KVS{l}"),))

    def ph_xload(blk):
        XF3 = XF[:].rearrange("p (a b) -> p a b", b=1024)
        PSb7 = PSbf[:, 7168:8192]
        PSb7v = PSb7.rearrange("p (a b) -> p a b", b=128)
        for tc in range(8):
            r0 = blk * TBL + tc * 128
            row_dma(X0S[:], bass.AP(x_d, r0 * D, [[D, 128], [1, D]]),
                    R("X0S"), tag="xld")
            P.op("DVE", (lambda h: h.tensor_copy(X0B[:], X0S[:])),
                 reads=(R("X0S"),), writes=(R("X0B"),))
            for dc in range(8):
                P.op("PE", (lambda h, dc=dc: h.transpose(
                    PSb7[:, dc * 128:dc * 128 + 128],
                    X0B[:, dc * 128:dc * 128 + 128], IDB)),
                    reads=(R("X0B"), R("CB")), writes=(RPS[7],))
            P.op("DVE", (lambda h, tc=tc: h.tensor_copy(
                XF3[:, :, tc * 128:tc * 128 + 128], PSb7v)),
                reads=(RPS[7],), writes=(R("XFh0"), R("XFh1")))

    def ph_ln_stats(hf):
        for dc in range(8):
            xsl = XF[:, dc * 1024 + hf * 512:dc * 1024 + hf * 512 + 512]
            xq = ES1[:, 0:512] if dc % 2 == 0 else ES1[:, 512:1024]
            xqr = RES[("ES1", dc % 2)]
            P.op("ACT", (lambda h, o=xq, i=xsl: h.activation(o, i, AF.Square)),
                 reads=(R("XFh%d" % hf),), writes=(xqr,))
            P.op("PE", (lambda h, i=xsl, st=(dc == 0), sp=(dc == 7): MM(
                h, PS[0:1, 1024:1536], ONC[:, 0:1], i, st, sp)),
                reads=(R("XFh%d" % hf), R("CB")), writes=(RPS[2],))
            P.op("PE", (lambda h, i=xq, st=(dc == 0), sp=(dc == 7): MM(
                h, PS[0:1, 1536:2048], ONC[:, 0:1], i, st, sp)),
                reads=(xqr, R("CB")), writes=(RPS[3],))
        P.op("DVE", (lambda h: h.tensor_scalar(
            ROWS[0:1, 0:1024], PS[0:1, 1024:2048], 1.0 / D, 0.0, OP.mult, OP.add)),
            reads=(RPS[2], RPS[3]), writes=(R("ROWS"),))
        P.op("DVE", (lambda h: h.tensor_tensor(
            ROWS[0:1, 1024:1536], ROWS[0:1, 0:512], ROWS[0:1, 0:512], OP.mult)),
            reads=(R("ROWS"),), writes=(R("ROWS"),))
        P.op("DVE", (lambda h: h.scalar_tensor_tensor(
            ROWS[0:1, 512:1024], ROWS[0:1, 512:1024], LN_EPS,
            ROWS[0:1, 1024:1536], OP.add, OP.subtract)),
            reads=(R("ROWS"),), writes=(R("ROWS"),))
        P.op("ACT", (lambda h: h.activation(
            ROWS[0:1, 1024:1536], ROWS[0:1, 512:1024], AF.Ln)),
            reads=(R("ROWS"),), writes=(R("ROWS"),))
        P.op("ACT", (lambda h: h.activation(
            ROWS[0:1, 1536:2048], ROWS[0:1, 1024:1536], AF.Exp, scale=-0.5)),
            reads=(R("ROWS"),), writes=(R("ROWS"),))
        P.op("DVE", (lambda h: h.tensor_tensor(
            ROWS[0:1, 1024:1536], ROWS[0:1, 0:512], ROWS[0:1, 1536:2048],
            OP.mult)), reads=(R("ROWS"),), writes=(R("ROWS"),))

    def ph_ln_apply(gcol, hf):
        P.op("PE", (lambda h: MM(h, PS[:, 1024:1536], ONR[:],
                                 ROWS[0:1, 1536:2048], True, True)),
             reads=(R("ROWS"), R("ONR")), writes=(RPS[2],))
        P.op("PE", (lambda h: MM(h, PS[:, 1536:2048], ONR[:],
                                 ROWS[0:1, 1024:1536], True, True)),
             reads=(R("ROWS"), R("ONR")), writes=(RPS[3],))
        for dc in range(8):
            xsl = XF[:, dc * 1024 + hf * 512:dc * 1024 + hf * 512 + 512]
            es = ES0[:, 0:512] if dc % 2 == 0 else ES0[:, 512:1024]
            esr = RES[("ES0", dc % 2)]
            P.op("DVE", (lambda h, o=es, i=xsl: h.tensor_tensor(
                o, i, PS[:, 1024:1536], OP.mult)),
                reads=(R("XFh%d" % hf), RPS[2]), writes=(esr,))
            if gcol is None:
                P.op("DVE", (lambda h, o=xsl, i=es: h.tensor_tensor(
                    o, i, PS[:, 1536:2048], OP.subtract)),
                    reads=(esr, RPS[3]), writes=(R("XFh%d" % hf),))
            else:
                P.op("DVE", (lambda h, o=es, i=es: h.tensor_tensor(
                    o, i, PS[:, 1536:2048], OP.subtract)),
                    reads=(esr, RPS[3]), writes=(esr,))
                gc, bc = gcol
                P.op("DVE", (lambda h, o=xsl, i=es, g=BIA[:, gc + dc:gc + dc + 1],
                             b=BIA[:, bc + dc:bc + dc + 1]: h.tensor_scalar(
                    o, i, g, b, OP.mult, OP.add)),
                    reads=(esr, R("BIA")), writes=(R("XFh%d" % hf),))

    def ph_ln(gcol, hf):
        ph_ln_stats(hf)
        ph_ln_apply(gcol, hf)

    def ph_router(blk):
        pump_until("router")
        grp, base, _, _ = scr["router"]
        in_ap = bass.AP(wscs[grp], base, [[16, 128], [128 * 16, 8], [1, 16]])
        out_ap = RWB[:].rearrange("p (k c) -> p k c", c=16)
        P.dma("w", (lambda h, o=out_ap, i=in_ap: h.dma_start(out=o, in_=i)),
              reads=(R("wsc_router"),), writes=(R("RWB"),))
        for tc in range(8):
            pb = tc % 2
            for dc in range(8):
                P.op("PE", (lambda h, dc=dc, tc=tc, pb=pb: MM(
                    h, PS[:, pb * 512:pb * 512 + 16],
                    XF[:, dc * 1024 + tc * 128:dc * 1024 + tc * 128 + 128],
                    RWB[:, dc * 16:dc * 16 + 16], dc == 0, dc == 7)),
                    reads=(R("XFh0"), R("XFh1"), R("RWB")), writes=(RPS[pb],))
            P.op("ACT", (lambda h, tc=tc, pb=pb: h.activation(
                SCB[:, tc * 16:tc * 16 + 16], PS[:, pb * 512:pb * 512 + 16],
                AF.Sigmoid)), reads=(RPS[pb],), writes=(R("SCB"),))
            ssl = SCB[:, tc * 16:tc * 16 + 16]
            P.op("DVE", (lambda h, s=ssl: h.tensor_tensor(G1[:], s, EBC, OP.add)),
                 reads=(R("SCB"), R("CF")), writes=(R("G1"),))
            P.op("DVE", (lambda h: h.reduce_max(GM[:], G1[:], axis=AX.X)),
                 reads=(R("G1"),), writes=(R("GM"),))
            P.op("DVE", (lambda h: h.tensor_scalar(
                G2[:], G1[:], GM[:], 0.0, OP.is_equal, OP.add)),
                reads=(R("G1"), R("GM")), writes=(R("G2"),))
            P.op("DVE", (lambda h: h.scalar_tensor_tensor(
                G3[:], G2[:], -1e30, G1[:], OP.mult, OP.add)),
                reads=(R("G2"), R("G1")), writes=(R("G3"),))
            P.op("DVE", (lambda h: h.reduce_max(GM[:], G3[:], axis=AX.X)),
                 reads=(R("G3"),), writes=(R("GM"),))
            P.op("DVE", (lambda h: h.tensor_scalar(
                G3[:], G1[:], GM[:], 0.0, OP.is_equal, OP.add)),
                reads=(R("G1"), R("GM")), writes=(R("G3"),))
            P.op("DVE", (lambda h: h.tensor_tensor(G2[:], G2[:], G3[:], OP.add)),
                 reads=(R("G2"), R("G3")), writes=(R("G2"),))
            P.op("DVE", (lambda h, s=ssl: h.tensor_tensor(G1[:], s, G2[:],
                                                          OP.mult)),
                 reads=(R("SCB"), R("G2")), writes=(R("G1"),))
            P.op("PE", (lambda h: h.transpose(PS[0:16, 1024:1152], G1[:], IDF)),
                 reads=(R("G1"), R("CF")), writes=(RPS[2],))
            P.op("DVE", (lambda h, tc=tc: h.tensor_copy(
                GFB[0:16, tc * 128:tc * 128 + 128], PS[0:16, 1024:1152])),
                reads=(RPS[2],), writes=(R("GFB"),))

    def ph_shared(blk):
        for hf in (1, 0):
            for sbk in range(8):
                w1s, w1r = load_slab("sw1", 0, sbk)
                w2s, w2r = load_slab("sw2", 0, sbk)
                for hcl in range(4):
                    hc = sbk * 4 + hcl
                    b1_ = 2 if hc % 2 == 0 else 4
                    b2_ = 3 if hc % 2 == 0 else 5
                    for kc in range(8):
                        P.op("PE", (lambda h, w=w1s, kc=kc, hcl=hcl, hf=hf,
                                    b1_=b1_: MM(
                            h, PS[:, b1_ * 512:b1_ * 512 + 512],
                            w[:, kc * 512 + hcl * 128:kc * 512 + hcl * 128 + 128],
                            XF[:, kc * 1024 + hf * 512:kc * 1024 + hf * 512 + 512],
                            kc == 0, kc == 7)),
                            reads=(w1r, R("XFh%d" % hf)), writes=(RPS[b1_],))
                        P.op("PE", (lambda h, w=w2s, kc=kc, hcl=hcl, hf=hf,
                                    b2_=b2_: MM(
                            h, PS[:, b2_ * 512:b2_ * 512 + 512],
                            w[:, kc * 512 + hcl * 128:kc * 512 + hcl * 128 + 128],
                            XF[:, kc * 1024 + hf * 512:kc * 1024 + hf * 512 + 512],
                            kc == 0, kc == 7)),
                            reads=(w2r, R("XFh%d" % hf)), writes=(RPS[b2_],))
                    es = ES0[:, 0:512] if hc % 2 == 0 else ES0[:, 512:1024]
                    esr = RES[("ES0", hc % 2)]
                    P.op("ACT", (lambda h, o=es, b1_=b1_: h.activation(
                        o, PS[:, b1_ * 512:b1_ * 512 + 512], AF.Silu)),
                        reads=(RPS[b1_],), writes=(esr,))
                    P.op("DVE", (lambda h, o=HB[:, hc * 512:hc * 512 + 512],
                                 i=es, b2_=b2_: h.tensor_tensor(
                        o, i, PS[:, b2_ * 512:b2_ * 512 + 512], OP.mult)),
                        reads=(esr, RPS[b2_]), writes=(R("HB"),))
            for c in range(2):
                for r4 in range(4):
                    w3s, w3r = load_slab("sw3", r4, c)
                    for mol in range(4):
                        for kc in range(8):
                            hidc = r4 * 8 + kc
                            P.op("PE", (lambda h, w=w3s, kc=kc, mol=mol,
                                        hidc=hidc, st=(r4 == 0 and kc == 0),
                                        sp=(r4 == 3 and kc == 7): MM(
                                h, PS[:, (4 + mol) * 512:(4 + mol) * 512 + 512],
                                w[:, kc * 512 + mol * 128:kc * 512 + mol * 128 + 128],
                                HB[:, hidc * 512:hidc * 512 + 512], st, sp)),
                                reads=(w3r, R("HB")), writes=(RPS[4 + mol],))
                for mol in range(4):
                    mo = c * 4 + mol
                    fsl = FA[:, mo * 1024 + hf * 512:mo * 1024 + hf * 512 + 512]
                    psl = PS[:, (4 + mol) * 512:(4 + mol) * 512 + 512]
                    P.op("DVE", (lambda h, o=fsl, i=psl: h.tensor_copy(o, i)),
                         reads=(RPS[4 + mol],), writes=(R("FA"),))

    def ph_routed(blk):
        for p in range(8):
            pump(6)
            e0 = 2 * p
            for px in range(2):
                e = e0 + px
                for tb in range(2):
                    pb = px * 2 + tb
                    P.op("PE", (lambda h, e=e, tb=tb, pb=pb: MM(
                        h, PS[:, pb * 512:pb * 512 + 512],
                        SEL[:, e * 128:e * 128 + 128],
                        GFB[0:16, tb * 512:tb * 512 + 512], True, True)),
                        reads=(R("SEL"), R("GFB")), writes=(RPS[pb],))
                P.op("DVE", (lambda h, px=px: h.tensor_copy(
                    GSB[:, px * 1024:px * 1024 + 1024],
                    PS[:, px * 1024:px * 1024 + 1024])),
                    reads=(RPS[2 * px], RPS[2 * px + 1]), writes=(R("GSB"),))
            for px in range(2):
                e = e0 + px
                for c in range(2):
                    w1s, w1r = load_slab(f"rw1_{e}", 0, c)
                    w2s, w2r = load_slab(f"rw2_{e}", 0, c)
                    for rhl in range(4):
                        rh = c * 4 + rhl
                        for kc in range(8):
                            for tb in range(2):
                                P.op("PE", (lambda h, w=w1s, kc=kc, rhl=rhl,
                                            tb=tb: MM(
                                    h, PS[:, 1024 + tb * 512:1024 + tb * 512 + 512],
                                    w[:, kc * 512 + rhl * 128:
                                      kc * 512 + rhl * 128 + 128],
                                    XF[:, kc * 1024 + tb * 512:
                                       kc * 1024 + tb * 512 + 512],
                                    kc == 0, kc == 7)),
                                    reads=(w1r, R("XFh0"), R("XFh1")),
                                    writes=(RPS[2 + tb],))
                        for kc in range(8):
                            for tb in range(2):
                                P.op("PE", (lambda h, w=w2s, kc=kc, rhl=rhl,
                                            tb=tb: MM(
                                    h, PS[:, 2048 + tb * 512:2048 + tb * 512 + 512],
                                    w[:, kc * 512 + rhl * 128:
                                      kc * 512 + rhl * 128 + 128],
                                    XF[:, kc * 1024 + tb * 512:
                                       kc * 1024 + tb * 512 + 512],
                                    kc == 0, kc == 7)),
                                    reads=(w2r, R("XFh0"), R("XFh1")),
                                    writes=(RPS[4 + tb],))
                        es = ES0 if rh % 2 == 0 else ES1
                        esn = "ES0" if rh % 2 == 0 else "ES1"
                        era, erb = RES[(esn, 0)], RES[(esn, 1)]
                        P.op("ACT", (lambda h, o=es[:, 0:1024]: h.activation(
                            o, PS[:, 1024:2048], AF.Silu)),
                            reads=(RPS[2], RPS[3]), writes=(era, erb))
                        P.op("DVE", (lambda h, o=es[:, 0:1024]: h.tensor_tensor(
                            o, o, PS[:, 2048:3072], OP.mult)),
                            reads=(era, erb, RPS[4], RPS[5]), writes=(era, erb))
                        P.op("POOL", (lambda h, o=HB[:, px * 8192 + rh * 1024:
                                                     px * 8192 + rh * 1024 + 1024],
                                      i=es[:, 0:1024], px=px: h.tensor_tensor(
                            o, i, GSB[:, px * 1024:px * 1024 + 1024], OP.mult)),
                            reads=(era, erb, R("GSB")), writes=(R("HB"),))
            for c in range(2):
                w3a, w3ar = load_slab(f"rw3_{e0}", 0, c)
                w3b, w3br = load_slab(f"rw3_{e0 + 1}", 0, c)
                for mol in range(4):
                    mo = c * 4 + mol
                    pb = (mol % 2) * 2
                    for px, (w3s, w3r) in enumerate(((w3a, w3ar), (w3b, w3br))):
                        for kc in range(8):
                            for tb in range(2):
                                P.op("PE", (lambda h, w=w3s, kc=kc, mol=mol,
                                            px=px, tb=tb, pb=pb,
                                            st=(px == 0 and kc == 0),
                                            sp=(px == 1 and kc == 7): MM(
                                    h, PS[:, pb * 512 + tb * 512:
                                          pb * 512 + tb * 512 + 512],
                                    w[:, kc * 512 + mol * 128:
                                      kc * 512 + mol * 128 + 128],
                                    HB[:, px * 8192 + kc * 1024 + tb * 512:
                                       px * 8192 + kc * 1024 + tb * 512 + 512],
                                    st, sp)),
                                    reads=(w3r, R("HB")), writes=(RPS[pb + tb],))
                    fsl = FA[:, mo * 1024:mo * 1024 + 1024]
                    psl = PS[:, pb * 512:pb * 512 + 1024]
                    P.op("DVE", (lambda h, o=fsl, i=psl: h.tensor_tensor(
                        o, i, o, OP.add)),
                        reads=(RPS[pb], RPS[pb + 1], R("FA")), writes=(R("FA"),))

    def ph_qkv(l, hf, mid_hook=None):
        for (wkey, dst, dstn, b1c, esn) in (
                (f"Wq_{l}", QF, "QF", 0, "ES0"), (f"Wk_{l}", KF, "KF", 8, "ES1")):
            es = ES0 if esn == "ES0" else ES1
            bq1c = l * 16 + b1c
            bpc = bcol(l, "bq" if b1c == 0 else "bk")
            for sbk in range(2):
                ws, wr = load_slab(wkey, 0, sbk)
                if wkey.startswith("Wq") and sbk == 0:
                    tap("wslab", ws[:], (wr,))
                if mid_hook is not None and wkey.startswith("Wk") and sbk == 0:
                    mid_hook()
                    mid_hook = None
                for mol in range(4):
                    mo = sbk * 4 + mol
                    pb = B4[mo % 4]
                    for kc in range(8):
                        P.op("PE", (lambda h, w=ws, kc=kc, mol=mol, hf=hf,
                                    pb=pb: MM(
                            h, PS[:, pb * 512:pb * 512 + 512],
                            w[:, kc * 512 + mol * 128:kc * 512 + mol * 128 + 128],
                            XF[:, kc * 1024 + hf * 512:kc * 1024 + hf * 512 + 512],
                            kc == 0, kc == 7)),
                            reads=(wr, R("XFh%d" % hf)), writes=(RPS[pb],))
                    if (wkey.startswith("Wq") and sbk == 0 and mol == 0
                            and "qpsum" in TAPS and "qpsum" not in tap_outs):
                        P.op("DVE", (lambda h: h.tensor_copy(
                            REC[:, 0:512], PS[:, 0:512])),
                            reads=(RPS[0],), writes=(R("REC"),))
                        tap("qpsum", REC[:, 0:512], (R("REC"),))
                    eh = es[:, 0:512] if mo % 2 == 0 else es[:, 512:1024]
                    ehr = RES[(esn, mo % 2)]
                    P.op("ACT", (lambda h, o=eh, pb=pb,
                                 b=BIA[:, bpc + mo:bpc + mo + 1]: h.activation(
                        o, PS[:, pb * 512:pb * 512 + 512], AF.Exp, bias=b)),
                        reads=(RPS[pb], R("BIA")), writes=(ehr,))
                    P.op("DVE", (lambda h, o=eh: h.tensor_scalar(
                        o, o, 1.0, 0.0, OP.min, OP.add)),
                        reads=(ehr,), writes=(ehr,))
                    P.op("DVE", (lambda h, o=dst[:, mo * 512:mo * 512 + 512],
                                 pb=pb, b=BQ1[:, bq1c + mo:bq1c + mo + 1],
                                 i=eh: h.scalar_tensor_tensor(
                        o, PS[:, pb * 512:pb * 512 + 512], b, i, OP.add, OP.max)),
                        reads=(RPS[pb], R("BQ1"), ehr), writes=(R(dstn),))
        PSb7 = PSbf[:, 7168:8192]
        for tcl in range(4):
            for dc in range(8):
                P.op("PE", (lambda h, dc=dc, tcl=tcl: h.transpose(
                    PSb7[:, dc * 128:dc * 128 + 128],
                    KF[:, dc * 512 + tcl * 128:dc * 512 + tcl * 128 + 128], IDB)),
                    reads=(R("KF"), R("CB")), writes=(RPS[7],))
            P.op("DVE", (lambda h, tcl=tcl: h.tensor_copy(
                KT[:, tcl * 1024:tcl * 1024 + 1024], PSb7)),
                reads=(RPS[7],), writes=(R("KT"),))
        for sbk in range(2):
            row_dma(VROW[0:1, :],
                    bass.AP(bv_d, l * D + sbk * 512, [[512, 1], [1, 512]]),
                    R("VROW"), tag="vrow")
            ws, wr = load_slab(f"Wv_{l}", 0, sbk)
            for tcl in range(4):
                pb = B4[(sbk * 4 + tcl) % 4]
                for kc in range(8):
                    P.op("PE", (lambda h, w=ws, kc=kc, tcl=tcl, hf=hf, pb=pb: MM(
                        h, PS[:, pb * 512:pb * 512 + 512],
                        XF[:, kc * 1024 + hf * 512 + tcl * 128:
                           kc * 1024 + hf * 512 + tcl * 128 + 128],
                        w[:, kc * 512:kc * 512 + 512], kc == 0, False)),
                        reads=(wr, R("XFh%d" % hf)), writes=(RPS[pb],))
                P.op("PE", (lambda h, pb=pb: MM(
                    h, PS[:, pb * 512:pb * 512 + 512], ONR[:],
                    VROW[0:1, 0:512], False, True)),
                    reads=(R("ONR"), R("VROW")), writes=(RPS[pb],))
                P.op("ACT", (lambda h, tcl=tcl, sbk=sbk, pb=pb: h.activation(
                    VT[:, tcl * 1024 + sbk * 512:tcl * 1024 + sbk * 512 + 512],
                    PS[:, pb * 512:pb * 512 + 512], AF.Copy)),
                    reads=(RPS[pb],), writes=(R("VT"),))

    def ph_chunks(l, hf, dbg=False):
        kvr = R(f"KVS{l}")
        tail_box = [None]

        def run_tail():
            if tail_box[0] is not None:
                tail_box[0]()
                tail_box[0] = None

        for cl in (3, 2, 1, 0):
            def mm1(par):
                for s in range(8):
                    kf = KF[64 * par:64 * par + 64,
                            s * 512 + cl * 128:s * 512 + cl * 128 + 128]
                    qf = QF[64 * par:64 * par + 64,
                            s * 512 + cl * 128:s * 512 + cl * 128 + 128]
                    P.op("PE", (lambda h, kf=kf, qf=qf, s=s: MM(
                        h, PS[:, s * 128:s * 128 + 128], kf, qf, True, True)),
                        reads=(R("KF"), R("QF")), writes=(RPS[s // 4],))

            def maskev(par):
                P.op("DVE", (lambda h, par=par: h.tensor_tensor(
                    MA[:, par * 1024:par * 1024 + 1024], PS[:, 0:1024], MSK,
                    OP.mult)), reads=(RPS[0], RPS[1], R("CB")), writes=(R("MA"),))

            def mm4s():
                for par in range(2):
                    p0 = 64 * par
                    for s in range(8):
                        hh = 2 * s + par
                        kt = KT[:, cl * 1024 + hh * 64:cl * 1024 + hh * 64 + 64]
                        vt = VT[:, cl * 1024 + hh * 64:cl * 1024 + hh * 64 + 64]
                        P.op("PE", (lambda h, kt=kt, vt=vt, s=s, p0=p0: MM(
                            h, PS[p0:p0 + 64, 3072 + s * 128:3072 + s * 128 + 64],
                            kt, vt, True, True)),
                            reads=(R("KT"), R("VT")), writes=(RPS[6 + s // 4],))
                        P.op("PE", (lambda h, kt=kt, s=s, p0=p0: MM(
                            h, PS[p0:p0 + 64,
                                  3072 + s * 128 + 64:3072 + s * 128 + 128],
                            kt, ONC[:], True, True)),
                            reads=(R("KT"), R("CB")), writes=(RPS[6 + s // 4],))

            mm1(0)
            maskev(0)
            P.op("DVE", (lambda h: h.tensor_copy(KVB[:], KVS[l][:])),
                 reads=(kvr,), writes=(R("KVB"),))
            mm4s()
            mm1(1)
            P.op("DVE", (lambda h: h.tensor_tensor(
                KVS[l][:], KVS[l][:], PS[:, 3072:4096], OP.add)),
                reads=(kvr, RPS[6], RPS[7]), writes=(kvr,))
            maskev(1)
            run_tail()
            for par in range(2):
                p0 = 64 * par
                for g in range(2):
                    P.op("PE", (lambda h, par=par, g=g, p0=p0: MM(
                        h, PS[p0:p0 + 64, 2048 + g * 512:2048 + g * 512 + 512],
                        ONC[:],
                        MA[:, par * 1024 + g * 512:par * 1024 + g * 512 + 512],
                        True, False)),
                        reads=(R("CB"), R("MA")), writes=(RPS[4 + g],))
                for s in range(8):
                    qf = QF[p0:p0 + 64, s * 512 + cl * 128:s * 512 + cl * 128 + 128]
                    P.op("PE", (lambda h, par=par, s=s, cl=cl, p0=p0: MM(
                        h, PS[p0:p0 + 64, 1024 + s * 128:1024 + s * 128 + 128],
                        VT[:, cl * 1024 + (2 * s + par) * 64:
                           cl * 1024 + (2 * s + par) * 64 + 64],
                        MA[:, par * 1024 + s * 128:par * 1024 + s * 128 + 128],
                        True, False)),
                        reads=(R("VT"), R("MA")), writes=(RPS[2 + s // 4],))
                    P.op("PE", (lambda h, s=s, qf=qf, p0=p0: MM(
                        h, PS[p0:p0 + 64, 1024 + s * 128:1024 + s * 128 + 128],
                        KVB[p0:p0 + 64, s * 128:s * 128 + 64], qf, False, True)),
                        reads=(R("KVB"), R("QF")), writes=(RPS[2 + s // 4],))
                    P.op("PE", (lambda h, s=s, qf=qf, p0=p0: MM(
                        h, PS[p0:p0 + 64, 2048 + s * 128:2048 + s * 128 + 128],
                        KVB[p0:p0 + 64, s * 128 + 64:s * 128 + 128], qf,
                        False, True)),
                        reads=(R("KVB"), R("QF")), writes=(RPS[4 + s // 4],))
            if dbg and cl == 3:
                tap("ma", MA[:], (R("MA"),))
                P.op("DVE", (lambda h: h.tensor_copy(ES1[:, 0:1024],
                                                     PS[:, 1024:2048])),
                     reads=(RPS[2], RPS[3]), writes=(RES[("ES1", 0)],
                                                     RES[("ES1", 1)]))
                tap("numo", ES1[:], (RES[("ES1", 0)], RES[("ES1", 1)]))
            def mk_tail(cl):
                def tail():
                    P.op("ACT", (lambda h: h.activation(
                        REC[:], PS[:, 2048:3072], AF.Ln)),
                        reads=(RPS[4], RPS[5]), writes=(R("REC"),))
                    P.op("ACT", (lambda h: h.activation(
                        REC[:], REC[:], AF.Exp, scale=-1.0)),
                        reads=(R("REC"),), writes=(R("REC"),))
                    AO3 = AO[:].rearrange("p (a b) -> p a b", b=512)
                    P.op("DVE", (lambda h, A=AO3, cl=cl: h.tensor_tensor(
                        A[:, :, cl * 128:cl * 128 + 128],
                        PS[:, 1024:2048].rearrange("p (a b) -> p a b", b=128),
                        REC[:].rearrange("p (a b) -> p a b", b=128), OP.mult)),
                        reads=(RPS[2], RPS[3], R("REC")), writes=(R("AO"),))
                return tail
            tail_box[0] = mk_tail(cl)
        run_tail()

    def ph_oproj(l, hf):
        boc = bcol(l, "bo")
        for sbk in range(2):
            ws, wr = load_slab(f"Wo_{l}", 0, sbk)
            for mol in range(4):
                mo = sbk * 4 + mol
                pb = B4[mo % 4]
                for kc in range(8):
                    P.op("PE", (lambda h, w=ws, kc=kc, mol=mol, pb=pb: MM(
                        h, PS[:, pb * 512:pb * 512 + 512],
                        w[:, kc * 512 + mol * 128:kc * 512 + mol * 128 + 128],
                        AO[:, kc * 512:kc * 512 + 512], kc == 0, kc == 7)),
                        reads=(wr, R("AO")), writes=(RPS[pb],))
                xsl = XF[:, mo * 1024 + hf * 512:mo * 1024 + hf * 512 + 512]
                P.op("DVE", (lambda h, o=xsl, pb=pb,
                             b=BIA[:, boc + mo:boc + mo + 1]:
                             h.scalar_tensor_tensor(
                    o, PS[:, pb * 512:pb * 512 + 512], b, o, OP.add, OP.add)),
                    reads=(RPS[pb], R("BIA"), R("XFh%d" % hf)),
                    writes=(R("XFh%d" % hf),))

    def ph_gelu_ffn(l, hf):
        b1c = bcol(l, "b1")
        for sbk in range(8):
            ws, wr = load_slab(f"W1_{l}", 0, sbk)
            for hcl in range(4):
                hc = sbk * 4 + hcl
                pb = B4[hc % 4]
                for kc in range(8):
                    P.op("PE", (lambda h, w=ws, kc=kc, hcl=hcl, hf=hf, pb=pb: MM(
                        h, PS[:, pb * 512:pb * 512 + 512],
                        w[:, kc * 512 + hcl * 128:kc * 512 + hcl * 128 + 128],
                        XF[:, kc * 1024 + hf * 512:kc * 1024 + hf * 512 + 512],
                        kc == 0, kc == 7)),
                        reads=(wr, R("XFh%d" % hf)), writes=(RPS[pb],))
                P.op("ACT", (lambda h, hc=hc, pb=pb,
                             b=BIA[:, b1c + hc:b1c + hc + 1]: h.activation(
                    HB[:, hc * 512:hc * 512 + 512],
                    PS[:, pb * 512:pb * 512 + 512], AF.Gelu, bias=b)),
                    reads=(RPS[pb], R("BIA")), writes=(R("HB"),))
        b2c = bcol(l, "b2")
        W2B = ((4, 5, 6, 7), (0, 1, 6, 7))
        for c in range(2):
            for r4 in range(4):
                ws, wr = load_slab(f"W2_{l}", r4, c)
                for kc in range(8):
                    hidc = r4 * 8 + kc
                    for mol in range(4):
                        bk = W2B[c][mol]
                        P.op("PE", (lambda h, w=ws, kc=kc, mol=mol, hidc=hidc,
                                    bk=bk, st=(r4 == 0 and kc == 0),
                                    sp=(r4 == 3 and kc == 7): MM(
                            h, PS[:, bk * 512:bk * 512 + 512],
                            w[:, kc * 512 + mol * 128:kc * 512 + mol * 128 + 128],
                            HB[:, hidc * 512:hidc * 512 + 512], st, sp)),
                            reads=(wr, R("HB")), writes=(RPS[bk],))
            for mol in range(4):
                mo = c * 4 + mol
                bk = W2B[c][mol]
                xsl = XF[:, mo * 1024 + hf * 512:mo * 1024 + hf * 512 + 512]
                P.op("DVE", (lambda h, o=xsl, bk=bk,
                             b=BIA[:, b2c + mo:b2c + mo + 1]:
                             h.scalar_tensor_tensor(
                    o, PS[:, bk * 512:bk * 512 + 512], b, o,
                    OP.add, OP.add)),
                    reads=(RPS[bk], R("BIA"), R("XFh%d" % hf)),
                    writes=(R("XFh%d" % hf),))

    def ph_assembly(blk):
        PSb7 = PSbf[:, 7168:8192]
        PSb6 = PSbf[:, 6144:7168]
        for tc in range(8):
            r0 = blk * TBL + tc * 128
            row_dma(X0S[:], bass.AP(x_d, r0 * D, [[D, 128], [1, D]]),
                    R("X0S"), tag="x2")
            for dc in range(8):
                P.op("PE", (lambda h, dc=dc, tc=tc: h.transpose(
                    PSb7[:, dc * 128:dc * 128 + 128],
                    XF[:, dc * 1024 + tc * 128:dc * 1024 + tc * 128 + 128], IDB)),
                    reads=(R("XFh0"), R("XFh1"), R("CB")), writes=(RPS[7],))
                P.op("PE", (lambda h, dc=dc, tc=tc: h.transpose(
                    PSb6[:, dc * 128:dc * 128 + 128],
                    FA[:, dc * 1024 + tc * 128:dc * 1024 + tc * 128 + 128], IDB)),
                    reads=(R("FA"), R("CB")), writes=(RPS[6],))
            ot = OT[0]
            otr = R("OT0")
            P.op("DVE", (lambda h, o=ot: h.tensor_tensor(
                o[:], PSb7, X0S[:], OP.add)),
                reads=(RPS[7], R("X0S")), writes=(otr,))
            P.op("DVE", (lambda h, o=ot: h.tensor_tensor(
                o[:], o[:], PSb6, OP.add)),
                reads=(otr, RPS[6]), writes=(otr,))
            P.dma("xio", (lambda h, i=ot, r0=r0: h.dma_start(
                out=bass.AP(out_d, r0 * D, [[D, 128], [1, D]]), in_=i[:])),
                reads=(otr,), writes=(R("outd"),), tag="st")

    ph_init()
    for blk in (3, 2, 1, 0):
        if blk == 3:
            pump(25)
        ph_xload(blk)
        for hf in (1, 0):
            ph_ln(None, hf)
        ph_router(blk)
        if DO_SHARED:
            pump(4)
            ph_shared(blk)
        else:
            for mo in range(8):
                P.op("DVE", (lambda h, mo=mo: h.memset(
                    FA[:, mo * 1024:mo * 1024 + 1024], 0.0)), writes=(R("FA"),))
        if DO_ROUTED:
            ph_routed(blk)
        pend = [None]

        def run_pend():
            if pend[0] is not None:
                pend[0]()
                pend[0] = None

        for l in range(N_LAYERS):
            if blk == 3:
                pump(12)
            ph_qkv(l, 1, mid_hook=(pend[0] if pend[0] else None))
            pend[0] = None
            ph_chunks(l, 1)
            ph_oproj(l, 1)
            ph_ln_stats(1)
            ph_qkv(l, 0, mid_hook=(
                lambda l=l: ph_ln_apply((bcol(l, "g1"), bcol(l, "be1")), 1)))
            ph_chunks(l, 0)
            ph_oproj(l, 0)
            ph_ln_stats(0)
            ph_gelu_ffn(l, 1)
            ph_ln_apply((bcol(l, "g1"), bcol(l, "be1")), 0)
            ph_ln_stats(1)
            ph_gelu_ffn(l, 0)
            ph_ln_apply((bcol(l, "g2"), bcol(l, "be2")), 1)
            ph_ln_stats(0)
            pend[0] = (lambda l=l: ph_ln_apply((bcol(l, "g2"), bcol(l, "be2")),
                                               0))
        run_pend()
        if DO_FINAL_LN:
            for hf in (1, 0):
                ph_ln((GFCOL, BFCOL), hf)
        ph_assembly(blk)
    pump(len(pc_units))

    n_instr = sum(len(v) for v in P.ops.values())
    n_waits = sum(len(o["waits"]) for v in P.ops.values() for o in v)
    sems = {}
    with stack:
        for e in ENGS:
            sems[("e", e)] = stack.enter_context(nc.semaphore("s_" + e))
        for s in P.streams:
            sems[("d", s)] = stack.enter_context(nc.semaphore("sd_" + s))
        block = stack.enter_context(nc.Block())

        def emit(ename, handle):
            for o in P.ops[ename]:
                for semkey, val in o["waits"]:
                    handle.wait_ge(sems[semkey], val)
                inst = o["fn"](handle)
                for semkey, amt in o["incs"]:
                    inst.then_inc(sems[semkey], amt)

        @block.tensor
        def _(h):
            emit("PE", h)

        @block.vector
        def _(h):
            emit("DVE", h)

        @block.scalar
        def _(h):
            emit("ACT", h)

        @block.gpsimd
        def _(h):
            emit("POOL", h)

        @block.sync
        def _(h):
            emit("SP", h)
            h.wait_ge(sems[("d", "xio")], P.streams["xio"]["count"])

    return nc, n_instr, n_waits


def _prep_core_inputs(inputs, core):
    d = {}
    d["x"] = np.ascontiguousarray(np.asarray(inputs["L0_attn_x"][core],
                                             dtype=np.float32))
    for k, src in (("Wq", "L0_attn_Wq"), ("Wk", "L0_attn_Wk"),
                   ("Wv", "L0_attn_Wv"), ("Wo", "L0_attn_Wo"),
                   ("W1", "L0_attn_W1"), ("W2", "L0_attn_W2"),
                   ("sw1", "L0_ffn_shared_w1"), ("sw2", "L0_ffn_shared_w2"),
                   ("sw3", "L0_ffn_shared_w3"), ("rw1", "L0_ffn_routed_w1"),
                   ("rw2", "L0_ffn_routed_w2"), ("rw3", "L0_ffn_routed_w3"),
                   ("router", "L0_ffn_router_w"), ("bv_rows", "L0_attn_bv")):
        d[k] = np.ascontiguousarray(np.asarray(inputs[src], dtype=np.float32))
    return d


def _prep_shared(inputs):
    f32 = np.float32
    bf16 = ml_dtypes.bfloat16

    def pp(v):
        return np.ascontiguousarray(np.asarray(v, dtype=f32).reshape(-1, 128).T)

    cols = []
    for l in range(NL):
        for key in ("L0_attn_bq", "L0_attn_bk", "L0_attn_bv", "L0_attn_bo",
                    "L0_attn_b2"):
            cols.append(pp(inputs[key][l]))
        cols.append(pp(inputs["L0_attn_b1"][l]))
        for key in ("L0_attn_g1", "L0_attn_be1", "L0_attn_g2", "L0_attn_be2"):
            cols.append(pp(inputs[key][l]))
    cols.append(pp(inputs["L0_attn_gf"]))
    cols.append(pp(inputs["L0_attn_bf"]))
    bias_pp = np.concatenate(cols, axis=1).astype(f32)
    assert bias_pp.shape == (128, 640), bias_pp.shape

    idn = np.eye(128, dtype=f32)
    mask = np.tril(np.ones((128, 128), dtype=f32))
    cb = np.concatenate([idn, np.tile(mask, (1, 8)),
                         np.ones((128, 64), dtype=f32)], axis=1).astype(bf16)
    cf = np.concatenate(
        [idn, np.tile(np.asarray(inputs["L0_ffn_expert_bias"],
                                 dtype=f32)[None, :], (128, 1))],
        axis=1).astype(f32)
    sel = np.zeros((16, 2048), dtype=f32)
    for e in range(16):
        sel[e, e * 128:(e + 1) * 128] = 1.0
    return {"bias_pp": bias_pp, "cst_bf": np.ascontiguousarray(cb),
            "cst_f32": np.ascontiguousarray(cf),
            "ones_row": np.ones((1, 128), dtype=f32),
            "sel": np.ascontiguousarray(sel.astype(bf16))}


def kernel(**inputs):
    shared = _prep_shared(inputs)
    in_maps = []
    for c in range(8):
        m = _prep_core_inputs(inputs, c)
        m.update(shared)
        in_maps.append(m)
    nc, n_instr, n_waits = _build()
    res = run_bass_kernel_spmd(nc, in_maps, core_ids=list(range(8)))
    out = np.stack([np.asarray(r["out"], dtype=np.float32)
                    for r in res.results], axis=0)
    if TAPS:
        kernel.taps = {k: np.asarray(v) for k, v in res.results[0].items()
                       if k.startswith("tap_")}
    return out
